# version 01
# baseline (speedup 1.0000x reference)
import numpy as np
import ml_dtypes
import concourse.bass as bass
import concourse.mybir as mybir
from concourse.bass_utils import run_bass_kernel_spmd

F32 = mybir.dt.float32
BF16 = mybir.dt.bfloat16
AF = mybir.ActivationFunctionType
ALU = mybir.AluOpType
AX = mybir.AxisListType
AP = bass.AP

NCORES = 8
BATCH, SEQ, D = 4, 4096, 1024
T = 2048
H, DH = 16, 64
HID = 4096
SCALE = 0.125
EPS = 1e-5
NCH = T // 128
NDC = D // 128
NGRP = 8
NEG = -30000.0
MOE_EORDER = None
XN_EVAC_DVE = True
KIN_E = 128 * T
VIN_E = T * 130
KVE = KIN_E + VIN_E
ENGINES = ("tensor", "vector", "scalar", "gpsimd", "sync")
RG_PAIRS = [[0, 1], [2, 3], [4, 5], [6, 7]]


class Sem:
    def __init__(self, nc, name):
        self.h = nc.alloc_semaphore(name=name)
        self.name = name
        self.v = 0


class Plan:
    def __init__(self, nc):
        self.nc = nc
        self.ops = {e: [] for e in ENGINES}
        self.waits = {e: {} for e in ENGINES}
        self.sems = {}
        self.last = {}

    def sem(self, name):
        if name not in self.sems:
            self.sems[name] = Sem(self.nc, name)
        return self.sems[name]

    def op(self, eng, fn, deps=(), inc=None, inck=1):
        if eng in ("vector", "scalar") and eng in self.last:
            deps = list(deps) + [self.last[eng]]
        need = []
        for d in deps:
            if d is None:
                continue
            s, v = d
            if self.waits[eng].get(s.name, 0) < v:
                self.waits[eng][s.name] = v
                need.append((s.h, v))
        tok = None
        if inc is not None:
            inc.v += inck
            tok = (inc, inc.v)

        def run(e, fn=fn, need=tuple(need), inc=inc, inck=inck):
            for h, v in need:
                e.wait_ge(h, v)
            ins = fn(e)
            if inc is not None:
                ins.then_inc(inc.h, inck)

        self.ops[eng].append(run)
        if tok is not None and eng in ("vector", "scalar"):
            self.last[eng] = tok
        return tok

    def emit(self):
        with self.nc.Block() as block:
            for eng in ENGINES:
                fns = self.ops[eng]

                def body(e, fns=fns):
                    for f in fns:
                        f(e)

                getattr(block, eng)(body)



def build_program(layers, dbg=(), upto="full"):
    nc = bass.Bass("TRN2", target_bir_lowering=False)
    P = Plan(nc)
    dbg = set(dbg)

    x_in = nc.dram_tensor("x", [T, D], F32, kind="ExternalInput")
    x_out = nc.dram_tensor("xout", [T, D], F32, kind="ExternalOutput")

    dins = {}

    def din(name, shape, dtype=F32):
        dins[name] = nc.dram_tensor(name, shape, dtype, kind="ExternalInput")

    din("ones_f", [1, 128])
    din("ident_f", [128, 128])
    din("ident_b", [128, 128], BF16)
    for li in layers:
        p = f"l{li}_"
        if li == 0:
            din(p + "wq", [D, D]); din(p + "wqs", [D, D])
            din(p + "wkv", [D, 2 * D]); din(p + "wks", [D, D])
            din(p + "wout", [D, D]); din(p + "bout", [1, D])
            din(p + "rwT", [128, 56])
            din(p + "wg", [8, D, HID]); din(p + "wu", [8, D, HID])
            din(p + "wd", [8, HID, D])
        else:
            din(p + "wqkv", [D, 3 * D]); din(p + "wqs", [D, D]); din(p + "wks", [D, D])
            din(p + "bias_fm", [128, 32])
            din(p + "bv", [1, D])
            din(p + "wo", [D, D]); din(p + "bo", [1, D])
            din(p + "gamma", [1, D]); din(p + "beta", [1, D])
            din(p + "w1", [8, D, HID]); din(p + "b1_fm", [128, 256])
            din(p + "w2", [8, HID, D]); din(p + "b2", [8, D])
            din(p + "wgT", [128, 64]); din(p + "bg", [1, 8])
        din(p + "cos", [128, T], BF16)
        din(p + "sin", [128, T], BF16)

    kv_in = nc.dram_tensor("kv_in", [KVE, 1], BF16, kind="Internal")
    kv_out = nc.dram_tensor("kv_out", [2 * KVE, 1], BF16, kind="Internal")

    sb = {}

    ARENA0 = 16512

    def sba(name, shape, dtype, off):
        sb[name] = nc.alloc_sbuf_tensor_at(name, shape, dtype, offset=ARENA0 + off)

    B0 = 98304
    sba("x", [128, 16 * 1024], F32, 0)
    sba("xnfm", [128, 8 * 2048], BF16, 65536)
    sba("hbuf", [128, 32 * 1024], BF16, B0)
    sba("khat", [128, 4096], BF16, B0 + 0)
    sba("vhat", [128, 32 * 130], BF16, B0 + 8192)
    sba("wo_t", [128, 8 * 1024], BF16, B0 + 0)
    sba("vown", [128, 16 * 130], BF16, B0 + 16640)
    sba("pt0", [128, 1024], BF16, B0 + 20800)
    sba("pt1", [128, 1024], BF16, B0 + 22848)
    sba("cos_t", [128, T], BF16, B0 + 24896)
    sba("sin_t", [128, T], BF16, B0 + 28992)
    sba("row0", [1, 1024], F32, B0 + 33088)
    sba("gb_bc", [128, 1024], BF16, B0 + 37184)
    sba("bb_bc", [128, 1024], BF16, B0 + 39232)
    sba("wk_t", [128, 1024], BF16, B0 + 41280)
    sba("wks_t", [128, 1024], BF16, B0 + 43328)
    sba("wv_t", [128, 1024], BF16, B0 + 45376)
    sba("wq_t", [128, 1024], BF16, B0 + 47424)
    sba("wqs_t", [128, 1024], BF16, B0 + 49472)
    sba("fmscr", [128, 1024], BF16, B0 + 51520)
    sba("combfm", [8, 128], BF16, B0 + 53568)
    sba("rtmp", [128, 2048], BF16, B0 + 53824)
    A0 = B0 + 65536
    sba("atm", [128, 16 * 1024], BF16, A0)
    sba("wg_t", [128, 4096], BF16, A0 + 0)
    sba("wu_t", [128, 4096], BF16, A0 + 8192)
    sba("wd_t", [128, 4096], BF16, A0 + 16384)
    sba("gsil", [128, 2048], BF16, A0 + 24576)
    Q0 = A0 + 32768
    sba("scr0", [128, 1024], F32, Q0)
    sba("scr1", [128, 1024], F32, Q0 + 4096)
    sba("kown", [128, 2048], BF16, Q0 + 0)
    sba("kswp", [128, 2048], BF16, Q0 + 4096)
    sba("qbuf", [128, 2048], BF16, Q0 + 0)
    sba("qswp", [128, 2048], BF16, Q0 + 4096)
    sba("b1fm", [128, 256], F32, Q0 + 0)
    sba("b2bf", [8, 1024], BF16, Q0 + 1024)
    P0 = Q0 + 8192
    sba("ones_f", [1, 128], F32, P0 + 0)
    sba("ident_f", [128, 128], F32, P0 + 512)
    sba("ident_b", [128, 128], BF16, P0 + 1024)
    sba("ones_c", [128, 1], BF16, P0 + 1280)
    sba("mu", [128, 16], F32, P0 + 1312)
    sba("rstd", [128, 16], F32, P0 + 1376)
    sba("sums", [128, 16], F32, P0 + 1440)
    sba("sumsq", [128, 16], F32, P0 + 1504)
    sba("muh", [128, 1], F32, P0 + 1568)
    sba("rstdh", [128, 1], F32, P0 + 1600)
    sba("ssq2", [128, 1], F32, P0 + 1632)
    sba("comb", [128, 128], F32, P0 + 1664)
    sba("negB", [128, 4], F32, P0 + 2176)
    sba("nrm", [1, 64], F32, P0 + 2208)
    sba("lg", [128, 8], F32, P0 + 2464)
    sba("m1", [128, 1], F32, P0 + 2496)
    sba("m2", [128, 1], F32, P0 + 2528)
    sba("eq", [128, 8], F32, P0 + 2560)
    sba("rden", [128, 1], F32, P0 + 2592)
    sba("rwT", [128, 64], BF16, P0 + 2624)
    sba("bgrow", [1, 8], F32, P0 + 2752)
    sba("biasfm", [128, 32], F32, P0 + 2784)
    sba("rcol", [128, 1], F32, P0 + 2912)
    assert ARENA0 + P0 + 2944 <= 229344, P0 + 2944
    nc.sbuf_base = ARENA0 + P0 + 2944 + (32 - (P0 + 2944) % 32) % 32

    ps = nc.alloc_psum_tensor("ps", [128, 4096], F32)
    psb = ps.bitcast(BF16)

    s_ldx = P.sem("ldx"); s_ldc = P.sem("ldc"); s_rw = P.sem("rw")
    s_bnc = P.sem("bnc"); s_cc = P.sem("cc"); s_rel = P.sem("rel")
    s_pe = P.sem("pe"); s_act = P.sem("act"); s_dve = P.sem("dve"); s_st = P.sem("st")
    s_wt = {n: P.sem("w" + n) for n in ("k", "ks", "v", "q", "qs", "o")}
    s_wg2 = [P.sem("wg0"), P.sem("wg1")]
    s_wu2 = [P.sem("wu0"), P.sem("wu1")]
    s_wd2 = [P.sem("wd0"), P.sem("wd1")]

    xsb, xn, ones_f = sb["x"], sb["xnfm"], sb["ones_f"]

    def xch(c):
        return xsb[:, c * 1024:(c + 1) * 1024]

    def xnc(dc, a, b_):
        return xn[:, dc * 2048 + a: dc * 2048 + b_]

    def pe_now():
        return (s_pe, s_pe.v)

    def dve_now():
        return (s_dve, s_dve.v)

    def act_now():
        return (s_act, s_act.v)

    dbg_t = {}

    def tap(name, shape, dtype, ap_fn, deps):
        if name not in dbg:
            return
        t = nc.dram_tensor("dbg_" + name, shape, dtype, kind="ExternalOutput")
        dbg_t[name] = t
        P.op("sync", lambda e: e.dma_start(out=t.ap(), in_=ap_fn()), deps=deps,
             inc=s_st, inck=16)

    x_ld = P.op("sync", lambda e: e.dma_start(
        out=xsb.ap().rearrange("p (c d) -> p c d", d=1024),
        in_=x_in.ap().rearrange("(c p) d -> p c d", p=128)),
        inc=s_ldx, inck=16)
    P.op("sync", lambda e: e.dma_start(out=ones_f[:, :], in_=dins["ones_f"].ap()),
         inc=s_ldc, inck=16)
    P.op("sync", lambda e: e.dma_start(out=sb["ident_f"][:, :], in_=dins["ident_f"].ap()),
         inc=s_ldc, inck=16)
    P.op("sync", lambda e: e.dma_start(out=sb["ident_b"][:, :], in_=dins["ident_b"].ap()),
         inc=s_ldc, inck=16)
    c0 = (s_ldc, s_ldc.v)
    c_ones = c_idf = c_idb = c0
    c_onesc = P.op("vector", lambda e: e.memset(sb["ones_c"][:, :], 1.0), inc=s_dve)
    P.op("vector", lambda e: e.memset(sb["comb"][:, :], 0.0), inc=s_dve)

    st = {"x_dep": [x_ld], "cc_free": None, "rel_done": [], "kb": None, "vb": None,
          "exp_last2": [None, None]}

    for li in layers:
        if upto == "x":
            break
        p = f"l{li}_"
        ROT = 32 if li == 0 else 64
        NE = 7 if li == 0 else 8
        layer_start = [pe_now(), dve_now(), act_now()]

        P.op("sync", lambda e, p=p: e.dma_start(out=sb["cos_t"][:, :], in_=dins[p + "cos"].ap()),
             deps=layer_start + [(s_ldc, s_ldc.v)], inc=s_ldc, inck=16)
        P.op("sync", lambda e, p=p: e.dma_start(out=sb["sin_t"][:, :], in_=dins[p + "sin"].ap()),
             inc=s_ldc, inck=16)
        if li == 0:
            c_rw = P.op("gpsimd", lambda e, p=p: e.dma_start(out=sb["rwT"][:, 0:56], in_=dins[p + "rwT"].ap()),
                        deps=layer_start, inc=s_rw, inck=16)
            cbatch = (s_ldc, s_ldc.v)
            c_cos = c_sin = cbatch
            c_bg = c_bias = c_gb = None
        else:
            c_rw = P.op("gpsimd", lambda e, p=p: e.dma_start(out=sb["rwT"][:, 0:64], in_=dins[p + "wgT"].ap()),
                        deps=layer_start, inc=s_rw, inck=16)
            P.op("sync", lambda e, p=p: e.dma_start(out=sb["bgrow"][:, :], in_=dins[p + "bg"].ap()),
                 inc=s_ldc, inck=16)
            P.op("sync", lambda e, p=p: e.dma_start(out=sb["biasfm"][:, :], in_=dins[p + "bias_fm"].ap()),
                 inc=s_ldc, inck=16)
            cbatch = (s_ldc, s_ldc.v)
            c_cos = c_sin = c_bg = c_bias = cbatch
            prev_ev = cbatch
            for nm, dst in (("gamma", "gb_bc"), ("beta", "bb_bc")):
                r = P.op("sync", lambda e, nm=nm, p=p: e.dma_start(out=sb["row0"][:, :], in_=dins[p + nm].ap()),
                         deps=layer_start + [prev_ev], inc=s_ldc, inck=16)
                mm = P.op("tensor", lambda e: (
                    e.matmul(ps[:, 0:512], ones_f[:, :], sb["row0"][:, 0:512], start=True, stop=True),
                    e.matmul(ps[:, 512:1024], ones_f[:, :], sb["row0"][:, 512:1024], start=True, stop=True))[-1],
                    deps=[r, c_ones] + layer_start, inc=s_pe)
                ev = P.op("scalar", lambda e, dst=dst: (
                    e.activation(sb[dst][:, 0:512], ps[:, 0:512], AF.Copy),
                    e.activation(sb[dst][:, 512:1024], ps[:, 512:1024], AF.Copy))[-1],
                    deps=[mm], inc=s_act)
                prev_ev = ev
            c_gb = prev_ev

        x_dep = st["x_dep"]

        for c in range(NCH):
            P.op("vector", lambda e, c=c: e.reduce_sum(sb["sums"][:, c:c + 1], xch(c), AX.X),
                 deps=x_dep + layer_start, inc=s_dve)
            P.op("vector", lambda e, c=c: e.scalar_tensor_tensor(
                sb["scr0"][:, :], xch(c), 1.0, xch(c), ALU.mult, ALU.mult,
                accum_out=sb["sumsq"][:, c:c + 1]), inc=s_dve)
        stat_in = dve_now()
        P.op("vector", lambda e: e.tensor_scalar_mul(sb["mu"][:, :], sb["sums"][:, :], 1.0 / D),
             deps=[stat_in], inc=s_dve)
        P.op("vector", lambda e: e.tensor_mul(sb["rstd"][:, :], sb["mu"][:, :], sb["mu"][:, :]),
             inc=s_dve)
        P.op("vector", lambda e: e.scalar_tensor_tensor(
            sb["rstd"][:, :], sb["sumsq"][:, :], 1.0 / D, sb["rstd"][:, :], ALU.mult, ALU.subtract),
            inc=s_dve)
        P.op("vector", lambda e: e.tensor_scalar_add(sb["rstd"][:, :], sb["rstd"][:, :], EPS),
             inc=s_dve)
        var_dep = dve_now()
        sq = P.op("scalar", lambda e: e.activation(sb["rstd"][:, :], sb["rstd"][:, :], AF.Sqrt),
                  deps=[var_dep], inc=s_act)
        stats_dep = P.op("vector", lambda e: e.reciprocal(sb["rstd"][:, :], sb["rstd"][:, :]),
                         deps=[sq], inc=s_dve)

        if upto == "stats":
            break
        ev_prev = [None, None]
        tp_last = {}
        _nch = 2 if upto in ("lnt1", "xntm1", "tponly") else NCH
        for c in range(_nch):
            scr = sb["scr0"] if c % 2 == 0 else sb["scr1"]
            deps = [stats_dep] + x_dep
            if c >= 2:
                deps.append(tp_last[c - 2])
            xn_tm = P.op("vector", lambda e, c=c, scr=scr: e.tensor_scalar(
                scr[:, :], xch(c), sb["mu"][:, c:c + 1], sb["rstd"][:, c:c + 1],
                ALU.subtract, ALU.mult), deps=deps, inc=s_dve)
            if upto in ("xntm", "xntm1"):
                tp_last[c] = xn_tm
                if upto == "xntm1" or c >= _nch - 1:
                    pass
                continue
            last = None
            for dc in range(NDC):
                reg = ps[:, 1024 + (dc % 2) * 512: 1024 + (dc % 2) * 512 + 128]
                mm = P.op("tensor", lambda e, scr=scr, dc=dc, reg=reg: e.transpose(
                    reg, scr[:, dc * 128:(dc + 1) * 128], sb["ident_f"][:, :]),
                    deps=[xn_tm, c_idf, ev_prev[dc % 2]], inc=s_pe)
                if upto == "tponly":
                    ev_prev[dc % 2] = mm
                    last = mm
                    continue
                if XN_EVAC_DVE:
                    ev = P.op("vector", lambda e, c=c, dc=dc, reg=reg: e.tensor_copy(
                        xnc(dc, c * 128, (c + 1) * 128), reg), deps=[mm], inc=s_dve)
                else:
                    ev = P.op("scalar", lambda e, c=c, dc=dc, reg=reg: e.activation(
                        xnc(dc, c * 128, (c + 1) * 128), reg, AF.Copy),
                        deps=[mm], inc=s_act)
                ev_prev[dc % 2] = ev
                last = ev
            tp_last[c] = last
        if upto in ("lnt1", "xntm", "xntm1", "tponly"):
            break
        xn_dep = [tp_last[c] for c in range(NCH)]
        tap(f"l{li}_xn", [128, NDC * 2048], BF16, lambda: xn.ap(), xn_dep)
        if upto == "lnt":
            break

        lg_cp = None
        for c in range(NCH):
            def mm_lg(e, c=c, NE=NE, li=li):
                for dc in range(NDC):
                    ins = e.matmul(ps[:, 2048:2048 + NE],
                                   xnc(dc, c * 128, (c + 1) * 128),
                                   sb["rwT"][:, dc * NE:(dc + 1) * NE],
                                   start=(dc == 0), stop=(dc == NDC - 1 and li == 0))
                if li == 1:
                    ins = e.matmul(ps[:, 2048:2048 + NE], ones_f[:, 0:128],
                                   sb["bgrow"][:, 0:NE], start=False, stop=True)
                return ins
            mm = P.op("tensor", mm_lg,
                      deps=[tp_last[c], c_rw, c_ones, lg_cp] + ([c_bg] if li == 1 else []),
                      inc=s_pe)
            lg_cp = P.op("vector", lambda e, NE=NE: e.tensor_copy(sb["lg"][:, 0:NE], ps[:, 2048:2048 + NE]),
                         deps=[mm], inc=s_dve)
            P.op("vector", lambda e, NE=NE: e.reduce_max(sb["m1"][:, :], sb["lg"][:, 0:NE],
                                                         AX.X, negate=True), inc=s_dve)
            m1d = dve_now()
            P.op("vector", lambda e, NE=NE: e.tensor_scalar(
                sb["eq"][:, 0:NE], sb["lg"][:, 0:NE], sb["m1"][:, :], None, ALU.add, ALU.bypass),
                deps=[m1d], inc=s_dve)
            P.op("vector", lambda e, NE=NE: e.tensor_scalar(
                sb["eq"][:, 0:NE], sb["eq"][:, 0:NE], 0.0, NEG, ALU.is_equal, ALU.mult),
                inc=s_dve)
            P.op("vector", lambda e, NE=NE: e.tensor_tensor(
                sb["eq"][:, 0:NE], sb["eq"][:, 0:NE], sb["lg"][:, 0:NE], ALU.add), inc=s_dve)
            P.op("vector", lambda e, NE=NE: e.reduce_max(sb["m2"][:, :], sb["eq"][:, 0:NE], AX.X),
                 inc=s_dve)
            m2d = dve_now()
            en = P.op("scalar", lambda e, NE=NE: e.activation(
                sb["eq"][:, 0:NE], sb["lg"][:, 0:NE], AF.Exp, bias=sb["m1"][:, :]),
                deps=[m2d], inc=s_act)
            e2 = P.op("scalar", lambda e: e.activation(
                sb["rden"][:, :], sb["m2"][:, :], AF.Exp, bias=sb["m1"][:, :]), inc=s_act)
            P.op("vector", lambda e: e.tensor_scalar_add(sb["rden"][:, :], sb["rden"][:, :], 1.0),
                 deps=[e2], inc=s_dve)
            P.op("vector", lambda e: e.reciprocal(sb["rden"][:, :], sb["rden"][:, :]), inc=s_dve)
            rdend = dve_now()
            P.op("vector", lambda e, NE=NE: e.tensor_scalar(
                sb["lg"][:, 0:NE], sb["lg"][:, 0:NE], sb["m2"][:, :], None, ALU.is_ge, ALU.bypass),
                deps=[rdend, en], inc=s_dve)
            P.op("vector", lambda e, NE=NE: e.tensor_tensor(
                sb["lg"][:, 0:NE], sb["lg"][:, 0:NE], sb["eq"][:, 0:NE], ALU.mult), inc=s_dve)
            P.op("vector", lambda e, c=c, NE=NE: e.tensor_scalar(
                sb["comb"][:, c * 8:c * 8 + NE], sb["lg"][:, 0:NE], sb["rden"][:, :], None,
                ALU.mult, ALU.bypass), inc=s_dve)
        comb_dep = dve_now()
        tap(f"l{li}_comb", [128, 128], F32, lambda: sb["comb"].ap(), [comb_dep])

        if upto in ("ln", "route"):
            continue

        if li == 0:
            wq_d, wqs_d = dins[p + "wq"], dins[p + "wqs"]
            wk_d, wks_d, wv_d = dins[p + "wkv"], dins[p + "wks"], dins[p + "wkv"]
            KOFF, VOFF, QOFF = 0, D, 0
        else:
            wq_d, wqs_d = dins[p + "wqkv"], dins[p + "wqs"]
            wk_d, wks_d, wv_d = dins[p + "wqkv"], dins[p + "wks"], dins[p + "wqkv"]
            KOFF, VOFF, QOFF = D, 2 * D, 0
            bvr = P.op("sync", lambda e, p=p: e.dma_start(out=sb["row0"][:, :], in_=dins[p + "bv"].ap()),
                       deps=[c_gb], inc=s_ldc, inck=16)

        vown3 = sb["vown"].ap().rearrange("p (jt c) -> p jt c", c=130)
        vhat4 = sb["vhat"].ap().rearrange("p (r jt c) -> p r jt c", r=2, c=130)
        vhat3 = sb["vhat"].ap().rearrange("p (jt c) -> p jt c", c=130)
        atm = sb["atm"]

        def wload(dst, src_d, col0, sem, guard):
            return P.op("gpsimd", lambda e: e.dma_start(
                out=sb[dst].ap().rearrange("p (dc c) -> p dc c", c=128),
                in_=src_d.ap().rearrange("(dc p) c -> p dc c", p=128)[:, :, col0:col0 + 128]),
                deps=[guard], inc=sem, inck=16)

        a_done = None
        exp_hist = []
        prev_flash_act = None
        wguard = {n: layer_start[0] for n in ("k", "ks", "v", "q", "qs")}
        ngrp = 1 if upto == "g1" else NGRP
        for g in range(ngrp):
            hd0 = g * 128
            gp = g % 2
            wk_l = wload("wk_t", wk_d, KOFF + hd0, s_wt["k"], wguard["k"])
            wks_l = wload("wks_t", wks_d, hd0, s_wt["ks"], wguard["ks"])
            kev = []
            for bi, (buf, wt, wl, bcol) in enumerate(
                    (("kown", "wk_t", wk_l, 2), ("kswp", "wks_t", wks_l, 3))):
                for n in range(4):
                    reg = ps[:, (n % 2) * 512:(n % 2) * 512 + 512]
                    def mmk(e, wt=wt, n=n, reg=reg):
                        for dc in range(NDC):
                            ins = e.matmul(reg, sb[wt][:, dc * 128:(dc + 1) * 128],
                                           xnc(dc, n * 512, (n + 1) * 512),
                                           start=(dc == 0), stop=(dc == NDC - 1))
                        return ins
                    dd = [wl, prev_flash_act] + xn_dep
                    if len(kev) >= 2:
                        dd.append(kev[-2])
                    mm = P.op("tensor", mmk, deps=dd, inc=s_pe)
                    wguard["ks" if buf == "kswp" else "k"] = mm
                    bias = sb["biasfm"][:, bcol * 8 + g:bcol * 8 + g + 1] if li == 1 else 0.0
                    fcp = AF.Identity if li == 1 else AF.Copy
                    ev = P.op("scalar", lambda e, buf=buf, n=n, reg=reg, bias=bias, fcp=fcp:
                              e.activation(
                        sb[buf][:, n * 512:(n + 1) * 512], reg, fcp, bias=bias),
                        deps=[mm, st["kb"]] + ([c_bias] if li == 1 else []), inc=s_act)
                    kev.append(ev)
            kr = []
            for slot in (0, 64):
                rws = slice(slot, slot + ROT)
                P.op("vector", lambda e, rws=rws: e.tensor_tensor(
                    sb["rtmp"][rws, :], sb["kown"][rws, :], sb["cos_t"][rws, :], ALU.mult),
                    deps=kev + [c_cos, c_sin], inc=s_dve)
                P.op("vector", lambda e, rws=rws: e.tensor_tensor(
                    sb["kown"][rws, :], sb["kswp"][rws, :], sb["sin_t"][rws, :], ALU.mult),
                    inc=s_dve)
                t3 = P.op("vector", lambda e, rws=rws: e.tensor_tensor(
                    sb["kown"][rws, :], sb["kown"][rws, :], sb["rtmp"][rws, :], ALU.add),
                    inc=s_dve)
                kr.append(t3)
            ksq = P.op("vector", lambda e: e.tensor_tensor(
                sb["kswp"][:, :], sb["kown"][:, :], sb["kown"][:, :], ALU.mult),
                deps=kr, inc=s_dve)
            nk = []
            for hl in (0, 1):
                def mmn(e, hl=hl):
                    for n in range(4):
                        ins = e.matmul(ps[0:1, 2048 + n * 512: 2048 + (n + 1) * 512],
                                       sb["ones_c"][hl * 64:hl * 64 + 64, 0:1],
                                       sb["kswp"][hl * 64:hl * 64 + 64, n * 512:(n + 1) * 512],
                                       start=True, stop=True)
                    return ins
                mm = P.op("tensor", mmn, deps=[ksq, c_onesc] + nk, inc=s_pe)
                rd = P.op("vector", lambda e, hl=hl: e.reduce_max(
                    sb["nrm"][0:1, 2 + hl:3 + hl], ps[0:1, 2048:4096], AX.X),
                    deps=[mm], inc=s_dve)
                nk.append(rd)
            kb = P.op("sync", lambda e: e.dma_start(
                out=AP(kv_in, 0, [[2048, 128], [1, 2048]]),
                in_=sb["kown"][:, :]),
                deps=[ksq, st["cc_free"]], inc=s_bnc, inck=16)
            st["kb"] = kb

            wv_l = wload("wv_t", wv_d, VOFF + hd0, s_wt["v"], wguard["v"])
            vset = P.op("vector", lambda e: e.memset(
                sb["vown"].ap().rearrange("p (jt h c) -> p jt h c", h=2, c=65)[:, :, :, 64:65], 1.0),
                deps=[st["vb"]], inc=s_dve)
            vev = []
            for jt in range(NCH):
                reg = ps[:, (jt % 2) * 512:(jt % 2) * 512 + 128]
                def mmv(e, jt=jt, reg=reg, li=li, hd0=hd0):
                    for dc in range(NDC):
                        ins = e.matmul(reg, xnc(dc, jt * 128, (jt + 1) * 128),
                                       sb["wv_t"][:, dc * 128:(dc + 1) * 128],
                                       start=(dc == 0), stop=(dc == NDC - 1 and li == 0))
                    if li == 1:
                        ins = e.matmul(reg, ones_f[:, 0:128], sb["row0"][:, hd0:hd0 + 128],
                                       start=False, stop=True)
                    return ins
                dd = [wv_l] + xn_dep + ([bvr, c_ones] if li == 1 else [])
                if len(vev) >= 2:
                    dd.append(vev[-2])
                mm = P.op("tensor", mmv, deps=dd, inc=s_pe)
                wguard["v"] = mm
                ev = P.op("scalar", lambda e, jt=jt, reg=reg: e.activation(
                    vown3[:, jt, :].rearrange("p (h c) -> p h c", h=2, c=65)[:, :, 0:64],
                    reg.rearrange("p (h c) -> p h c", h=2, c=64), AF.Copy),
                    deps=[mm, vset], inc=s_act)
                vev.append(ev)
            vb = P.op("sync", lambda e: e.dma_start(
                out=AP(kv_in, KIN_E, [[130, 128], [130 * 128, 16], [1, 130]]),
                in_=vown3),
                deps=vev + [st["cc_free"]], inc=s_bnc, inck=16)
            bnc_all = (s_bnc, s_bnc.v)
            kb = bnc_all
            vb = bnc_all
            st["vb"] = bnc_all
            st["kb"] = bnc_all

            cc = P.op("gpsimd", lambda e: e.collective_compute(
                "AllGather", ALU.bypass, RG_PAIRS,
                ins=[kv_in.ap().opt()], outs=[kv_out.ap().opt()]),
                deps=[kb, vb] + st["rel_done"], inc=s_cc)
            st["cc_free"] = cc

            wq_l = wload("wq_t", wq_d, QOFF + hd0, s_wt["q"], wguard["q"])
            wqs_l = wload("wqs_t", wqs_d, hd0, s_wt["qs"], wguard["qs"])
            qev = []
            for bi, (buf, wt, wl, bcol) in enumerate(
                    (("qbuf", "wq_t", wq_l, 0), ("qswp", "wqs_t", wqs_l, 1))):
                for n in range(4):
                    reg = ps[:, (n % 2) * 512:(n % 2) * 512 + 512]
                    def mmq(e, wt=wt, n=n, reg=reg):
                        for dc in range(NDC):
                            ins = e.matmul(reg, sb[wt][:, dc * 128:(dc + 1) * 128],
                                           xnc(dc, n * 512, (n + 1) * 512),
                                           start=(dc == 0), stop=(dc == NDC - 1))
                        return ins
                    dd = [wl] + xn_dep
                    if len(qev) >= 2:
                        dd.append(qev[-2])
                    mm = P.op("tensor", mmq, deps=dd, inc=s_pe)
                    wguard["qs" if buf == "qswp" else "q"] = mm
                    bias = sb["biasfm"][:, bcol * 8 + g:bcol * 8 + g + 1] if li == 1 else 0.0
                    fcp = AF.Identity if li == 1 else AF.Copy
                    ev = P.op("scalar", lambda e, buf=buf, n=n, reg=reg, bias=bias, fcp=fcp:
                              e.activation(
                        sb[buf][:, n * 512:(n + 1) * 512], reg, fcp, bias=bias),
                        deps=[mm, kb] + ([c_bias] if li == 1 else []), inc=s_act)
                    qev.append(ev)
            qr = []
            for slot in (0, 64):
                rws = slice(slot, slot + ROT)
                P.op("vector", lambda e, rws=rws: e.tensor_tensor(
                    sb["rtmp"][rws, :], sb["qbuf"][rws, :], sb["cos_t"][rws, :], ALU.mult),
                    deps=qev + [c_cos, c_sin], inc=s_dve)
                P.op("vector", lambda e, rws=rws: e.tensor_tensor(
                    sb["qbuf"][rws, :], sb["qswp"][rws, :], sb["sin_t"][rws, :], ALU.mult),
                    inc=s_dve)
                t3 = P.op("vector", lambda e, rws=rws: e.tensor_tensor(
                    sb["qbuf"][rws, :], sb["qbuf"][rws, :], sb["rtmp"][rws, :], ALU.add),
                    inc=s_dve)
                qr.append(t3)
            qsq = P.op("vector", lambda e: e.tensor_tensor(
                sb["rtmp"][:, :], sb["qbuf"][:, :], sb["qbuf"][:, :], ALU.mult),
                deps=qr, inc=s_dve)
            nq = []
            for hl in (0, 1):
                def mmn(e, hl=hl):
                    for n in range(4):
                        ins = e.matmul(ps[0:1, 2048 + n * 512: 2048 + (n + 1) * 512],
                                       sb["ones_c"][hl * 64:hl * 64 + 64, 0:1],
                                       sb["rtmp"][hl * 64:hl * 64 + 64, n * 512:(n + 1) * 512],
                                       start=True, stop=True)
                    return ins
                mm = P.op("tensor", mmn, deps=[qsq, c_onesc] + nk + nq, inc=s_pe)
                rd = P.op("vector", lambda e, hl=hl: e.reduce_max(
                    sb["nrm"][0:1, hl:hl + 1], ps[0:1, 2048:4096], AX.X),
                    deps=[mm], inc=s_dve)
                nq.append(rd)
            P.op("vector", lambda e: e.tensor_tensor(
                sb["nrm"][0:1, 4:6], sb["nrm"][0:1, 0:2], sb["nrm"][0:1, 2:4], ALU.add),
                deps=nq + nk, inc=s_dve)
            nb2 = P.op("vector", lambda e: e.tensor_scalar_mul(
                sb["nrm"][0:1, 4:6], sb["nrm"][0:1, 4:6], -SCALE / 2.0), inc=s_dve)
            nbm = P.op("tensor", lambda e: e.matmul(
                ps[:, 1024:1026], ones_f[:, :], sb["nrm"][0:1, 4:6], start=True, stop=True),
                deps=[nb2, c_ones], inc=s_pe)
            negBd = P.op("vector", lambda e, gp=gp: e.tensor_copy(
                sb["negB"][:, gp * 2:gp * 2 + 2], ps[:, 1024:1026]),
                deps=[nbm, st["exp_last2"][gp]], inc=s_dve)

            P.op("sync", lambda e: e.dma_start(
                out=sb["khat"].ap().rearrange("p (r j) -> p r j", r=2),
                in_=AP(kv_out, 0, [[2048, 128], [KVE, 2], [1, 2048]])),
                deps=[cc, pe_now()], inc=s_rel, inck=16)
            P.op("sync", lambda e: e.dma_start(
                out=vhat3[:, 0:16, :],
                in_=AP(kv_out, KIN_E, [[130, 128], [130 * 128, 16], [1, 130]])),
                deps=[cc], inc=s_rel, inck=16)
            P.op("sync", lambda e: e.dma_start(
                out=vhat3[:, 16:32, :],
                in_=AP(kv_out, KVE + KIN_E, [[130, 128], [130 * 128, 16], [1, 130]])),
                inc=s_rel, inck=16)
            st["rel_done"] = [(s_rel, s_rel.v)]

            OUTREG = [ps[:, 2048 + r * 1024: 2048 + (r + 1) * 1024] for r in (0, 1)]
            out_par = 0
            out_free = [None, None]
            pt = [sb["pt0"], sb["pt1"]]
            for hl in (0, 1):
                slot = hl * 64
                for ic in (0, 1):
                    outreg = OUTREG[out_par]
                    pv_tok = [None] * 32
                    exp_tok = [None] * 32
                    for j in range(32):
                        sreg = ps[:, (j % 2) * 1024:(j % 2) * 1024 + 1024]
                        def mmqk(e, j=j, slot=slot, ic=ic, sreg=sreg):
                            for nh in (0, 1):
                                ins = e.matmul(
                                    sreg[:, nh * 512:(nh + 1) * 512],
                                    sb["khat"][slot:slot + 64, j * 128:(j + 1) * 128],
                                    sb["qbuf"][slot:slot + 64,
                                               ic * 1024 + nh * 512: ic * 1024 + (nh + 1) * 512],
                                    start=True, stop=True)
                            return ins
                        dd = st["rel_done"] + [qr[hl]]
                        if j >= 2:
                            dd.append(exp_tok[j - 2])
                        elif len(exp_hist) >= 2 - j:
                            dd.append(exp_hist[-2 + j])
                        qk = P.op("tensor", mmqk, deps=dd, inc=s_pe)
                        dd = [qk, negBd]
                        if j >= 2:
                            dd.append(pv_tok[j - 2])
                        exp_tok[j] = P.op("scalar", lambda e, j=j, gp=gp, hl=hl, sreg=sreg:
                                          e.activation(
                            pt[j % 2][:, :], sreg, AF.Exp, scale=SCALE,
                            bias=sb["negB"][:, gp * 2 + hl:gp * 2 + hl + 1]),
                            deps=dd, inc=s_act)
                        exp_hist.append(exp_tok[j])
                        def mmpv(e, j=j, hl=hl, outreg=outreg):
                            for isub in range(8):
                                ins = e.matmul(
                                    outreg[:, isub * 128: isub * 128 + 65],
                                    pt[j % 2][:, isub * 128:(isub + 1) * 128],
                                    vhat3[:, j, hl * 65:(hl + 1) * 65],
                                    start=(j == 0 and isub % 4 == 0),
                                    stop=(j == 31 and isub % 4 == 3),
                                    skip_group_check=True)
                            return ins
                        pv_tok[j] = P.op("tensor", mmpv,
                                         deps=[exp_tok[j], out_free[out_par]], inc=s_pe)
                    last = None
                    for isub in range(8):
                        rc = P.op("vector", lambda e, isub=isub, outreg=outreg: e.reciprocal(
                            sb["rcol"][:, :], outreg[:, isub * 128 + 64: isub * 128 + 65]),
                            deps=[pv_tok[31]], inc=s_dve)
                        wr = P.op("vector", lambda e, isub=isub, ic=ic, g=g, hl=hl, outreg=outreg:
                                  e.tensor_scalar(
                            atm[:, (ic * 8 + isub) * 1024 + (2 * g + hl) * 64:
                                (ic * 8 + isub) * 1024 + (2 * g + hl) * 64 + 64],
                            outreg[:, isub * 128: isub * 128 + 64],
                            sb["rcol"][:, :], None, ALU.mult, ALU.bypass),
                            deps=[rc], inc=s_dve)
                        last = wr
                    out_free[out_par] = last
                    out_par ^= 1
                st["exp_last2"][gp] = exp_tok[31]
            a_done = last
            prev_flash_act = act_now()

        if upto == "g1":
            tap(f"l{li}_atm", [128, 16 * 128], BF16,
                lambda: atm.ap().rearrange("p (c h) -> p c h", h=1024)[:, :, 0:128],
                [a_done, prev_flash_act])
            continue
        tap(f"l{li}_atm", [128, 16 * 1024], BF16, lambda: atm.ap(),
            [a_done, prev_flash_act])

        if li == 0:
            wo_src, bo_src = dins[p + "wout"], dins[p + "bout"]
        else:
            wo_src, bo_src = dins[p + "wo"], dins[p + "bo"]
        wo_l = P.op("gpsimd", lambda e, wo_src=wo_src: e.dma_start(
            out=sb["wo_t"].ap().rearrange("p (dc c) -> p dc c", c=1024),
            in_=wo_src.ap().rearrange("(dc p) c -> p dc c", p=128)),
            deps=[prev_flash_act, dve_now()], inc=s_wt["o"], inck=16)
        bor = P.op("sync", lambda e, bo_src=bo_src: e.dma_start(
            out=sb["row0"][:, :], in_=bo_src.ap()),
            deps=[pe_now()], inc=s_ldc, inck=16)

        tp_prev = [None, None]
        proj_prev = [None, None]
        for c in range(NCH):
            fm_ev = []
            for dc in range(NDC):
                reg = psb[:, (dc % 2) * 1024:(dc % 2) * 1024 + 128]
                mm = P.op("tensor", lambda e, c=c, dc=dc, reg=reg: e.transpose(
                    reg, atm[:, c * 1024 + dc * 128: c * 1024 + (dc + 1) * 128],
                    sb["ident_b"][:, :]),
                    deps=[a_done, c_idb, tp_prev[dc % 2]], inc=s_pe)
                ev = P.op("scalar", lambda e, dc=dc, reg=reg: e.activation(
                    sb["fmscr"][:, dc * 128:(dc + 1) * 128], reg, AF.Copy),
                    deps=[mm], inc=s_act)
                tp_prev[dc % 2] = ev
                fm_ev.append(ev)
            preg = ps[:, 2048 + (c % 2) * 1024: 2048 + (c % 2) * 1024 + 1024]
            def mmproj(e, preg=preg):
                for nh in (0, 1):
                    for dc in range(NDC):
                        e.matmul(preg[:, nh * 512:(nh + 1) * 512],
                                 sb["fmscr"][:, dc * 128:(dc + 1) * 128],
                                 sb["wo_t"][:, dc * 1024 + nh * 512: dc * 1024 + (nh + 1) * 512],
                                 start=(dc == 0), stop=False)
                    ins = e.matmul(preg[:, nh * 512:(nh + 1) * 512], ones_f[:, 0:128],
                                   sb["row0"][:, nh * 512:(nh + 1) * 512],
                                   start=False, stop=True)
                return ins
            mm = P.op("tensor", mmproj,
                      deps=fm_ev + [wo_l, bor, c_ones, proj_prev[c % 2]], inc=s_pe)
            if li == 0:
                wr = P.op("vector", lambda e, c=c, preg=preg: (
                    e.tensor_tensor(xch(c)[:, 0:512], xch(c)[:, 0:512], preg[:, 0:512], ALU.add),
                    e.tensor_tensor(xch(c)[:, 512:1024], xch(c)[:, 512:1024], preg[:, 512:1024],
                                    ALU.add))[-1], deps=[mm] + x_dep, inc=s_dve)
                proj_prev[c % 2] = wr
                st["x_dep"] = st["x_dep"] + [wr]
            else:
                P.op("vector", lambda e, c=c: e.tensor_scalar(
                    sb["scr0"][:, :], xch(c), sb["mu"][:, c:c + 1], sb["rstd"][:, c:c + 1],
                    ALU.subtract, ALU.mult), deps=[mm] + x_dep, inc=s_dve)
                h2 = P.op("vector", lambda e, preg=preg: e.tensor_tensor(
                    sb["scr0"][:, :], sb["scr0"][:, :], preg, ALU.add), inc=s_dve)
                proj_prev[c % 2] = h2
                su = P.op("vector", lambda e: e.reduce_sum(sb["muh"][:, :], sb["scr0"][:, :], AX.X),
                          inc=s_dve)
                sq2 = P.op("vector", lambda e: e.scalar_tensor_tensor(
                    sb["scr1"][:, :], sb["scr0"][:, :], 1.0, sb["scr0"][:, :], ALU.mult, ALU.mult,
                    accum_out=sb["ssq2"][:, :]), inc=s_dve)
                P.op("vector", lambda e: e.tensor_scalar_mul(sb["muh"][:, :], sb["muh"][:, :], 1.0 / D),
                     deps=[su], inc=s_dve)
                P.op("vector", lambda e: e.tensor_mul(sb["rstdh"][:, :], sb["muh"][:, :], sb["muh"][:, :]),
                     inc=s_dve)
                P.op("vector", lambda e: e.scalar_tensor_tensor(
                    sb["rstdh"][:, :], sb["ssq2"][:, :], 1.0 / D, sb["rstdh"][:, :],
                    ALU.mult, ALU.subtract), deps=[sq2], inc=s_dve)
                vv = P.op("vector", lambda e: e.tensor_scalar_add(
                    sb["rstdh"][:, :], sb["rstdh"][:, :], EPS), inc=s_dve)
                sq3 = P.op("scalar", lambda e: e.activation(
                    sb["rstdh"][:, :], sb["rstdh"][:, :], AF.Sqrt), deps=[vv], inc=s_act)
                rr = P.op("vector", lambda e: e.reciprocal(sb["rstdh"][:, :], sb["rstdh"][:, :]),
                          deps=[sq3], inc=s_dve)
                P.op("vector", lambda e: e.tensor_scalar(
                    sb["scr1"][:, :], sb["scr0"][:, :], sb["muh"][:, :], sb["rstdh"][:, :],
                    ALU.subtract, ALU.mult), deps=[rr], inc=s_dve)
                P.op("vector", lambda e: e.tensor_tensor(
                    sb["scr1"][:, :], sb["scr1"][:, :], sb["gb_bc"][:, :], ALU.mult),
                    deps=[c_gb], inc=s_dve)
                P.op("vector", lambda e: e.tensor_tensor(
                    sb["scr1"][:, :], sb["scr1"][:, :], sb["bb_bc"][:, :], ALU.add), inc=s_dve)
                wr = P.op("vector", lambda e, c=c: e.tensor_tensor(
                    xch(c), xch(c), sb["scr1"][:, :], ALU.add), inc=s_dve)
                st["x_dep"] = st["x_dep"] + [wr]
        x_dep = st["x_dep"]
        tap(f"l{li}_xattn", [128, NCH * 1024], F32, lambda: xsb.ap(), [dve_now()])
        if upto == "attn":
            continue

        if li == 0:
            wg_d, wu_d, wd_d = dins[p + "wg"], dins[p + "wu"], dins[p + "wd"]
            eorder = MOE_EORDER if MOE_EORDER is not None else [7] + list(range(7))
            actf = {e_: AF.Silu for e_ in range(8)}
            b1l = None
        else:
            wg_d, wd_d = dins[p + "w1"], dins[p + "w2"]
            wu_d = None
            eorder = [0, 2, 4, 6, 1, 3, 5, 7]
            actf = {e_: (AF.Gelu if e_ % 2 == 0 else AF.Silu) for e_ in range(8)}
            b1l = P.op("sync", lambda e, p=p: e.dma_start(out=sb["b1fm"][:, :], in_=dins[p + "b1_fm"].ap()),
                       deps=[dve_now(), act_now(), pe_now()], inc=s_ldc, inck=16)

        moe_dep = [pe_now(), dve_now(), act_now()]

        if li == 1:
            b2d = P.op("gpsimd", lambda e, p=p: e.dma_start(out=sb["b2bf"][:, :], in_=dins[p + "b2"].ap()),
                       deps=moe_dep, inc=s_rw, inck=16)
            ctp_prev = None
            for c in range(NCH):
                ctp = P.op("tensor", lambda e, c=c: e.transpose(
                    ps[0:8, 1024:1152], sb["comb"][:, c * 8:(c + 1) * 8], sb["ident_f"][:, :]),
                    deps=[comb_dep, c_idf, ctp_prev] + moe_dep, inc=s_pe)
                cev = P.op("vector", lambda e: e.tensor_copy(sb["combfm"][:, :], ps[0:8, 1024:1152]),
                           deps=[ctp] + moe_dep, inc=s_dve)
                mmb = P.op("tensor", lambda e: (
                    e.matmul(ps[:, 2048:2560], sb["combfm"][:, :], sb["b2bf"][:, 0:512],
                             start=True, stop=True),
                    e.matmul(ps[:, 2560:3072], sb["combfm"][:, :], sb["b2bf"][:, 512:1024],
                             start=True, stop=True))[-1], deps=[cev, b2d], inc=s_pe)
                wr = P.op("vector", lambda e, c=c: (
                    e.tensor_tensor(xch(c)[:, 0:512], xch(c)[:, 0:512], ps[:, 2048:2560], ALU.add),
                    e.tensor_tensor(xch(c)[:, 512:1024], xch(c)[:, 512:1024], ps[:, 2560:3072],
                                    ALU.add))[-1], deps=[mmb] + x_dep, inc=s_dve)
                ctp_prev = wr
                st["x_dep"] = st["x_dep"] + [wr]
            x_dep = st["x_dep"]
            moe_dep = [pe_now(), dve_now(), act_now()]

        gu_guard = [None, None]
        wd_guard = [None, None]
        evac_last = None
        for ic in (0, 1):
            for e_ in eorder:
                ld_tok = {}

                def gu_load(hp, e_=e_, wg_d=wg_d, wu_d=wu_d):
                    par = hp % 2
                    t1 = P.op("gpsimd", lambda e, hp=hp, par=par, e_=e_: e.dma_start(
                        out=sb["wg_t"].ap().rearrange("p (b dc c) -> p b dc c", b=2, c=256)[:, par],
                        in_=wg_d.ap()[e_].rearrange("(dc p) hh -> p dc hh", p=128)
                        [:, :, hp * 256:(hp + 1) * 256]),
                        deps=moe_dep + [gu_guard[par]], inc=s_wg2[par], inck=16)
                    toks = [t1]
                    if wu_d is not None:
                        t2 = P.op("gpsimd", lambda e, hp=hp, par=par, e_=e_: e.dma_start(
                            out=sb["wu_t"].ap().rearrange("p (b dc c) -> p b dc c", b=2, c=256)[:, par],
                            in_=wu_d.ap()[e_].rearrange("(dc p) hh -> p dc hh", p=128)
                            [:, :, hp * 256:(hp + 1) * 256]),
                            deps=moe_dep + [gu_guard[par]], inc=s_wu2[par], inck=16)
                        toks.append(t2)
                    ld_tok[hp] = toks
                    return toks

                gu_load(0)
                h_tok = {}
                sil_tok = {}
                for hp in range(16):
                    if hp + 1 < 16 and (hp + 1) not in ld_tok:
                        gu_load(hp + 1)
                    par = hp % 2
                    wt = ld_tok[hp]
                    for hcl in (0, 1):
                        hc = hp * 2 + hcl
                        pg = hc % 2
                        for nh in (0, 1):
                            greg = ps[:, (2 * pg + nh) * 512:(2 * pg + nh + 1) * 512]
                            ureg = ps[:, 2048 + (2 * pg + nh) * 512: 2048 + (2 * pg + nh + 1) * 512]
                            def mmg(e, par=par, hcl=hcl, nh=nh, greg=greg, ic=ic):
                                for dc in range(NDC):
                                    ins = e.matmul(
                                        greg,
                                        sb["wg_t"][:, par * 2048 + dc * 256 + hcl * 128:
                                                   par * 2048 + dc * 256 + hcl * 128 + 128],
                                        xnc(dc, ic * 1024 + nh * 512, ic * 1024 + (nh + 1) * 512),
                                        start=(dc == 0), stop=(dc == NDC - 1))
                                return ins
                            dd = [wt[0]] + xn_dep + moe_dep + [h_tok.get((hc - 2, nh))]
                            mm = P.op("tensor", mmg, deps=dd, inc=s_pe)
                            gu_guard[par] = mm
                            if li == 0:
                                sil = P.op("scalar", lambda e, pg=pg, nh=nh, greg=greg:
                                           e.activation(sb["gsil"][:, pg * 1024 + nh * 512:
                                                                   pg * 1024 + (nh + 1) * 512],
                                                        greg, AF.Silu),
                                           deps=[mm], inc=s_act)
                                sil_tok[(hc, nh)] = sil
                                def mmu(e, par=par, hcl=hcl, nh=nh, ureg=ureg, ic=ic):
                                    for dc in range(NDC):
                                        ins = e.matmul(
                                            ureg,
                                            sb["wu_t"][:, par * 2048 + dc * 256 + hcl * 128:
                                                       par * 2048 + dc * 256 + hcl * 128 + 128],
                                            xnc(dc, ic * 1024 + nh * 512, ic * 1024 + (nh + 1) * 512),
                                            start=(dc == 0), stop=(dc == NDC - 1))
                                    return ins
                                mmu_t = P.op("tensor", mmu,
                                             deps=[wt[1], h_tok.get((hc - 2, nh))], inc=s_pe)
                                gu_guard[par] = mmu_t
                                ht = P.op("vector", lambda e, hc=hc, nh=nh, pg=pg, ureg=ureg:
                                          e.tensor_tensor(
                                    sb["hbuf"][:, hc * 1024 + nh * 512: hc * 1024 + (nh + 1) * 512],
                                    sb["gsil"][:, pg * 1024 + nh * 512: pg * 1024 + (nh + 1) * 512],
                                    ureg, ALU.mult), deps=[sil, mmu_t], inc=s_dve)
                            else:
                                ht = P.op("scalar", lambda e, hc=hc, nh=nh, e_=e_, greg=greg,
                                          actf=actf: e.activation(
                                    sb["hbuf"][:, hc * 1024 + nh * 512: hc * 1024 + (nh + 1) * 512],
                                    greg, actf[e_],
                                    bias=sb["b1fm"][:, e_ * 32 + hc: e_ * 32 + hc + 1]),
                                    deps=[mm, b1l], inc=s_act)
                            h_tok[(hc, nh)] = ht
                h_last = dve_now() if li == 0 else act_now()

                for dh in (0, 1):
                    wdld = {}

                    def wd_load(hq, dh=dh, e_=e_, wd_d=wd_d):
                        par = hq % 2
                        t = P.op("gpsimd", lambda e, hq=hq, dh=dh, par=par, e_=e_: e.dma_start(
                            out=sb["wd_t"].ap().rearrange("p (b q d) -> p b q d", b=2, d=512)[:, par],
                            in_=wd_d.ap()[e_][hq * 512:(hq + 1) * 512, dh * 512:(dh + 1) * 512]
                            .rearrange("(q p) d -> p q d", p=128)),
                            deps=[wd_guard[par], h_last], inc=s_wd2[par], inck=16)
                        wdld[hq] = t
                        return t

                    wd_load(0)
                    mm_last = None
                    for hq in range(8):
                        if hq + 1 < 8 and (hq + 1) not in wdld:
                            wd_load(hq + 1)
                        par = hq % 2
                        for ql in range(4):
                            hc = hq * 4 + ql
                            def mmd(e, hc=hc, ql=ql, par=par):
                                for isub in range(8):
                                    ins = e.matmul(
                                        ps[:, isub * 512:(isub + 1) * 512],
                                        sb["hbuf"][:, hc * 1024 + isub * 128: hc * 1024 + (isub + 1) * 128],
                                        sb["wd_t"][:, par * 2048 + ql * 512:(par * 2048) + (ql + 1) * 512],
                                        start=(hc == 0), stop=(hc == 31))
                                return ins
                            dd = [h_last, wdld[hq]]
                            if hc == 0:
                                dd.append(evac_last)
                            mm_last = P.op("tensor", mmd, deps=dd, inc=s_pe)
                            wd_guard[par] = mm_last
                    evs = []
                    for isub in range(8):
                        c = ic * 8 + isub
                        if li == 0 and e_ == 7:
                            fn = lambda e, c=c, isub=isub, dh=dh: e.scalar_tensor_tensor(
                                xch(c)[:, dh * 512:(dh + 1) * 512],
                                ps[:, isub * 512:(isub + 1) * 512], 1.0,
                                xch(c)[:, dh * 512:(dh + 1) * 512], ALU.mult, ALU.add)
                        else:
                            fn = lambda e, c=c, isub=isub, dh=dh, e_=e_: e.scalar_tensor_tensor(
                                xch(c)[:, dh * 512:(dh + 1) * 512],
                                ps[:, isub * 512:(isub + 1) * 512],
                                sb["comb"][:, c * 8 + e_: c * 8 + e_ + 1],
                                xch(c)[:, dh * 512:(dh + 1) * 512], ALU.mult, ALU.add)
                        ev = P.op("vector", fn, deps=[mm_last, comb_dep] + x_dep, inc=s_dve)
                        evs.append(ev)
                    evac_last = evs[-1]
                    st["x_dep"] = st["x_dep"] + [evac_last]
                moe_dep = [evac_last]
                x_dep = st["x_dep"]

        st["x_dep"] = [dve_now()]
        st["exp_last2"] = [None, None]

    P.op("sync", lambda e: e.dma_start(
        out=x_out.ap().rearrange("(c p) d -> p c d", p=128),
        in_=xsb.ap().rearrange("p (c d) -> p c d", d=1024)),
        deps=st["x_dep"] + [pe_now(), act_now()], inc=s_st, inck=16)
    P.op("sync", lambda e: e.nop(), deps=[(s_st, s_st.v)])

    P.emit()
    return nc



def _swap_weight(w, rot):
    out = np.zeros_like(w)
    for dd in range(rot):
        cols = np.arange(H) * DH + dd
        src = np.arange(H) * DH + (dd ^ 1)
        out[:, cols] = w[:, src]
    return out


def _swap_bias(b, rot):
    out = np.zeros_like(b)
    for dd in range(rot):
        idx = np.arange(H) * DH + dd
        src = np.arange(H) * DH + (dd ^ 1)
        out[idx] = b[src]
    return out


def _fm(b):
    return np.ascontiguousarray(b.reshape(8, 128).T.astype(np.float32))


def _rope_tables(rot, t0):
    m = np.arange(rot // 2)
    inv = 10000.0 ** (-(2.0 * m) / rot)
    t = np.arange(t0, t0 + T, dtype=np.float64)
    ang = t[:, None] * inv[None, :]
    cos = np.cos(ang)
    sin = np.sin(ang)
    cosA = np.zeros((128, T), np.float32)
    sinB = np.zeros((128, T), np.float32)
    for d in range(rot):
        cc = cos[:, d // 2]
        ss = sin[:, d // 2] * (-1.0 if d % 2 == 0 else 1.0)
        cosA[d] = cc
        cosA[64 + d] = cc
        sinB[d] = ss
        sinB[64 + d] = ss
    return (np.ascontiguousarray(cosA.astype(ml_dtypes.bfloat16)),
            np.ascontiguousarray(sinB.astype(ml_dtypes.bfloat16)))


def _layer_inputs(li, inputs, half):
    f32 = np.float32
    o = {}
    if li == 0:
        wq = np.asarray(inputs["L0_attn_Wq"], f32)
        wkv = np.asarray(inputs["L0_attn_Wkv"], f32)
        o["l0_wq"] = wq
        o["l0_wqs"] = _swap_weight(wq, 32)
        o["l0_wkv"] = wkv
        o["l0_wks"] = _swap_weight(np.ascontiguousarray(wkv[:, 0:D]), 32)
        o["l0_wout"] = np.asarray(inputs["L0_attn_Wout"], f32)
        o["l0_bout"] = np.asarray(inputs["L0_attn_bout"], f32).reshape(1, D)
        rw = np.asarray(inputs["L0_ffn_router_w"], f32)
        o["l0_rwT"] = np.ascontiguousarray(rw.T.reshape(8, 128, 7).transpose(1, 0, 2).reshape(128, 56))
        o["l0_wg"] = np.ascontiguousarray(np.concatenate(
            [np.asarray(inputs["L0_ffn_w_gate"], f32), np.asarray(inputs["L0_ffn_sw_gate"], f32)], 0))
        o["l0_wu"] = np.ascontiguousarray(np.concatenate(
            [np.asarray(inputs["L0_ffn_w_up"], f32), np.asarray(inputs["L0_ffn_sw_up"], f32)], 0))
        o["l0_wd"] = np.ascontiguousarray(np.concatenate(
            [np.asarray(inputs["L0_ffn_w_down"], f32), np.asarray(inputs["L0_ffn_sw_down"], f32)], 0))
        o["l0_cos"], o["l0_sin"] = _rope_tables(32, half * T)
    else:
        wqkv = np.asarray(inputs["L1_attn_Wqkv"], f32)
        bqkv = np.asarray(inputs["L1_attn_bqkv"], f32)
        o["l1_wqkv"] = wqkv
        o["l1_wqs"] = _swap_weight(np.ascontiguousarray(wqkv[:, 0:D]), 64)
        o["l1_wks"] = _swap_weight(np.ascontiguousarray(wqkv[:, D:2 * D]), 64)
        bq, bk, bv = bqkv[0:D], bqkv[D:2 * D], bqkv[2 * D:3 * D]
        o["l1_bias_fm"] = np.ascontiguousarray(np.concatenate(
            [_fm(bq), _fm(_swap_bias(bq, 64)), _fm(bk), _fm(_swap_bias(bk, 64))], 1))
        o["l1_bv"] = np.ascontiguousarray(bv.reshape(1, D).astype(f32))
        o["l1_wo"] = np.asarray(inputs["L1_attn_Wo"], f32)
        o["l1_bo"] = np.asarray(inputs["L1_attn_bo"], f32).reshape(1, D)
        o["l1_gamma"] = np.asarray(inputs["L1_attn_gamma"], f32).reshape(1, D)
        o["l1_beta"] = np.asarray(inputs["L1_attn_beta"], f32).reshape(1, D)
        o["l1_w1"] = np.asarray(inputs["L1_ffn_W1"], f32)
        b1 = np.asarray(inputs["L1_ffn_b1"], f32)
        o["l1_b1_fm"] = np.ascontiguousarray(b1.reshape(8, 32, 128).transpose(2, 0, 1).reshape(128, 256))
        o["l1_w2"] = np.asarray(inputs["L1_ffn_W2"], f32)
        o["l1_b2"] = np.asarray(inputs["L1_ffn_b2"], f32)
        wg = np.asarray(inputs["L1_ffn_Wg"], f32)
        o["l1_wgT"] = np.ascontiguousarray(wg.reshape(8, 128, 8).transpose(1, 0, 2).reshape(128, 64))
        o["l1_bg"] = np.asarray(inputs["L1_ffn_bg"], f32).reshape(1, 8)
        o["l1_cos"], o["l1_sin"] = _rope_tables(64, half * T)
    return o


def _const_inputs():
    return {
        "ones_f": np.ones((1, 128), np.float32),
        "ident_f": np.eye(128, dtype=np.float32),
        "ident_b": np.eye(128, dtype=np.float32).astype(ml_dtypes.bfloat16),
    }


def kernel(**inputs):
    xfull = np.asarray(inputs["L0_attn_queries"], np.float32)
    consts = _const_inputs()
    xs = []
    for c in range(NCORES):
        b, half = c // 2, c % 2
        xs.append(np.ascontiguousarray(xfull[b, half * T:(half + 1) * T]))

    lay = {li: [_layer_inputs(li, inputs, half) for half in (0, 1)] for li in (0, 1)}

    nc0 = build_program([0])
    maps0 = []
    for c in range(NCORES):
        m = {"x": xs[c]}
        m.update(consts)
        m.update(lay[0][c % 2])
        maps0.append(m)
    res0 = run_bass_kernel_spmd(nc0, maps0, core_ids=list(range(NCORES)))

    nc1 = build_program([1])
    maps1 = []
    for c in range(NCORES):
        m = {"x": np.ascontiguousarray(res0.results[c]["xout"])}
        m.update(consts)
        m.update(lay[1][c % 2])
        maps1.append(m)
    res1 = run_bass_kernel_spmd(nc1, maps1, core_ids=list(range(NCORES)))

    out = np.empty((BATCH, SEQ, D), np.float32)
    for c in range(NCORES):
        b, half = c // 2, c % 2
        out[b, half * T:(half + 1) * T] = res1.results[c]["xout"]
    return out


# version 03
# speedup vs baseline: 1.0111x; 1.0111x over previous
import numpy as np
import ml_dtypes
import concourse.bass as bass
import concourse.mybir as mybir
from concourse.bass_utils import run_bass_kernel_spmd

F32 = mybir.dt.float32
BF16 = mybir.dt.bfloat16
AF = mybir.ActivationFunctionType
ALU = mybir.AluOpType
AX = mybir.AxisListType
AP = bass.AP

NCORES = 8
BATCH, SEQ, D = 4, 4096, 1024
T = 2048
H, DH = 16, 64
HID = 4096
SCALE = 0.125
EPS = 1e-5
NCH = T // 128
NDC = D // 128
NGRP = 8
NEG = -30000.0
MOE_EORDER = None
XN_EVAC_DVE = True
KIN_E = 128 * T
VIN_E = T * 130
KVE = KIN_E + VIN_E
ENGINES = ("tensor", "vector", "scalar", "gpsimd", "sync")
RG_PAIRS = [[0, 1], [2, 3], [4, 5], [6, 7]]


class Sem:
    def __init__(self, nc, name):
        self.h = nc.alloc_semaphore(name=name)
        self.name = name
        self.v = 0


class Plan:
    def __init__(self, nc):
        self.nc = nc
        self.ops = {e: [] for e in ENGINES}
        self.waits = {e: {} for e in ENGINES}
        self.sems = {}
        self.last = {}

    def sem(self, name):
        if name not in self.sems:
            self.sems[name] = Sem(self.nc, name)
        return self.sems[name]

    def op(self, eng, fn, deps=(), inc=None, inck=1):
        if eng in ("vector", "scalar") and eng in self.last:
            deps = list(deps) + [self.last[eng]]
        need = []
        for d in deps:
            if d is None:
                continue
            s, v = d
            if self.waits[eng].get(s.name, 0) < v:
                self.waits[eng][s.name] = v
                need.append((s.h, v))
        tok = None
        if inc is not None:
            inc.v += inck
            tok = (inc, inc.v)

        def run(e, fn=fn, need=tuple(need), inc=inc, inck=inck):
            for h, v in need:
                e.wait_ge(h, v)
            ins = fn(e)
            if inc is not None:
                ins.then_inc(inc.h, inck)

        self.ops[eng].append(run)
        if tok is not None and eng in ("vector", "scalar"):
            self.last[eng] = tok
        return tok

    def emit(self):
        with self.nc.Block() as block:
            for eng in ENGINES:
                fns = self.ops[eng]

                def body(e, fns=fns):
                    for f in fns:
                        f(e)

                getattr(block, eng)(body)



def build_program(layers, dbg=(), upto="full"):
    nc = bass.Bass("TRN2", target_bir_lowering=False)
    P = Plan(nc)
    dbg = set(dbg)

    x_in = nc.dram_tensor("x", [T, D], F32, kind="ExternalInput")
    x_out = nc.dram_tensor("xout", [T, D], F32, kind="ExternalOutput")

    dins = {}

    def din(name, shape, dtype=F32):
        dins[name] = nc.dram_tensor(name, shape, dtype, kind="ExternalInput")

    din("ones_f", [1, 128])
    din("ident_f", [128, 128])
    din("ident_b", [128, 128], BF16)
    for li in layers:
        p = f"l{li}_"
        if li == 0:
            din(p + "wq", [D, D]); din(p + "wqs", [D, D])
            din(p + "wkv", [D, 2 * D]); din(p + "wks", [D, D])
            din(p + "wout", [D, D]); din(p + "bout", [1, D])
            din(p + "rwT", [128, 56])
            din(p + "wg", [8, D, HID]); din(p + "wu", [8, D, HID])
            din(p + "wd", [8, HID, D])
        else:
            din(p + "wqkv", [D, 3 * D]); din(p + "wqs", [D, D]); din(p + "wks", [D, D])
            din(p + "bias_fm", [128, 32])
            din(p + "bv", [1, D])
            din(p + "wo", [D, D]); din(p + "bo", [1, D])
            din(p + "gamma", [1, D]); din(p + "beta", [1, D])
            din(p + "w1", [8, D, HID]); din(p + "b1_fm", [128, 256])
            din(p + "w2", [8, HID, D]); din(p + "b2", [8, D])
            din(p + "wgT", [128, 64]); din(p + "bg", [1, 8])
        din(p + "cos", [128, T], BF16)
        din(p + "sin", [128, T], BF16)

    kv_in = nc.dram_tensor("kv_in", [KVE, 1], BF16, kind="Internal")
    kv_out = nc.dram_tensor("kv_out", [2 * KVE, 1], BF16, kind="Internal")

    sb = {}

    ARENA0 = 16512

    def sba(name, shape, dtype, off):
        sb[name] = nc.alloc_sbuf_tensor_at(name, shape, dtype, offset=ARENA0 + off)

    B0 = 98304
    sba("x", [128, 16 * 1024], F32, 0)
    sba("xnfm", [128, 8 * 2048], BF16, 65536)
    sba("hbuf", [128, 32 * 1024], BF16, B0)
    sba("khat", [128, 4096], BF16, B0 + 0)
    sba("vhat", [128, 32 * 130], BF16, B0 + 8192)
    sba("wo_t", [128, 8 * 1024], BF16, B0 + 0)
    sba("vown", [128, 16 * 130], BF16, B0 + 16640)
    sba("pt0", [128, 1024], BF16, B0 + 20800)
    sba("pt1", [128, 1024], BF16, B0 + 22848)
    sba("cos_t", [128, T], BF16, B0 + 24896)
    sba("sin_t", [128, T], BF16, B0 + 28992)
    sba("row0", [1, 1024], F32, B0 + 33088)
    sba("gb_bc", [128, 1024], BF16, B0 + 37184)
    sba("bb_bc", [128, 1024], BF16, B0 + 39232)
    sba("wk_t", [128, 1024], BF16, B0 + 41280)
    sba("wks_t", [128, 1024], BF16, B0 + 43328)
    sba("wv_t", [128, 1024], BF16, B0 + 45376)
    sba("wq_t", [128, 1024], BF16, B0 + 47424)
    sba("wqs_t", [128, 1024], BF16, B0 + 49472)
    sba("fmscr", [128, 1024], BF16, B0 + 51520)
    sba("combfm", [8, 128], BF16, B0 + 53568)
    sba("rtmp", [128, 2048], BF16, B0 + 53824)
    A0 = B0 + 65536
    sba("atm", [128, 16 * 1024], BF16, A0)
    sba("wg_t", [128, 4096], BF16, A0 + 0)
    sba("wu_t", [128, 4096], BF16, A0 + 8192)
    sba("wd_t", [128, 4096], BF16, A0 + 16384)
    sba("gsil", [128, 2048], BF16, A0 + 24576)
    Q0 = A0 + 32768
    sba("scr0", [128, 1024], F32, Q0)
    sba("scr1", [128, 1024], F32, Q0 + 4096)
    sba("kown", [128, 2048], BF16, Q0 + 0)
    sba("kswp", [128, 2048], BF16, Q0 + 4096)
    sba("qbuf", [128, 2048], BF16, Q0 + 0)
    sba("qswp", [128, 2048], BF16, Q0 + 4096)
    sba("b1fm", [128, 256], F32, Q0 + 0)
    sba("b2bf", [8, 1024], BF16, Q0 + 1024)
    P0 = Q0 + 8192
    sba("ones_f", [1, 128], F32, P0 + 0)
    sba("ident_f", [128, 128], F32, P0 + 512)
    sba("ident_b", [128, 128], BF16, P0 + 1024)
    sba("ones_c", [128, 1], BF16, P0 + 1280)
    sba("mu", [128, 16], F32, P0 + 1312)
    sba("rstd", [128, 16], F32, P0 + 1376)
    sba("sums", [128, 16], F32, P0 + 1440)
    sba("sumsq", [128, 16], F32, P0 + 1504)
    sba("muh", [128, 1], F32, P0 + 1568)
    sba("rstdh", [128, 1], F32, P0 + 1600)
    sba("ssq2", [128, 1], F32, P0 + 1632)
    sba("comb", [128, 128], F32, P0 + 1664)
    sba("negB", [128, 4], F32, P0 + 2176)
    sba("nrm", [1, 64], F32, P0 + 2208)
    sba("lg", [128, 8], F32, P0 + 2464)
    sba("m1", [128, 1], F32, P0 + 2496)
    sba("m2", [128, 1], F32, P0 + 2528)
    sba("eq", [128, 8], F32, P0 + 2560)
    sba("rden", [128, 1], F32, P0 + 2592)
    sba("rwT", [128, 64], BF16, P0 + 2624)
    sba("bgrow", [1, 8], F32, P0 + 2752)
    sba("biasfm", [128, 32], F32, P0 + 2784)
    sba("rcol", [128, 1], F32, P0 + 2912)
    assert ARENA0 + P0 + 2944 <= 229344, P0 + 2944
    nc.sbuf_base = ARENA0 + P0 + 2944 + (32 - (P0 + 2944) % 32) % 32

    ps = nc.alloc_psum_tensor("ps", [128, 4096], F32)
    psb = ps.bitcast(BF16)

    s_ldx = P.sem("ldx"); s_ldc = P.sem("ldc"); s_rw = P.sem("rw")
    s_bnc = P.sem("bnc"); s_cc = P.sem("cc"); s_rel = P.sem("rel")
    s_pe = P.sem("pe"); s_act = P.sem("act"); s_dve = P.sem("dve"); s_st = P.sem("st")
    s_wt = {n: P.sem("w" + n) for n in ("k", "ks", "v", "q", "qs", "o")}
    s_wg2 = [P.sem("wg0"), P.sem("wg1")]
    s_wu2 = [P.sem("wu0"), P.sem("wu1")]
    s_wd2 = [P.sem("wd0"), P.sem("wd1")]

    xsb, xn, ones_f = sb["x"], sb["xnfm"], sb["ones_f"]

    def xch(c):
        return xsb[:, c * 1024:(c + 1) * 1024]

    def xnc(dc, a, b_):
        return xn[:, dc * 2048 + a: dc * 2048 + b_]

    def pe_now():
        return (s_pe, s_pe.v)

    def dve_now():
        return (s_dve, s_dve.v)

    def act_now():
        return (s_act, s_act.v)

    dbg_t = {}

    def tap(name, shape, dtype, ap_fn, deps):
        if name not in dbg:
            return
        t = nc.dram_tensor("dbg_" + name, shape, dtype, kind="ExternalOutput")
        dbg_t[name] = t
        P.op("sync", lambda e: e.dma_start(out=t.ap(), in_=ap_fn()), deps=deps,
             inc=s_st, inck=16)

    x_ld = P.op("sync", lambda e: e.dma_start(
        out=xsb.ap().rearrange("p (c d) -> p c d", d=1024),
        in_=x_in.ap().rearrange("(c p) d -> p c d", p=128)),
        inc=s_ldx, inck=16)
    P.op("sync", lambda e: e.dma_start(out=ones_f[:, :], in_=dins["ones_f"].ap()),
         inc=s_ldc, inck=16)
    P.op("sync", lambda e: e.dma_start(out=sb["ident_f"][:, :], in_=dins["ident_f"].ap()),
         inc=s_ldc, inck=16)
    P.op("sync", lambda e: e.dma_start(out=sb["ident_b"][:, :], in_=dins["ident_b"].ap()),
         inc=s_ldc, inck=16)
    c0 = (s_ldc, s_ldc.v)
    c_ones = c_idf = c_idb = c0
    c_onesc = P.op("vector", lambda e: e.memset(sb["ones_c"][:, :], 1.0), inc=s_dve)
    P.op("vector", lambda e: e.memset(sb["comb"][:, :], 0.0), inc=s_dve)

    st = {"x_dep": [x_ld], "cc_free": None, "rel_done": [], "kb": None, "vb": None,
          "exp_last2": [None, None]}

    for li in layers:
        if upto == "x":
            break
        p = f"l{li}_"
        ROT = 32 if li == 0 else 64
        NE = 7 if li == 0 else 8
        layer_start = [pe_now(), dve_now(), act_now()]

        P.op("sync", lambda e, p=p: e.dma_start(out=sb["cos_t"][:, :], in_=dins[p + "cos"].ap()),
             deps=layer_start + [(s_ldc, s_ldc.v)], inc=s_ldc, inck=16)
        P.op("sync", lambda e, p=p: e.dma_start(out=sb["sin_t"][:, :], in_=dins[p + "sin"].ap()),
             inc=s_ldc, inck=16)
        if li == 0:
            c_rw = P.op("gpsimd", lambda e, p=p: e.dma_start(out=sb["rwT"][:, 0:56], in_=dins[p + "rwT"].ap()),
                        deps=layer_start, inc=s_rw, inck=16)
            cbatch = (s_ldc, s_ldc.v)
            c_cos = c_sin = cbatch
            c_bg = c_bias = c_gb = None
        else:
            c_rw = P.op("gpsimd", lambda e, p=p: e.dma_start(out=sb["rwT"][:, 0:64], in_=dins[p + "wgT"].ap()),
                        deps=layer_start, inc=s_rw, inck=16)
            P.op("sync", lambda e, p=p: e.dma_start(out=sb["bgrow"][:, :], in_=dins[p + "bg"].ap()),
                 inc=s_ldc, inck=16)
            P.op("sync", lambda e, p=p: e.dma_start(out=sb["biasfm"][:, :], in_=dins[p + "bias_fm"].ap()),
                 inc=s_ldc, inck=16)
            cbatch = (s_ldc, s_ldc.v)
            c_cos = c_sin = c_bg = c_bias = cbatch
            prev_ev = cbatch
            for nm, dst in (("gamma", "gb_bc"), ("beta", "bb_bc")):
                r = P.op("sync", lambda e, nm=nm, p=p: e.dma_start(out=sb["row0"][:, :], in_=dins[p + nm].ap()),
                         deps=layer_start + [prev_ev], inc=s_ldc, inck=16)
                mm = P.op("tensor", lambda e: (
                    e.matmul(ps[:, 0:512], ones_f[:, :], sb["row0"][:, 0:512], start=True, stop=True),
                    e.matmul(ps[:, 512:1024], ones_f[:, :], sb["row0"][:, 512:1024], start=True, stop=True))[-1],
                    deps=[r, c_ones] + layer_start, inc=s_pe)
                ev = P.op("scalar", lambda e, dst=dst: (
                    e.activation(sb[dst][:, 0:512], ps[:, 0:512], AF.Copy),
                    e.activation(sb[dst][:, 512:1024], ps[:, 512:1024], AF.Copy))[-1],
                    deps=[mm], inc=s_act)
                prev_ev = ev
            c_gb = prev_ev

        x_dep = st["x_dep"]

        for c in range(NCH):
            P.op("vector", lambda e, c=c: e.reduce_sum(sb["sums"][:, c:c + 1], xch(c), AX.X),
                 deps=x_dep + layer_start, inc=s_dve)
            P.op("vector", lambda e, c=c: e.scalar_tensor_tensor(
                sb["scr0"][:, :], xch(c), 1.0, xch(c), ALU.mult, ALU.mult,
                accum_out=sb["sumsq"][:, c:c + 1]), inc=s_dve)
        stat_in = dve_now()
        P.op("vector", lambda e: e.tensor_scalar_mul(sb["mu"][:, :], sb["sums"][:, :], 1.0 / D),
             deps=[stat_in], inc=s_dve)
        P.op("vector", lambda e: e.tensor_mul(sb["rstd"][:, :], sb["mu"][:, :], sb["mu"][:, :]),
             inc=s_dve)
        P.op("vector", lambda e: e.scalar_tensor_tensor(
            sb["rstd"][:, :], sb["sumsq"][:, :], 1.0 / D, sb["rstd"][:, :], ALU.mult, ALU.subtract),
            inc=s_dve)
        P.op("vector", lambda e: e.tensor_scalar_add(sb["rstd"][:, :], sb["rstd"][:, :], EPS),
             inc=s_dve)
        var_dep = dve_now()
        sq = P.op("scalar", lambda e: e.activation(sb["rstd"][:, :], sb["rstd"][:, :], AF.Sqrt),
                  deps=[var_dep], inc=s_act)
        stats_dep = P.op("vector", lambda e: e.reciprocal(sb["rstd"][:, :], sb["rstd"][:, :]),
                         deps=[sq], inc=s_dve)

        if upto == "stats":
            break
        ev_prev = [None, None]
        tp_last = {}
        _nch = 2 if upto in ("lnt1", "xntm1", "tponly") else NCH
        for c in range(_nch):
            scr = sb["scr0"] if c % 2 == 0 else sb["scr1"]
            deps = [stats_dep] + x_dep
            if c >= 2:
                deps.append(tp_last[c - 2])
            xn_tm = P.op("vector", lambda e, c=c, scr=scr: e.tensor_scalar(
                scr[:, :], xch(c), sb["mu"][:, c:c + 1], sb["rstd"][:, c:c + 1],
                ALU.subtract, ALU.mult), deps=deps, inc=s_dve)
            if upto in ("xntm", "xntm1"):
                tp_last[c] = xn_tm
                if upto == "xntm1" or c >= _nch - 1:
                    pass
                continue
            last = None
            for dc in range(NDC):
                reg = ps[:, 1024 + (dc % 2) * 512: 1024 + (dc % 2) * 512 + 128]
                mm = P.op("tensor", lambda e, scr=scr, dc=dc, reg=reg: e.transpose(
                    reg, scr[:, dc * 128:(dc + 1) * 128], sb["ident_f"][:, :]),
                    deps=[xn_tm, c_idf, ev_prev[dc % 2]], inc=s_pe)
                if upto == "tponly":
                    ev_prev[dc % 2] = mm
                    last = mm
                    continue
                if XN_EVAC_DVE:
                    ev = P.op("vector", lambda e, c=c, dc=dc, reg=reg: e.tensor_copy(
                        xnc(dc, c * 128, (c + 1) * 128), reg), deps=[mm], inc=s_dve)
                else:
                    ev = P.op("scalar", lambda e, c=c, dc=dc, reg=reg: e.activation(
                        xnc(dc, c * 128, (c + 1) * 128), reg, AF.Copy),
                        deps=[mm], inc=s_act)
                ev_prev[dc % 2] = ev
                last = ev
            tp_last[c] = last
        if upto in ("lnt1", "xntm", "xntm1", "tponly"):
            break
        xn_dep = [tp_last[c] for c in range(NCH)]
        tap(f"l{li}_xn", [128, NDC * 2048], BF16, lambda: xn.ap(), xn_dep)
        if upto == "lnt":
            break

        lg_cp = None
        for c in range(NCH):
            def mm_lg(e, c=c, NE=NE, li=li):
                for dc in range(NDC):
                    ins = e.matmul(ps[:, 2048:2048 + NE],
                                   xnc(dc, c * 128, (c + 1) * 128),
                                   sb["rwT"][:, dc * NE:(dc + 1) * NE],
                                   start=(dc == 0), stop=(dc == NDC - 1 and li == 0))
                if li == 1:
                    ins = e.matmul(ps[:, 2048:2048 + NE], ones_f[:, 0:128],
                                   sb["bgrow"][:, 0:NE], start=False, stop=True)
                return ins
            mm = P.op("tensor", mm_lg,
                      deps=[tp_last[c], c_rw, c_ones, lg_cp] + ([c_bg] if li == 1 else []),
                      inc=s_pe)
            lg_cp = P.op("vector", lambda e, NE=NE: e.tensor_copy(sb["lg"][:, 0:NE], ps[:, 2048:2048 + NE]),
                         deps=[mm], inc=s_dve)
            P.op("vector", lambda e, NE=NE: e.reduce_max(sb["m1"][:, :], sb["lg"][:, 0:NE],
                                                         AX.X, negate=True), inc=s_dve)
            m1d = dve_now()
            P.op("vector", lambda e, NE=NE: e.tensor_scalar(
                sb["eq"][:, 0:NE], sb["lg"][:, 0:NE], sb["m1"][:, :], None, ALU.add, ALU.bypass),
                deps=[m1d], inc=s_dve)
            P.op("vector", lambda e, NE=NE: e.tensor_scalar(
                sb["eq"][:, 0:NE], sb["eq"][:, 0:NE], 0.0, NEG, ALU.is_equal, ALU.mult),
                inc=s_dve)
            P.op("vector", lambda e, NE=NE: e.tensor_tensor(
                sb["eq"][:, 0:NE], sb["eq"][:, 0:NE], sb["lg"][:, 0:NE], ALU.add), inc=s_dve)
            P.op("vector", lambda e, NE=NE: e.reduce_max(sb["m2"][:, :], sb["eq"][:, 0:NE], AX.X),
                 inc=s_dve)
            m2d = dve_now()
            en = P.op("scalar", lambda e, NE=NE: e.activation(
                sb["eq"][:, 0:NE], sb["lg"][:, 0:NE], AF.Exp, bias=sb["m1"][:, :]),
                deps=[m2d], inc=s_act)
            e2 = P.op("scalar", lambda e: e.activation(
                sb["rden"][:, :], sb["m2"][:, :], AF.Exp, bias=sb["m1"][:, :]), inc=s_act)
            P.op("vector", lambda e: e.tensor_scalar_add(sb["rden"][:, :], sb["rden"][:, :], 1.0),
                 deps=[e2], inc=s_dve)
            P.op("vector", lambda e: e.reciprocal(sb["rden"][:, :], sb["rden"][:, :]), inc=s_dve)
            rdend = dve_now()
            P.op("vector", lambda e, NE=NE: e.tensor_scalar(
                sb["lg"][:, 0:NE], sb["lg"][:, 0:NE], sb["m2"][:, :], None, ALU.is_ge, ALU.bypass),
                deps=[rdend, en], inc=s_dve)
            P.op("vector", lambda e, NE=NE: e.tensor_tensor(
                sb["lg"][:, 0:NE], sb["lg"][:, 0:NE], sb["eq"][:, 0:NE], ALU.mult), inc=s_dve)
            P.op("vector", lambda e, c=c, NE=NE: e.tensor_scalar(
                sb["comb"][:, c * 8:c * 8 + NE], sb["lg"][:, 0:NE], sb["rden"][:, :], None,
                ALU.mult, ALU.bypass), inc=s_dve)
        comb_dep = dve_now()
        tap(f"l{li}_comb", [128, 128], F32, lambda: sb["comb"].ap(), [comb_dep])

        if upto in ("ln", "route"):
            continue

        if li == 0:
            wq_d, wqs_d = dins[p + "wq"], dins[p + "wqs"]
            wk_d, wks_d, wv_d = dins[p + "wkv"], dins[p + "wks"], dins[p + "wkv"]
            KOFF, VOFF, QOFF = 0, D, 0
        else:
            wq_d, wqs_d = dins[p + "wqkv"], dins[p + "wqs"]
            wk_d, wks_d, wv_d = dins[p + "wqkv"], dins[p + "wks"], dins[p + "wqkv"]
            KOFF, VOFF, QOFF = D, 2 * D, 0
            bvr = P.op("sync", lambda e, p=p: e.dma_start(out=sb["row0"][:, :], in_=dins[p + "bv"].ap()),
                       deps=[c_gb], inc=s_ldc, inck=16)

        vown3 = sb["vown"].ap().rearrange("p (jt c) -> p jt c", c=130)
        vhat4 = sb["vhat"].ap().rearrange("p (r jt c) -> p r jt c", r=2, c=130)
        vhat3 = sb["vhat"].ap().rearrange("p (jt c) -> p jt c", c=130)
        atm = sb["atm"]

        def wload(dst, src_d, col0, sem, guard):
            return P.op("gpsimd", lambda e: e.dma_start(
                out=sb[dst].ap().rearrange("p (dc c) -> p dc c", c=128),
                in_=src_d.ap().rearrange("(dc p) c -> p dc c", p=128)[:, :, col0:col0 + 128]),
                deps=[guard], inc=sem, inck=16)

        a_done = None
        exp_hist = []
        prev_flash_act = None
        wguard = {n: layer_start[0] for n in ("k", "ks", "v", "q", "qs")}
        ngrp = 1 if upto == "g1" else NGRP
        for g in range(ngrp):
            hd0 = g * 128
            gp = g % 2
            wk_l = wload("wk_t", wk_d, KOFF + hd0, s_wt["k"], wguard["k"])
            wks_l = wload("wks_t", wks_d, hd0, s_wt["ks"], wguard["ks"])
            kev = []
            for bi, (buf, wt, wl, bcol) in enumerate(
                    (("kown", "wk_t", wk_l, 2), ("kswp", "wks_t", wks_l, 3))):
                for n in range(4):
                    reg = ps[:, (n % 2) * 512:(n % 2) * 512 + 512]
                    def mmk(e, wt=wt, n=n, reg=reg):
                        for dc in range(NDC):
                            ins = e.matmul(reg, sb[wt][:, dc * 128:(dc + 1) * 128],
                                           xnc(dc, n * 512, (n + 1) * 512),
                                           start=(dc == 0), stop=(dc == NDC - 1))
                        return ins
                    dd = [wl, prev_flash_act] + xn_dep
                    if len(kev) >= 2:
                        dd.append(kev[-2])
                    mm = P.op("tensor", mmk, deps=dd, inc=s_pe)
                    wguard["ks" if buf == "kswp" else "k"] = mm
                    bias = sb["biasfm"][:, bcol * 8 + g:bcol * 8 + g + 1] if li == 1 else 0.0
                    fcp = AF.Identity if li == 1 else AF.Copy
                    ev = P.op("scalar", lambda e, buf=buf, n=n, reg=reg, bias=bias, fcp=fcp:
                              e.activation(
                        sb[buf][:, n * 512:(n + 1) * 512], reg, fcp, bias=bias),
                        deps=[mm, st["kb"]] + ([c_bias] if li == 1 else []), inc=s_act)
                    kev.append(ev)
            kr = []
            for slot in (0, 64):
                rws = slice(slot, slot + ROT)
                P.op("vector", lambda e, rws=rws: e.tensor_tensor(
                    sb["rtmp"][rws, :], sb["kown"][rws, :], sb["cos_t"][rws, :], ALU.mult),
                    deps=kev + [c_cos, c_sin], inc=s_dve)
                P.op("vector", lambda e, rws=rws: e.tensor_tensor(
                    sb["kown"][rws, :], sb["kswp"][rws, :], sb["sin_t"][rws, :], ALU.mult),
                    inc=s_dve)
                t3 = P.op("vector", lambda e, rws=rws: e.tensor_tensor(
                    sb["kown"][rws, :], sb["kown"][rws, :], sb["rtmp"][rws, :], ALU.add),
                    inc=s_dve)
                kr.append(t3)
            ksq = P.op("vector", lambda e: e.tensor_tensor(
                sb["kswp"][:, :], sb["kown"][:, :], sb["kown"][:, :], ALU.mult),
                deps=kr, inc=s_dve)
            nk = []
            for hl in (0, 1):
                def mmn(e, hl=hl):
                    for n in range(4):
                        ins = e.matmul(ps[0:1, 2048 + n * 512: 2048 + (n + 1) * 512],
                                       sb["ones_c"][hl * 64:hl * 64 + 64, 0:1],
                                       sb["kswp"][hl * 64:hl * 64 + 64, n * 512:(n + 1) * 512],
                                       start=True, stop=True)
                    return ins
                mm = P.op("tensor", mmn, deps=[ksq, c_onesc] + nk, inc=s_pe)
                rd = P.op("vector", lambda e, hl=hl: e.reduce_max(
                    sb["nrm"][0:1, 2 + hl:3 + hl], ps[0:1, 2048:4096], AX.X),
                    deps=[mm], inc=s_dve)
                nk.append(rd)
            kb = P.op("sync", lambda e: e.dma_start(
                out=AP(kv_in, 0, [[2048, 128], [1, 2048]]),
                in_=sb["kown"][:, :]),
                deps=[ksq, st["cc_free"]], inc=s_bnc, inck=16)
            st["kb"] = kb

            wv_l = wload("wv_t", wv_d, VOFF + hd0, s_wt["v"], wguard["v"])
            vset = P.op("vector", lambda e: e.memset(
                sb["vown"].ap().rearrange("p (jt h c) -> p jt h c", h=2, c=65)[:, :, :, 64:65], 1.0),
                deps=[st["vb"]], inc=s_dve)
            vev = []
            for jt in range(NCH):
                reg = ps[:, (jt % 2) * 512:(jt % 2) * 512 + 128]
                def mmv(e, jt=jt, reg=reg, li=li, hd0=hd0):
                    for dc in range(NDC):
                        ins = e.matmul(reg, xnc(dc, jt * 128, (jt + 1) * 128),
                                       sb["wv_t"][:, dc * 128:(dc + 1) * 128],
                                       start=(dc == 0), stop=(dc == NDC - 1 and li == 0))
                    if li == 1:
                        ins = e.matmul(reg, ones_f[:, 0:128], sb["row0"][:, hd0:hd0 + 128],
                                       start=False, stop=True)
                    return ins
                dd = [wv_l] + xn_dep + ([bvr, c_ones] if li == 1 else [])
                if len(vev) >= 2:
                    dd.append(vev[-2])
                mm = P.op("tensor", mmv, deps=dd, inc=s_pe)
                wguard["v"] = mm
                ev = P.op("scalar", lambda e, jt=jt, reg=reg: e.activation(
                    vown3[:, jt, :].rearrange("p (h c) -> p h c", h=2, c=65)[:, :, 0:64],
                    reg.rearrange("p (h c) -> p h c", h=2, c=64), AF.Copy),
                    deps=[mm, vset], inc=s_act)
                vev.append(ev)
            vb = P.op("sync", lambda e: e.dma_start(
                out=AP(kv_in, KIN_E, [[130, 128], [130 * 128, 16], [1, 130]]),
                in_=vown3),
                deps=vev + [st["cc_free"]], inc=s_bnc, inck=16)
            bnc_all = (s_bnc, s_bnc.v)
            kb = bnc_all
            vb = bnc_all
            st["vb"] = bnc_all
            st["kb"] = bnc_all

            cc = P.op("gpsimd", lambda e: e.collective_compute(
                "AllGather", ALU.bypass, RG_PAIRS,
                ins=[kv_in.ap().opt()], outs=[kv_out.ap().opt()]),
                deps=[kb, vb] + st["rel_done"], inc=s_cc)
            st["cc_free"] = cc

            wq_l = wload("wq_t", wq_d, QOFF + hd0, s_wt["q"], wguard["q"])
            wqs_l = wload("wqs_t", wqs_d, hd0, s_wt["qs"], wguard["qs"])
            qev = []
            for bi, (buf, wt, wl, bcol) in enumerate(
                    (("qbuf", "wq_t", wq_l, 0), ("qswp", "wqs_t", wqs_l, 1))):
                for n in range(4):
                    reg = ps[:, (n % 2) * 512:(n % 2) * 512 + 512]
                    def mmq(e, wt=wt, n=n, reg=reg):
                        for dc in range(NDC):
                            ins = e.matmul(reg, sb[wt][:, dc * 128:(dc + 1) * 128],
                                           xnc(dc, n * 512, (n + 1) * 512),
                                           start=(dc == 0), stop=(dc == NDC - 1))
                        return ins
                    dd = [wl] + xn_dep
                    if len(qev) >= 2:
                        dd.append(qev[-2])
                    mm = P.op("tensor", mmq, deps=dd, inc=s_pe)
                    wguard["qs" if buf == "qswp" else "q"] = mm
                    bias = sb["biasfm"][:, bcol * 8 + g:bcol * 8 + g + 1] if li == 1 else 0.0
                    fcp = AF.Identity if li == 1 else AF.Copy
                    ev = P.op("scalar", lambda e, buf=buf, n=n, reg=reg, bias=bias, fcp=fcp:
                              e.activation(
                        sb[buf][:, n * 512:(n + 1) * 512], reg, fcp, bias=bias),
                        deps=[mm, kb] + ([c_bias] if li == 1 else []), inc=s_act)
                    qev.append(ev)
            qr = []
            for slot in (0, 64):
                rws = slice(slot, slot + ROT)
                P.op("vector", lambda e, rws=rws: e.tensor_tensor(
                    sb["rtmp"][rws, :], sb["qbuf"][rws, :], sb["cos_t"][rws, :], ALU.mult),
                    deps=qev + [c_cos, c_sin], inc=s_dve)
                P.op("vector", lambda e, rws=rws: e.tensor_tensor(
                    sb["qbuf"][rws, :], sb["qswp"][rws, :], sb["sin_t"][rws, :], ALU.mult),
                    inc=s_dve)
                t3 = P.op("vector", lambda e, rws=rws: e.tensor_tensor(
                    sb["qbuf"][rws, :], sb["qbuf"][rws, :], sb["rtmp"][rws, :], ALU.add),
                    inc=s_dve)
                qr.append(t3)
            qsq = P.op("vector", lambda e: e.tensor_tensor(
                sb["rtmp"][:, :], sb["qbuf"][:, :], sb["qbuf"][:, :], ALU.mult),
                deps=qr, inc=s_dve)
            nq = []
            for hl in (0, 1):
                def mmn(e, hl=hl):
                    for n in range(4):
                        ins = e.matmul(ps[0:1, 2048 + n * 512: 2048 + (n + 1) * 512],
                                       sb["ones_c"][hl * 64:hl * 64 + 64, 0:1],
                                       sb["rtmp"][hl * 64:hl * 64 + 64, n * 512:(n + 1) * 512],
                                       start=True, stop=True)
                    return ins
                mm = P.op("tensor", mmn, deps=[qsq, c_onesc] + nk + nq, inc=s_pe)
                rd = P.op("vector", lambda e, hl=hl: e.reduce_max(
                    sb["nrm"][0:1, hl:hl + 1], ps[0:1, 2048:4096], AX.X),
                    deps=[mm], inc=s_dve)
                nq.append(rd)
            P.op("vector", lambda e: e.tensor_tensor(
                sb["nrm"][0:1, 4:6], sb["nrm"][0:1, 0:2], sb["nrm"][0:1, 2:4], ALU.add),
                deps=nq + nk, inc=s_dve)
            nb2 = P.op("vector", lambda e: e.tensor_scalar_mul(
                sb["nrm"][0:1, 4:6], sb["nrm"][0:1, 4:6], -SCALE / 2.0), inc=s_dve)
            nbm = P.op("tensor", lambda e: e.matmul(
                ps[:, 2048:2050], ones_f[:, :], sb["nrm"][0:1, 4:6], start=True, stop=True),
                deps=[nb2, c_ones], inc=s_pe)
            negBd = P.op("vector", lambda e, gp=gp: e.tensor_copy(
                sb["negB"][:, gp * 2:gp * 2 + 2], ps[:, 2048:2050]),
                deps=[nbm, st["exp_last2"][gp]], inc=s_dve)

            P.op("sync", lambda e: e.dma_start(
                out=sb["khat"].ap().rearrange("p (r j) -> p r j", r=2),
                in_=AP(kv_out, 0, [[2048, 128], [KVE, 2], [1, 2048]])),
                deps=[cc, pe_now()], inc=s_rel, inck=16)
            P.op("sync", lambda e: e.dma_start(
                out=vhat3[:, 0:16, :],
                in_=AP(kv_out, KIN_E, [[130, 128], [130 * 128, 16], [1, 130]])),
                deps=[cc], inc=s_rel, inck=16)
            P.op("sync", lambda e: e.dma_start(
                out=vhat3[:, 16:32, :],
                in_=AP(kv_out, KVE + KIN_E, [[130, 128], [130 * 128, 16], [1, 130]])),
                inc=s_rel, inck=16)
            st["rel_done"] = [(s_rel, s_rel.v)]

            SREG = [ps[:, 0:1024], ps[:, 1024:2048]]
            OUTREG = [ps[:, 2048:3072], ps[:, 3072:4096]]
            pt = [sb["pt0"], sb["pt1"]]
            out_free = [None, None]
            qk_tok = {}
            exp_tok = {}
            pv_tok = {}
            last = None
            for ic in (0, 1):
                for j in range(32):
                    for hl in (0, 1):
                        slot = hl * 64
                        def mmqk(e, j=j, slot=slot, ic=ic, hl=hl):
                            for nh in (0, 1):
                                ins = e.matmul(
                                    SREG[hl][:, nh * 512:(nh + 1) * 512],
                                    sb["khat"][slot:slot + 64, j * 128:(j + 1) * 128],
                                    sb["qbuf"][slot:slot + 64,
                                               ic * 1024 + nh * 512: ic * 1024 + (nh + 1) * 512],
                                    start=True, stop=True)
                            return ins
                        dd = st["rel_done"] + [qr[hl], exp_tok.get((hl, ic, j - 1)),
                                               exp_tok.get((hl, ic - 1, 31))]
                        qk_tok[(hl, ic, j)] = P.op("tensor", mmqk, deps=dd, inc=s_pe)
                        dd = [qk_tok[(hl, ic, j)], negBd, pv_tok.get((hl, ic, j - 1)),
                              pv_tok.get((hl, ic - 1, 31))]
                        exp_tok[(hl, ic, j)] = P.op("scalar", lambda e, j=j, gp=gp, hl=hl:
                                                    e.activation(
                            pt[hl][:, :], SREG[hl], AF.Exp, scale=SCALE,
                            bias=sb["negB"][:, gp * 2 + hl:gp * 2 + hl + 1]),
                            deps=dd, inc=s_act)
                        def mmpv(e, j=j, hl=hl):
                            for isub in range(8):
                                ins = e.matmul(
                                    OUTREG[hl][:, isub * 128: isub * 128 + 65],
                                    pt[hl][:, isub * 128:(isub + 1) * 128],
                                    vhat3[:, j, hl * 65:(hl + 1) * 65],
                                    start=(j == 0 and isub % 4 == 0),
                                    stop=(j == 31 and isub % 4 == 3),
                                    skip_group_check=True)
                            return ins
                        pv_tok[(hl, ic, j)] = P.op("tensor", mmpv,
                                                   deps=[exp_tok[(hl, ic, j)],
                                                         out_free[hl]], inc=s_pe)
                for hl in (0, 1):
                    for isub in range(8):
                        rc = P.op("vector", lambda e, isub=isub, hl=hl: e.reciprocal(
                            sb["rcol"][:, :], OUTREG[hl][:, isub * 128 + 64: isub * 128 + 65]),
                            deps=[pv_tok[(hl, ic, 31)]], inc=s_dve)
                        wr = P.op("vector", lambda e, isub=isub, ic=ic, g=g, hl=hl:
                                  e.tensor_scalar(
                            atm[:, (ic * 8 + isub) * 1024 + (2 * g + hl) * 64:
                                (ic * 8 + isub) * 1024 + (2 * g + hl) * 64 + 64],
                            OUTREG[hl][:, isub * 128: isub * 128 + 64],
                            sb["rcol"][:, :], None, ALU.mult, ALU.bypass),
                            deps=[rc], inc=s_dve)
                        last = wr
                    out_free[hl] = last
            st["exp_last2"][gp] = exp_tok[(1, 1, 31)]
            a_done = last
            prev_flash_act = act_now()

        if upto == "g1":
            tap(f"l{li}_atm", [128, 16 * 128], BF16,
                lambda: atm.ap().rearrange("p (c h) -> p c h", h=1024)[:, :, 0:128],
                [a_done, prev_flash_act])
            continue
        tap(f"l{li}_atm", [128, 16 * 1024], BF16, lambda: atm.ap(),
            [a_done, prev_flash_act])

        if li == 0:
            wo_src, bo_src = dins[p + "wout"], dins[p + "bout"]
        else:
            wo_src, bo_src = dins[p + "wo"], dins[p + "bo"]
        wo_l = P.op("gpsimd", lambda e, wo_src=wo_src: e.dma_start(
            out=sb["wo_t"].ap().rearrange("p (dc c) -> p dc c", c=1024),
            in_=wo_src.ap().rearrange("(dc p) c -> p dc c", p=128)),
            deps=[prev_flash_act, dve_now()], inc=s_wt["o"], inck=16)
        bor = P.op("sync", lambda e, bo_src=bo_src: e.dma_start(
            out=sb["row0"][:, :], in_=bo_src.ap()),
            deps=[pe_now()], inc=s_ldc, inck=16)

        tp_prev = [None, None]
        proj_prev = [None, None]
        for c in range(NCH):
            fm_ev = []
            for dc in range(NDC):
                reg = psb[:, (dc % 2) * 1024:(dc % 2) * 1024 + 128]
                mm = P.op("tensor", lambda e, c=c, dc=dc, reg=reg: e.transpose(
                    reg, atm[:, c * 1024 + dc * 128: c * 1024 + (dc + 1) * 128],
                    sb["ident_b"][:, :]),
                    deps=[a_done, c_idb, tp_prev[dc % 2]], inc=s_pe)
                ev = P.op("scalar", lambda e, dc=dc, reg=reg: e.activation(
                    sb["fmscr"][:, dc * 128:(dc + 1) * 128], reg, AF.Copy),
                    deps=[mm], inc=s_act)
                tp_prev[dc % 2] = ev
                fm_ev.append(ev)
            preg = ps[:, 2048 + (c % 2) * 1024: 2048 + (c % 2) * 1024 + 1024]
            def mmproj(e, preg=preg):
                for nh in (0, 1):
                    for dc in range(NDC):
                        e.matmul(preg[:, nh * 512:(nh + 1) * 512],
                                 sb["fmscr"][:, dc * 128:(dc + 1) * 128],
                                 sb["wo_t"][:, dc * 1024 + nh * 512: dc * 1024 + (nh + 1) * 512],
                                 start=(dc == 0), stop=False)
                    ins = e.matmul(preg[:, nh * 512:(nh + 1) * 512], ones_f[:, 0:128],
                                   sb["row0"][:, nh * 512:(nh + 1) * 512],
                                   start=False, stop=True)
                return ins
            mm = P.op("tensor", mmproj,
                      deps=fm_ev + [wo_l, bor, c_ones, proj_prev[c % 2]], inc=s_pe)
            if li == 0:
                wr = P.op("vector", lambda e, c=c, preg=preg: (
                    e.tensor_tensor(xch(c)[:, 0:512], xch(c)[:, 0:512], preg[:, 0:512], ALU.add),
                    e.tensor_tensor(xch(c)[:, 512:1024], xch(c)[:, 512:1024], preg[:, 512:1024],
                                    ALU.add))[-1], deps=[mm] + x_dep, inc=s_dve)
                proj_prev[c % 2] = wr
                st["x_dep"] = st["x_dep"] + [wr]
            else:
                P.op("vector", lambda e, c=c: e.tensor_scalar(
                    sb["scr0"][:, :], xch(c), sb["mu"][:, c:c + 1], sb["rstd"][:, c:c + 1],
                    ALU.subtract, ALU.mult), deps=[mm] + x_dep, inc=s_dve)
                h2 = P.op("vector", lambda e, preg=preg: e.tensor_tensor(
                    sb["scr0"][:, :], sb["scr0"][:, :], preg, ALU.add), inc=s_dve)
                proj_prev[c % 2] = h2
                su = P.op("vector", lambda e: e.reduce_sum(sb["muh"][:, :], sb["scr0"][:, :], AX.X),
                          inc=s_dve)
                sq2 = P.op("vector", lambda e: e.scalar_tensor_tensor(
                    sb["scr1"][:, :], sb["scr0"][:, :], 1.0, sb["scr0"][:, :], ALU.mult, ALU.mult,
                    accum_out=sb["ssq2"][:, :]), inc=s_dve)
                P.op("vector", lambda e: e.tensor_scalar_mul(sb["muh"][:, :], sb["muh"][:, :], 1.0 / D),
                     deps=[su], inc=s_dve)
                P.op("vector", lambda e: e.tensor_mul(sb["rstdh"][:, :], sb["muh"][:, :], sb["muh"][:, :]),
                     inc=s_dve)
                P.op("vector", lambda e: e.scalar_tensor_tensor(
                    sb["rstdh"][:, :], sb["ssq2"][:, :], 1.0 / D, sb["rstdh"][:, :],
                    ALU.mult, ALU.subtract), deps=[sq2], inc=s_dve)
                vv = P.op("vector", lambda e: e.tensor_scalar_add(
                    sb["rstdh"][:, :], sb["rstdh"][:, :], EPS), inc=s_dve)
                sq3 = P.op("scalar", lambda e: e.activation(
                    sb["rstdh"][:, :], sb["rstdh"][:, :], AF.Sqrt), deps=[vv], inc=s_act)
                rr = P.op("vector", lambda e: e.reciprocal(sb["rstdh"][:, :], sb["rstdh"][:, :]),
                          deps=[sq3], inc=s_dve)
                P.op("vector", lambda e: e.tensor_scalar(
                    sb["scr1"][:, :], sb["scr0"][:, :], sb["muh"][:, :], sb["rstdh"][:, :],
                    ALU.subtract, ALU.mult), deps=[rr], inc=s_dve)
                P.op("vector", lambda e: e.tensor_tensor(
                    sb["scr1"][:, :], sb["scr1"][:, :], sb["gb_bc"][:, :], ALU.mult),
                    deps=[c_gb], inc=s_dve)
                P.op("vector", lambda e: e.tensor_tensor(
                    sb["scr1"][:, :], sb["scr1"][:, :], sb["bb_bc"][:, :], ALU.add), inc=s_dve)
                wr = P.op("vector", lambda e, c=c: e.tensor_tensor(
                    xch(c), xch(c), sb["scr1"][:, :], ALU.add), inc=s_dve)
                st["x_dep"] = st["x_dep"] + [wr]
        x_dep = st["x_dep"]
        tap(f"l{li}_xattn", [128, NCH * 1024], F32, lambda: xsb.ap(), [dve_now()])
        if upto == "attn":
            continue

        if li == 0:
            wg_d, wu_d, wd_d = dins[p + "wg"], dins[p + "wu"], dins[p + "wd"]
            eorder = MOE_EORDER if MOE_EORDER is not None else [7] + list(range(7))
            actf = {e_: AF.Silu for e_ in range(8)}
            b1l = None
        else:
            wg_d, wd_d = dins[p + "w1"], dins[p + "w2"]
            wu_d = None
            eorder = [0, 2, 4, 6, 1, 3, 5, 7]
            actf = {e_: (AF.Gelu if e_ % 2 == 0 else AF.Silu) for e_ in range(8)}
            b1l = P.op("sync", lambda e, p=p: e.dma_start(out=sb["b1fm"][:, :], in_=dins[p + "b1_fm"].ap()),
                       deps=[dve_now(), act_now(), pe_now()], inc=s_ldc, inck=16)

        moe_dep = [pe_now(), dve_now(), act_now()]

        if li == 1:
            b2d = P.op("gpsimd", lambda e, p=p: e.dma_start(out=sb["b2bf"][:, :], in_=dins[p + "b2"].ap()),
                       deps=moe_dep, inc=s_rw, inck=16)
            ctp_prev = None
            for c in range(NCH):
                ctp = P.op("tensor", lambda e, c=c: e.transpose(
                    ps[0:8, 1024:1152], sb["comb"][:, c * 8:(c + 1) * 8], sb["ident_f"][:, :]),
                    deps=[comb_dep, c_idf, ctp_prev] + moe_dep, inc=s_pe)
                cev = P.op("vector", lambda e: e.tensor_copy(sb["combfm"][:, :], ps[0:8, 1024:1152]),
                           deps=[ctp] + moe_dep, inc=s_dve)
                mmb = P.op("tensor", lambda e: (
                    e.matmul(ps[:, 2048:2560], sb["combfm"][:, :], sb["b2bf"][:, 0:512],
                             start=True, stop=True),
                    e.matmul(ps[:, 2560:3072], sb["combfm"][:, :], sb["b2bf"][:, 512:1024],
                             start=True, stop=True))[-1], deps=[cev, b2d], inc=s_pe)
                wr = P.op("vector", lambda e, c=c: (
                    e.tensor_tensor(xch(c)[:, 0:512], xch(c)[:, 0:512], ps[:, 2048:2560], ALU.add),
                    e.tensor_tensor(xch(c)[:, 512:1024], xch(c)[:, 512:1024], ps[:, 2560:3072],
                                    ALU.add))[-1], deps=[mmb] + x_dep, inc=s_dve)
                ctp_prev = wr
                st["x_dep"] = st["x_dep"] + [wr]
            x_dep = st["x_dep"]
            moe_dep = [pe_now(), dve_now(), act_now()]

        gu_guard = [None, None]
        wd_guard = [None, None]
        evac_last = None
        prev_evs = None
        ld_tok = {}
        seq = [(ic_, e__) for ic_ in (0, 1) for e__ in eorder]

        def gu_load(ic_, e_, hp, extra=()):
            par = hp % 2
            t1 = P.op("gpsimd", lambda e, hp=hp, par=par, e_=e_, wg_d=wg_d: e.dma_start(
                out=sb["wg_t"].ap().rearrange("p (b dc c) -> p b dc c", b=2, c=256)[:, par],
                in_=wg_d.ap()[e_].rearrange("(dc p) hh -> p dc hh", p=128)
                [:, :, hp * 256:(hp + 1) * 256]),
                deps=list(extra) + [gu_guard[par]], inc=s_wg2[par], inck=16)
            toks = [t1]
            if wu_d is not None:
                t2 = P.op("gpsimd", lambda e, hp=hp, par=par, e_=e_, wu_d=wu_d: e.dma_start(
                    out=sb["wu_t"].ap().rearrange("p (b dc c) -> p b dc c", b=2, c=256)[:, par],
                    in_=wu_d.ap()[e_].rearrange("(dc p) hh -> p dc hh", p=128)
                    [:, :, hp * 256:(hp + 1) * 256]),
                    deps=list(extra) + [gu_guard[par]], inc=s_wu2[par], inck=16)
                toks.append(t2)
            ld_tok[(ic_, e_, hp)] = toks
            return toks

        for si, (ic, e_) in enumerate(seq):
            if True:
                if (ic, e_, 0) not in ld_tok:
                    gu_load(ic, e_, 0, extra=moe_dep)
                h_tok = {}
                sil_tok = {}
                for hp in range(16):
                    if hp + 1 < 16 and (ic, e_, hp + 1) not in ld_tok:
                        gu_load(ic, e_, hp + 1)
                    par = hp % 2
                    wt = ld_tok[(ic, e_, hp)]
                    for hcl in (0, 1):
                        hc = hp * 2 + hcl
                        pg = hc % 2
                        for nh in (0, 1):
                            greg = ps[:, (2 * pg + nh) * 512:(2 * pg + nh + 1) * 512]
                            ureg = ps[:, 2048 + (2 * pg + nh) * 512: 2048 + (2 * pg + nh + 1) * 512]
                            def mmg(e, par=par, hcl=hcl, nh=nh, greg=greg, ic=ic):
                                for dc in range(NDC):
                                    ins = e.matmul(
                                        greg,
                                        sb["wg_t"][:, par * 2048 + dc * 256 + hcl * 128:
                                                   par * 2048 + dc * 256 + hcl * 128 + 128],
                                        xnc(dc, ic * 1024 + nh * 512, ic * 1024 + (nh + 1) * 512),
                                        start=(dc == 0), stop=(dc == NDC - 1))
                                return ins
                            bnk = 2 * pg + nh
                            dd = [wt[0]] + xn_dep + [h_tok.get((hc - 2, nh)),
                                                     prev_evs[bnk] if prev_evs else None]
                            if si == 0:
                                dd += moe_dep
                            mm = P.op("tensor", mmg, deps=dd, inc=s_pe)
                            gu_guard[par] = mm
                            if li == 0:
                                sil = P.op("scalar", lambda e, pg=pg, nh=nh, greg=greg:
                                           e.activation(sb["gsil"][:, pg * 1024 + nh * 512:
                                                                   pg * 1024 + (nh + 1) * 512],
                                                        greg, AF.Silu),
                                           deps=[mm], inc=s_act)
                                sil_tok[(hc, nh)] = sil
                                def mmu(e, par=par, hcl=hcl, nh=nh, ureg=ureg, ic=ic):
                                    for dc in range(NDC):
                                        ins = e.matmul(
                                            ureg,
                                            sb["wu_t"][:, par * 2048 + dc * 256 + hcl * 128:
                                                       par * 2048 + dc * 256 + hcl * 128 + 128],
                                            xnc(dc, ic * 1024 + nh * 512, ic * 1024 + (nh + 1) * 512),
                                            start=(dc == 0), stop=(dc == NDC - 1))
                                    return ins
                                mmu_t = P.op("tensor", mmu,
                                             deps=[wt[1], h_tok.get((hc - 2, nh)),
                                                   prev_evs[4 + bnk] if prev_evs else None] +
                                                  (moe_dep if si == 0 else []), inc=s_pe)
                                gu_guard[par] = mmu_t
                                ht = P.op("vector", lambda e, hc=hc, nh=nh, pg=pg, ureg=ureg:
                                          e.tensor_tensor(
                                    sb["hbuf"][:, hc * 1024 + nh * 512: hc * 1024 + (nh + 1) * 512],
                                    sb["gsil"][:, pg * 1024 + nh * 512: pg * 1024 + (nh + 1) * 512],
                                    ureg, ALU.mult), deps=[sil, mmu_t], inc=s_dve)
                            else:
                                ht = P.op("scalar", lambda e, hc=hc, nh=nh, e_=e_, greg=greg,
                                          actf=actf: e.activation(
                                    sb["hbuf"][:, hc * 1024 + nh * 512: hc * 1024 + (nh + 1) * 512],
                                    greg, actf[e_],
                                    bias=sb["b1fm"][:, e_ * 32 + hc: e_ * 32 + hc + 1]),
                                    deps=[mm, b1l], inc=s_act)
                            h_tok[(hc, nh)] = ht
                h_last = dve_now() if li == 0 else act_now()

                for dh in (0, 1):
                    wdld = {}

                    def wd_load(hq, dh=dh, e_=e_, wd_d=wd_d):
                        par = hq % 2
                        t = P.op("gpsimd", lambda e, hq=hq, dh=dh, par=par, e_=e_, wd_d=wd_d:
                                 e.dma_start(
                            out=sb["wd_t"].ap().rearrange("p (b q d) -> p b q d", b=2, d=512)[:, par],
                            in_=wd_d.ap()[e_][hq * 512:(hq + 1) * 512, dh * 512:(dh + 1) * 512]
                            .rearrange("(q p) d -> p q d", p=128)),
                            deps=[wd_guard[par]], inc=s_wd2[par], inck=16)
                        wdld[hq] = t
                        return t

                    wd_load(0)
                    wd_load(1)
                    if dh == 0 and si + 1 < len(seq):
                        nic, ne = seq[si + 1]
                        gu_load(nic, ne, 0)
                        gu_load(nic, ne, 1)
                    mm_last = None
                    for hq in range(8):
                        if hq + 1 < 8 and (hq + 1) not in wdld:
                            wd_load(hq + 1)
                        par = hq % 2
                        for ql in range(4):
                            hc = hq * 4 + ql
                            def mmd(e, hc=hc, ql=ql, par=par):
                                for isub in range(8):
                                    ins = e.matmul(
                                        ps[:, isub * 512:(isub + 1) * 512],
                                        sb["hbuf"][:, hc * 1024 + isub * 128: hc * 1024 + (isub + 1) * 128],
                                        sb["wd_t"][:, par * 2048 + ql * 512:(par * 2048) + (ql + 1) * 512],
                                        start=(hc == 0), stop=(hc == 31))
                                return ins
                            dd = [h_last, wdld[hq]]
                            if hc == 0:
                                dd.append(evac_last)
                            mm_last = P.op("tensor", mmd, deps=dd, inc=s_pe)
                            wd_guard[par] = mm_last
                    evs = []
                    for isub in range(8):
                        c = ic * 8 + isub
                        c = c
                        if li == 0 and e_ == 7:
                            fn = lambda e, c=c, isub=isub, dh=dh: e.scalar_tensor_tensor(
                                xch(c)[:, dh * 512:(dh + 1) * 512],
                                ps[:, isub * 512:(isub + 1) * 512], 1.0,
                                xch(c)[:, dh * 512:(dh + 1) * 512], ALU.mult, ALU.add)
                        else:
                            fn = lambda e, c=c, isub=isub, dh=dh, e_=e_: e.scalar_tensor_tensor(
                                xch(c)[:, dh * 512:(dh + 1) * 512],
                                ps[:, isub * 512:(isub + 1) * 512],
                                sb["comb"][:, c * 8 + e_: c * 8 + e_ + 1],
                                xch(c)[:, dh * 512:(dh + 1) * 512], ALU.mult, ALU.add)
                        ev = P.op("vector", fn, deps=[mm_last, comb_dep] + x_dep, inc=s_dve)
                        evs.append(ev)
                    evac_last = evs[-1]
                    st["x_dep"] = st["x_dep"] + [evac_last]
                    prev_evs = evs
                moe_dep = [evac_last]
                x_dep = st["x_dep"]

        st["x_dep"] = [dve_now()]
        st["exp_last2"] = [None, None]

    P.op("sync", lambda e: e.dma_start(
        out=x_out.ap().rearrange("(c p) d -> p c d", p=128),
        in_=xsb.ap().rearrange("p (c d) -> p c d", d=1024)),
        deps=st["x_dep"] + [pe_now(), act_now()], inc=s_st, inck=16)
    P.op("sync", lambda e: e.nop(), deps=[(s_st, s_st.v)])

    P.emit()
    return nc



def _swap_weight(w, rot):
    out = np.zeros_like(w)
    for dd in range(rot):
        cols = np.arange(H) * DH + dd
        src = np.arange(H) * DH + (dd ^ 1)
        out[:, cols] = w[:, src]
    return out


def _swap_bias(b, rot):
    out = np.zeros_like(b)
    for dd in range(rot):
        idx = np.arange(H) * DH + dd
        src = np.arange(H) * DH + (dd ^ 1)
        out[idx] = b[src]
    return out


def _fm(b):
    return np.ascontiguousarray(b.reshape(8, 128).T.astype(np.float32))


def _rope_tables(rot, t0):
    m = np.arange(rot // 2)
    inv = 10000.0 ** (-(2.0 * m) / rot)
    t = np.arange(t0, t0 + T, dtype=np.float64)
    ang = t[:, None] * inv[None, :]
    cos = np.cos(ang)
    sin = np.sin(ang)
    cosA = np.zeros((128, T), np.float32)
    sinB = np.zeros((128, T), np.float32)
    for d in range(rot):
        cc = cos[:, d // 2]
        ss = sin[:, d // 2] * (-1.0 if d % 2 == 0 else 1.0)
        cosA[d] = cc
        cosA[64 + d] = cc
        sinB[d] = ss
        sinB[64 + d] = ss
    return (np.ascontiguousarray(cosA.astype(ml_dtypes.bfloat16)),
            np.ascontiguousarray(sinB.astype(ml_dtypes.bfloat16)))


def _layer_inputs(li, inputs, half):
    f32 = np.float32
    o = {}
    if li == 0:
        wq = np.asarray(inputs["L0_attn_Wq"], f32)
        wkv = np.asarray(inputs["L0_attn_Wkv"], f32)
        o["l0_wq"] = wq
        o["l0_wqs"] = _swap_weight(wq, 32)
        o["l0_wkv"] = wkv
        o["l0_wks"] = _swap_weight(np.ascontiguousarray(wkv[:, 0:D]), 32)
        o["l0_wout"] = np.asarray(inputs["L0_attn_Wout"], f32)
        o["l0_bout"] = np.asarray(inputs["L0_attn_bout"], f32).reshape(1, D)
        rw = np.asarray(inputs["L0_ffn_router_w"], f32)
        o["l0_rwT"] = np.ascontiguousarray(rw.T.reshape(8, 128, 7).transpose(1, 0, 2).reshape(128, 56))
        o["l0_wg"] = np.ascontiguousarray(np.concatenate(
            [np.asarray(inputs["L0_ffn_w_gate"], f32), np.asarray(inputs["L0_ffn_sw_gate"], f32)], 0))
        o["l0_wu"] = np.ascontiguousarray(np.concatenate(
            [np.asarray(inputs["L0_ffn_w_up"], f32), np.asarray(inputs["L0_ffn_sw_up"], f32)], 0))
        o["l0_wd"] = np.ascontiguousarray(np.concatenate(
            [np.asarray(inputs["L0_ffn_w_down"], f32), np.asarray(inputs["L0_ffn_sw_down"], f32)], 0))
        o["l0_cos"], o["l0_sin"] = _rope_tables(32, half * T)
    else:
        wqkv = np.asarray(inputs["L1_attn_Wqkv"], f32)
        bqkv = np.asarray(inputs["L1_attn_bqkv"], f32)
        o["l1_wqkv"] = wqkv
        o["l1_wqs"] = _swap_weight(np.ascontiguousarray(wqkv[:, 0:D]), 64)
        o["l1_wks"] = _swap_weight(np.ascontiguousarray(wqkv[:, D:2 * D]), 64)
        bq, bk, bv = bqkv[0:D], bqkv[D:2 * D], bqkv[2 * D:3 * D]
        o["l1_bias_fm"] = np.ascontiguousarray(np.concatenate(
            [_fm(bq), _fm(_swap_bias(bq, 64)), _fm(bk), _fm(_swap_bias(bk, 64))], 1))
        o["l1_bv"] = np.ascontiguousarray(bv.reshape(1, D).astype(f32))
        o["l1_wo"] = np.asarray(inputs["L1_attn_Wo"], f32)
        o["l1_bo"] = np.asarray(inputs["L1_attn_bo"], f32).reshape(1, D)
        o["l1_gamma"] = np.asarray(inputs["L1_attn_gamma"], f32).reshape(1, D)
        o["l1_beta"] = np.asarray(inputs["L1_attn_beta"], f32).reshape(1, D)
        o["l1_w1"] = np.asarray(inputs["L1_ffn_W1"], f32)
        b1 = np.asarray(inputs["L1_ffn_b1"], f32)
        o["l1_b1_fm"] = np.ascontiguousarray(b1.reshape(8, 32, 128).transpose(2, 0, 1).reshape(128, 256))
        o["l1_w2"] = np.asarray(inputs["L1_ffn_W2"], f32)
        o["l1_b2"] = np.asarray(inputs["L1_ffn_b2"], f32)
        wg = np.asarray(inputs["L1_ffn_Wg"], f32)
        o["l1_wgT"] = np.ascontiguousarray(wg.reshape(8, 128, 8).transpose(1, 0, 2).reshape(128, 64))
        o["l1_bg"] = np.asarray(inputs["L1_ffn_bg"], f32).reshape(1, 8)
        o["l1_cos"], o["l1_sin"] = _rope_tables(64, half * T)
    return o


def _const_inputs():
    return {
        "ones_f": np.ones((1, 128), np.float32),
        "ident_f": np.eye(128, dtype=np.float32),
        "ident_b": np.eye(128, dtype=np.float32).astype(ml_dtypes.bfloat16),
    }


def kernel(**inputs):
    xfull = np.asarray(inputs["L0_attn_queries"], np.float32)
    consts = _const_inputs()
    nc = build_program([0, 1])
    maps = []
    for c in range(NCORES):
        b, half = c // 2, c % 2
        m = {"x": np.ascontiguousarray(xfull[b, half * T:(half + 1) * T])}
        m.update(consts)
        m.update(_layer_inputs(0, inputs, half))
        m.update(_layer_inputs(1, inputs, half))
        maps.append(m)
    res = run_bass_kernel_spmd(nc, maps, core_ids=list(range(NCORES)))
    out = np.empty((BATCH, SEQ, D), np.float32)
    for c in range(NCORES):
        b, half = c // 2, c % 2
        out[b, half * T:(half + 1) * T] = res.results[c]["xout"]
    return out


# version 04
# speedup vs baseline: 1.0244x; 1.0131x over previous
import numpy as np
import ml_dtypes
import concourse.bass as bass
import concourse.mybir as mybir
from concourse.bass_utils import run_bass_kernel_spmd

F32 = mybir.dt.float32
BF16 = mybir.dt.bfloat16
AF = mybir.ActivationFunctionType
ALU = mybir.AluOpType
AX = mybir.AxisListType
AP = bass.AP

NCORES = 8
BATCH, SEQ, D = 4, 4096, 1024
T = 2048
H, DH = 16, 64
HID = 4096
SCALE = 0.125
EPS = 1e-5
NCH = T // 128
NDC = D // 128
NGRP = 8
NEG = -30000.0
MOE_EORDER = None
XN_EVAC_DVE = True
KIN_E = 128 * T
VIN_E = T * 130
KVE = KIN_E + VIN_E
ENGINES = ("tensor", "vector", "scalar", "gpsimd", "sync")
RG_PAIRS = [[0, 1], [2, 3], [4, 5], [6, 7]]


class Sem:
    def __init__(self, nc, name):
        self.h = nc.alloc_semaphore(name=name)
        self.name = name
        self.v = 0


class Plan:
    def __init__(self, nc):
        self.nc = nc
        self.ops = {e: [] for e in ENGINES}
        self.waits = {e: {} for e in ENGINES}
        self.sems = {}
        self.last = {}

    def sem(self, name):
        if name not in self.sems:
            self.sems[name] = Sem(self.nc, name)
        return self.sems[name]

    def op(self, eng, fn, deps=(), inc=None, inck=1):
        if eng in ("vector", "scalar") and eng in self.last:
            deps = list(deps) + [self.last[eng]]
        need = []
        for d in deps:
            if d is None:
                continue
            s, v = d
            if self.waits[eng].get(s.name, 0) < v:
                self.waits[eng][s.name] = v
                need.append((s.h, v))
        tok = None
        if inc is not None:
            inc.v += inck
            tok = (inc, inc.v)

        def run(e, fn=fn, need=tuple(need), inc=inc, inck=inck):
            for h, v in need:
                e.wait_ge(h, v)
            ins = fn(e)
            if inc is not None:
                ins.then_inc(inc.h, inck)

        self.ops[eng].append(run)
        if tok is not None and eng in ("vector", "scalar"):
            self.last[eng] = tok
        return tok

    def emit(self):
        with self.nc.Block() as block:
            for eng in ENGINES:
                fns = self.ops[eng]

                def body(e, fns=fns):
                    for f in fns:
                        f(e)

                getattr(block, eng)(body)



def build_program(layers, dbg=(), upto="full"):
    nc = bass.Bass("TRN2", target_bir_lowering=False)
    P = Plan(nc)
    dbg = set(dbg)

    x_in = nc.dram_tensor("x", [T, D], F32, kind="ExternalInput")
    x_out = nc.dram_tensor("xout", [T, D], F32, kind="ExternalOutput")

    dins = {}

    def din(name, shape, dtype=F32):
        dins[name] = nc.dram_tensor(name, shape, dtype, kind="ExternalInput")

    din("ones_f", [1, 128])
    din("ident_f", [128, 128])
    din("ident_b", [128, 128], BF16)
    for li in layers:
        p = f"l{li}_"
        if li == 0:
            din(p + "wq", [D, D]); din(p + "wqs", [D, D])
            din(p + "wkv", [D, 2 * D]); din(p + "wks", [D, D])
            din(p + "wout", [D, D]); din(p + "bout", [1, D])
            din(p + "rwT", [128, 56])
            din(p + "wg", [8, D, HID]); din(p + "wu", [8, D, HID])
            din(p + "wd", [8, HID, D])
        else:
            din(p + "wqkv", [D, 3 * D]); din(p + "wqs", [D, D]); din(p + "wks", [D, D])
            din(p + "bias_fm", [128, 32])
            din(p + "bv", [1, D])
            din(p + "wo", [D, D]); din(p + "bo", [1, D])
            din(p + "gamma", [1, D]); din(p + "beta", [1, D])
            din(p + "w1", [8, D, HID]); din(p + "b1_fm", [128, 256])
            din(p + "w2", [8, HID, D]); din(p + "b2", [8, D])
            din(p + "wgT", [128, 64]); din(p + "bg", [1, 8])
        din(p + "cos", [128, T], BF16)
        din(p + "sin", [128, T], BF16)

    kv_in = nc.dram_tensor("kv_in", [KVE, 1], BF16, kind="Internal")
    kv_out = nc.dram_tensor("kv_out", [2 * KVE, 1], BF16, kind="Internal")

    sb = {}

    ARENA0 = 16512

    def sba(name, shape, dtype, off):
        sb[name] = nc.alloc_sbuf_tensor_at(name, shape, dtype, offset=ARENA0 + off)

    B0 = 98304
    sba("x", [128, 16 * 1024], F32, 0)
    sba("xnfm", [128, 8 * 2048], BF16, 65536)
    sba("hbuf", [128, 32 * 1024], BF16, B0)
    sba("khat", [128, 4096], BF16, B0 + 0)
    sba("vhat", [128, 32 * 130], BF16, B0 + 8192)
    sba("wo_t", [128, 8 * 1024], BF16, B0 + 0)
    sba("vown", [128, 16 * 130], BF16, B0 + 16640)
    sba("pt0", [128, 1024], BF16, B0 + 20800)
    sba("pt1", [128, 1024], BF16, B0 + 22848)
    sba("cos_t", [128, T], BF16, B0 + 24896)
    sba("sin_t", [128, T], BF16, B0 + 28992)
    sba("row0", [1, 1024], F32, B0 + 33088)
    sba("gb_bc", [128, 1024], BF16, B0 + 37184)
    sba("bb_bc", [128, 1024], BF16, B0 + 39232)
    sba("wk_t", [128, 1024], BF16, B0 + 41280)
    sba("wks_t", [128, 1024], BF16, B0 + 43328)
    sba("wv_t", [128, 1024], BF16, B0 + 45376)
    sba("wq_t", [128, 1024], BF16, B0 + 47424)
    sba("wqs_t", [128, 1024], BF16, B0 + 49472)
    sba("fmscr", [128, 1024], BF16, B0 + 51520)
    sba("combfm", [8, 128], BF16, B0 + 53568)
    sba("rtmp", [128, 2048], BF16, B0 + 53824)
    sba("kown", [128, 2048], BF16, B0 + 57920)
    sba("kswp", [128, 2048], BF16, B0 + 37184)
    A0 = B0 + 65536
    sba("atm", [128, 16 * 1024], BF16, A0)
    sba("wg_t", [128, 4096], BF16, A0 + 0)
    sba("wu_t", [128, 4096], BF16, A0 + 8192)
    sba("wd_t", [128, 4096], BF16, A0 + 16384)
    sba("gsil", [128, 2048], BF16, A0 + 24576)
    Q0 = A0 + 32768
    sba("scr0", [128, 1024], F32, Q0)
    sba("scr1", [128, 1024], F32, Q0 + 4096)
    sba("qbufA", [128, 2048], BF16, Q0 + 0)
    sba("qbufB", [128, 2048], BF16, Q0 + 4096)
    sba("qswp", [128, 2048], BF16, B0 + 57920)
    P0 = Q0 + 8192
    sba("ones_f", [1, 128], F32, P0 + 0)
    sba("ident_f", [128, 128], F32, P0 + 512)
    sba("ident_b", [128, 128], BF16, P0 + 1024)
    sba("ones_c", [128, 1], BF16, P0 + 1280)
    sba("mu", [128, 16], F32, P0 + 1312)
    sba("rstd", [128, 16], F32, P0 + 1376)
    sba("sums", [128, 16], F32, P0 + 1440)
    sba("sumsq", [128, 16], F32, P0 + 1504)
    sba("muh", [128, 1], F32, P0 + 1568)
    sba("rstdh", [128, 1], F32, P0 + 1600)
    sba("ssq2", [128, 1], F32, P0 + 1632)
    sba("comb", [128, 128], F32, P0 + 1664)
    sba("negB", [128, 4], F32, P0 + 2176)
    sba("nrm", [1, 64], F32, P0 + 2208)
    sba("lg", [128, 8], F32, P0 + 2464)
    sba("m1", [128, 1], F32, P0 + 2496)
    sba("m2", [128, 1], F32, P0 + 2528)
    sba("eq", [128, 8], F32, P0 + 2560)
    sba("rden", [128, 1], F32, P0 + 2592)
    sba("rwT", [128, 64], BF16, P0 + 2624)
    sba("bgrow", [1, 8], F32, P0 + 2752)
    sba("biasfm", [128, 32], F32, P0 + 2784)
    sba("rcol", [128, 1], F32, P0 + 2912)
    sba("b1fm", [128, 256], F32, P0 + 2944)
    sba("b2bf", [8, 1024], BF16, P0 + 3968)
    assert ARENA0 + P0 + 6016 <= 229344, P0 + 6016
    nc.sbuf_base = ARENA0 + P0 + 6016

    ps = nc.alloc_psum_tensor("ps", [128, 4096], F32)
    psb = ps.bitcast(BF16)

    s_ldx = P.sem("ldx"); s_ldc = P.sem("ldc"); s_rw = P.sem("rw")
    s_bnc = P.sem("bnc"); s_cc = P.sem("cc"); s_rel = P.sem("rel")
    s_pe = P.sem("pe"); s_act = P.sem("act"); s_dve = P.sem("dve"); s_st = P.sem("st")
    s_wt = {n: P.sem("w" + n) for n in ("k", "ks", "v", "q", "qs", "o")}
    s_wg2 = [P.sem("wg0"), P.sem("wg1")]
    s_wu2 = [P.sem("wu0"), P.sem("wu1")]
    s_wd2 = [P.sem("wd0"), P.sem("wd1")]

    xsb, xn, ones_f = sb["x"], sb["xnfm"], sb["ones_f"]

    def xch(c):
        return xsb[:, c * 1024:(c + 1) * 1024]

    def xnc(dc, a, b_):
        return xn[:, dc * 2048 + a: dc * 2048 + b_]

    def pe_now():
        return (s_pe, s_pe.v)

    def dve_now():
        return (s_dve, s_dve.v)

    def act_now():
        return (s_act, s_act.v)

    dbg_t = {}

    def tap(name, shape, dtype, ap_fn, deps):
        if name not in dbg:
            return
        t = nc.dram_tensor("dbg_" + name, shape, dtype, kind="ExternalOutput")
        dbg_t[name] = t
        P.op("sync", lambda e: e.dma_start(out=t.ap(), in_=ap_fn()), deps=deps,
             inc=s_st, inck=16)

    x_ld = P.op("sync", lambda e: e.dma_start(
        out=xsb.ap().rearrange("p (c d) -> p c d", d=1024),
        in_=x_in.ap().rearrange("(c p) d -> p c d", p=128)),
        inc=s_ldx, inck=16)
    P.op("sync", lambda e: e.dma_start(out=ones_f[:, :], in_=dins["ones_f"].ap()),
         inc=s_ldc, inck=16)
    P.op("sync", lambda e: e.dma_start(out=sb["ident_f"][:, :], in_=dins["ident_f"].ap()),
         inc=s_ldc, inck=16)
    P.op("sync", lambda e: e.dma_start(out=sb["ident_b"][:, :], in_=dins["ident_b"].ap()),
         inc=s_ldc, inck=16)
    c0 = (s_ldc, s_ldc.v)
    c_ones = c_idf = c_idb = c0
    c_onesc = P.op("vector", lambda e: e.memset(sb["ones_c"][:, :], 1.0), inc=s_dve)
    P.op("vector", lambda e: e.memset(sb["comb"][:, :], 0.0), inc=s_dve)

    st = {"x_dep": [x_ld], "cc_free": None, "rel_done": [], "kb": None, "vb": None,
          "exp_last2": [None, None]}

    for li in layers:
        if upto == "x":
            break
        p = f"l{li}_"
        ROT = 32 if li == 0 else 64
        NE = 7 if li == 0 else 8
        layer_start = [pe_now(), dve_now(), act_now()]

        P.op("sync", lambda e, p=p: e.dma_start(out=sb["cos_t"][:, :], in_=dins[p + "cos"].ap()),
             deps=layer_start + [(s_ldc, s_ldc.v)], inc=s_ldc, inck=16)
        P.op("sync", lambda e, p=p: e.dma_start(out=sb["sin_t"][:, :], in_=dins[p + "sin"].ap()),
             inc=s_ldc, inck=16)
        if li == 0:
            c_rw = P.op("gpsimd", lambda e, p=p: e.dma_start(out=sb["rwT"][:, 0:56], in_=dins[p + "rwT"].ap()),
                        deps=layer_start, inc=s_rw, inck=16)
            cbatch = (s_ldc, s_ldc.v)
            c_cos = c_sin = cbatch
            c_bg = c_bias = c_gb = None
        else:
            c_rw = P.op("gpsimd", lambda e, p=p: e.dma_start(out=sb["rwT"][:, 0:64], in_=dins[p + "wgT"].ap()),
                        deps=layer_start, inc=s_rw, inck=16)
            P.op("sync", lambda e, p=p: e.dma_start(out=sb["bgrow"][:, :], in_=dins[p + "bg"].ap()),
                 inc=s_ldc, inck=16)
            P.op("sync", lambda e, p=p: e.dma_start(out=sb["biasfm"][:, :], in_=dins[p + "bias_fm"].ap()),
                 inc=s_ldc, inck=16)
            cbatch = (s_ldc, s_ldc.v)
            c_cos = c_sin = c_bg = c_bias = cbatch
            c_gb = None

        x_dep = st["x_dep"]

        for c in range(NCH):
            P.op("vector", lambda e, c=c: e.reduce_sum(sb["sums"][:, c:c + 1], xch(c), AX.X),
                 deps=x_dep + layer_start, inc=s_dve)
            P.op("vector", lambda e, c=c: e.scalar_tensor_tensor(
                sb["scr0"][:, :], xch(c), 1.0, xch(c), ALU.mult, ALU.mult,
                accum_out=sb["sumsq"][:, c:c + 1]), inc=s_dve)
        stat_in = dve_now()
        P.op("vector", lambda e: e.tensor_scalar_mul(sb["mu"][:, :], sb["sums"][:, :], 1.0 / D),
             deps=[stat_in], inc=s_dve)
        P.op("vector", lambda e: e.tensor_mul(sb["rstd"][:, :], sb["mu"][:, :], sb["mu"][:, :]),
             inc=s_dve)
        P.op("vector", lambda e: e.scalar_tensor_tensor(
            sb["rstd"][:, :], sb["sumsq"][:, :], 1.0 / D, sb["rstd"][:, :], ALU.mult, ALU.subtract),
            inc=s_dve)
        P.op("vector", lambda e: e.tensor_scalar_add(sb["rstd"][:, :], sb["rstd"][:, :], EPS),
             inc=s_dve)
        var_dep = dve_now()
        sq = P.op("scalar", lambda e: e.activation(sb["rstd"][:, :], sb["rstd"][:, :], AF.Sqrt),
                  deps=[var_dep], inc=s_act)
        stats_dep = P.op("vector", lambda e: e.reciprocal(sb["rstd"][:, :], sb["rstd"][:, :]),
                         deps=[sq], inc=s_dve)

        if upto == "stats":
            break
        ev_prev = [None, None]
        tp_last = {}
        _nch = 2 if upto in ("lnt1", "xntm1", "tponly") else NCH
        for c in range(_nch):
            scr = sb["scr0"] if c % 2 == 0 else sb["scr1"]
            deps = [stats_dep] + x_dep
            if c >= 2:
                deps.append(tp_last[c - 2])
            xn_tm = P.op("vector", lambda e, c=c, scr=scr: e.tensor_scalar(
                scr[:, :], xch(c), sb["mu"][:, c:c + 1], sb["rstd"][:, c:c + 1],
                ALU.subtract, ALU.mult), deps=deps, inc=s_dve)
            if upto in ("xntm", "xntm1"):
                tp_last[c] = xn_tm
                if upto == "xntm1" or c >= _nch - 1:
                    pass
                continue
            last = None
            for dc in range(NDC):
                reg = ps[:, 1024 + (dc % 2) * 512: 1024 + (dc % 2) * 512 + 128]
                mm = P.op("tensor", lambda e, scr=scr, dc=dc, reg=reg: e.transpose(
                    reg, scr[:, dc * 128:(dc + 1) * 128], sb["ident_f"][:, :]),
                    deps=[xn_tm, c_idf, ev_prev[dc % 2]], inc=s_pe)
                if upto == "tponly":
                    ev_prev[dc % 2] = mm
                    last = mm
                    continue
                if XN_EVAC_DVE:
                    ev = P.op("vector", lambda e, c=c, dc=dc, reg=reg: e.tensor_copy(
                        xnc(dc, c * 128, (c + 1) * 128), reg), deps=[mm], inc=s_dve)
                else:
                    ev = P.op("scalar", lambda e, c=c, dc=dc, reg=reg: e.activation(
                        xnc(dc, c * 128, (c + 1) * 128), reg, AF.Copy),
                        deps=[mm], inc=s_act)
                ev_prev[dc % 2] = ev
                last = ev
            tp_last[c] = last
        if upto in ("lnt1", "xntm", "xntm1", "tponly"):
            break
        xn_dep = [tp_last[c] for c in range(NCH)]
        tap(f"l{li}_xn", [128, NDC * 2048], BF16, lambda: xn.ap(), xn_dep)
        if upto == "lnt":
            break

        lg_cp = None
        for c in range(NCH):
            def mm_lg(e, c=c, NE=NE, li=li):
                for dc in range(NDC):
                    ins = e.matmul(ps[:, 2048:2048 + NE],
                                   xnc(dc, c * 128, (c + 1) * 128),
                                   sb["rwT"][:, dc * NE:(dc + 1) * NE],
                                   start=(dc == 0), stop=(dc == NDC - 1 and li == 0))
                if li == 1:
                    ins = e.matmul(ps[:, 2048:2048 + NE], ones_f[:, 0:128],
                                   sb["bgrow"][:, 0:NE], start=False, stop=True)
                return ins
            mm = P.op("tensor", mm_lg,
                      deps=[tp_last[c], c_rw, c_ones, lg_cp] + ([c_bg] if li == 1 else []),
                      inc=s_pe)
            lg_cp = P.op("vector", lambda e, NE=NE: e.tensor_copy(sb["lg"][:, 0:NE], ps[:, 2048:2048 + NE]),
                         deps=[mm], inc=s_dve)
            P.op("vector", lambda e, NE=NE: e.reduce_max(sb["m1"][:, :], sb["lg"][:, 0:NE],
                                                         AX.X, negate=True), inc=s_dve)
            m1d = dve_now()
            P.op("vector", lambda e, NE=NE: e.tensor_scalar(
                sb["eq"][:, 0:NE], sb["lg"][:, 0:NE], sb["m1"][:, :], None, ALU.add, ALU.bypass),
                deps=[m1d], inc=s_dve)
            P.op("vector", lambda e, NE=NE: e.tensor_scalar(
                sb["eq"][:, 0:NE], sb["eq"][:, 0:NE], 0.0, NEG, ALU.is_equal, ALU.mult),
                inc=s_dve)
            P.op("vector", lambda e, NE=NE: e.tensor_tensor(
                sb["eq"][:, 0:NE], sb["eq"][:, 0:NE], sb["lg"][:, 0:NE], ALU.add), inc=s_dve)
            P.op("vector", lambda e, NE=NE: e.reduce_max(sb["m2"][:, :], sb["eq"][:, 0:NE], AX.X),
                 inc=s_dve)
            m2d = dve_now()
            en = P.op("scalar", lambda e, NE=NE: e.activation(
                sb["eq"][:, 0:NE], sb["lg"][:, 0:NE], AF.Exp, bias=sb["m1"][:, :]),
                deps=[m2d], inc=s_act)
            e2 = P.op("scalar", lambda e: e.activation(
                sb["rden"][:, :], sb["m2"][:, :], AF.Exp, bias=sb["m1"][:, :]), inc=s_act)
            P.op("vector", lambda e: e.tensor_scalar_add(sb["rden"][:, :], sb["rden"][:, :], 1.0),
                 deps=[e2], inc=s_dve)
            P.op("vector", lambda e: e.reciprocal(sb["rden"][:, :], sb["rden"][:, :]), inc=s_dve)
            rdend = dve_now()
            P.op("vector", lambda e, NE=NE: e.tensor_scalar(
                sb["lg"][:, 0:NE], sb["lg"][:, 0:NE], sb["m2"][:, :], None, ALU.is_ge, ALU.bypass),
                deps=[rdend, en], inc=s_dve)
            P.op("vector", lambda e, NE=NE: e.tensor_tensor(
                sb["lg"][:, 0:NE], sb["lg"][:, 0:NE], sb["eq"][:, 0:NE], ALU.mult), inc=s_dve)
            P.op("vector", lambda e, c=c, NE=NE: e.tensor_scalar(
                sb["comb"][:, c * 8:c * 8 + NE], sb["lg"][:, 0:NE], sb["rden"][:, :], None,
                ALU.mult, ALU.bypass), inc=s_dve)
        comb_dep = dve_now()
        tap(f"l{li}_comb", [128, 128], F32, lambda: sb["comb"].ap(), [comb_dep])

        if upto in ("ln", "route"):
            continue

        if li == 1:
            b2d = P.op("gpsimd", lambda e, p=p: e.dma_start(out=sb["b2bf"][:, :], in_=dins[p + "b2"].ap()),
                       deps=layer_start, inc=s_rw, inck=16)
            ctp_prev = None
            for c in range(NCH):
                ctp = P.op("tensor", lambda e, c=c: e.transpose(
                    ps[0:8, 1024:1152], sb["comb"][:, c * 8:(c + 1) * 8], sb["ident_f"][:, :]),
                    deps=[comb_dep, c_idf, ctp_prev], inc=s_pe)
                cev = P.op("vector", lambda e: e.tensor_copy(sb["combfm"][:, :], ps[0:8, 1024:1152]),
                           deps=[ctp], inc=s_dve)
                mmb = P.op("tensor", lambda e: (
                    e.matmul(ps[:, 0:512], sb["combfm"][:, :], sb["b2bf"][:, 0:512],
                             start=True, stop=True),
                    e.matmul(ps[:, 512:1024], sb["combfm"][:, :], sb["b2bf"][:, 512:1024],
                             start=True, stop=True))[-1], deps=[cev, b2d], inc=s_pe)
                wr = P.op("vector", lambda e, c=c: (
                    e.tensor_tensor(xch(c)[:, 0:512], xch(c)[:, 0:512], ps[:, 0:512], ALU.add),
                    e.tensor_tensor(xch(c)[:, 512:1024], xch(c)[:, 512:1024], ps[:, 512:1024],
                                    ALU.add))[-1], deps=[mmb] + x_dep, inc=s_dve)
                ctp_prev = wr
                st["x_dep"] = st["x_dep"] + [wr]
            x_dep = st["x_dep"]

        if li == 0:
            wq_d, wqs_d = dins[p + "wq"], dins[p + "wqs"]
            wk_d, wks_d, wv_d = dins[p + "wkv"], dins[p + "wks"], dins[p + "wkv"]
            KOFF, VOFF, QOFF = 0, D, 0
        else:
            wq_d, wqs_d = dins[p + "wqkv"], dins[p + "wqs"]
            wk_d, wks_d, wv_d = dins[p + "wqkv"], dins[p + "wks"], dins[p + "wqkv"]
            KOFF, VOFF, QOFF = D, 2 * D, 0
            bvr = P.op("sync", lambda e, p=p: e.dma_start(out=sb["row0"][:, :], in_=dins[p + "bv"].ap()),
                       deps=[cbatch], inc=s_ldc, inck=16)

        vown3 = sb["vown"].ap().rearrange("p (jt c) -> p jt c", c=130)
        vhat4 = sb["vhat"].ap().rearrange("p (r jt c) -> p r jt c", r=2, c=130)
        vhat3 = sb["vhat"].ap().rearrange("p (jt c) -> p jt c", c=130)
        atm = sb["atm"]

        def wload(dst, src_d, col0, sem, guard):
            return P.op("gpsimd", lambda e: e.dma_start(
                out=sb[dst].ap().rearrange("p (dc c) -> p dc c", c=128),
                in_=src_d.ap().rearrange("(dc p) c -> p dc c", p=128)[:, :, col0:col0 + 128]),
                deps=[guard], inc=sem, inck=16)

        a_done = None
        prev_flash_act = None
        wguard = {n: layer_start[0] for n in ("k", "ks", "v", "q", "qs")}
        atm = atm

        def prep_gen(g):
            hd0 = g * 128
            qb = "qbufA" if g % 2 == 0 else "qbufB"
            wk_l = wload("wk_t", wk_d, KOFF + hd0, s_wt["k"], wguard["k"])
            wks_l = wload("wks_t", wks_d, hd0, s_wt["ks"], wguard["ks"])
            kev = []
            for bi, (buf, wt, wl, bcol) in enumerate(
                    (("kown", "wk_t", wk_l, 2), ("kswp", "wks_t", wks_l, 3))):
                for n in range(4):
                    reg = ps[:, 3072 + (n % 2) * 512: 3072 + (n % 2) * 512 + 512]
                    def mmk(e, wt=wt, n=n, reg=reg):
                        for dc in range(NDC):
                            ins = e.matmul(reg, sb[wt][:, dc * 128:(dc + 1) * 128],
                                           xnc(dc, n * 512, (n + 1) * 512),
                                           start=(dc == 0), stop=(dc == NDC - 1))
                        return ins
                    dd = [wl, st["kb"], st.get("negBd")] + xn_dep
                    if len(kev) >= 2:
                        dd.append(kev[-2])
                    mm = P.op("tensor", mmk, deps=dd, inc=s_pe)
                    wguard["ks" if buf == "kswp" else "k"] = mm
                    ev = P.op("vector", lambda e, buf=buf, n=n, reg=reg: e.tensor_copy(
                        sb[buf][:, n * 512:(n + 1) * 512], reg), deps=[mm], inc=s_dve)
                    if li == 1:
                        bias = sb["biasfm"][:, bcol * 8 + g:bcol * 8 + g + 1]
                        ev = P.op("vector", lambda e, buf=buf, n=n, bias=bias: e.tensor_scalar(
                            sb[buf][:, n * 512:(n + 1) * 512], sb[buf][:, n * 512:(n + 1) * 512],
                            bias, None, ALU.add, ALU.bypass), deps=[ev, c_bias], inc=s_dve)
                    kev.append(ev)
                    yield
            kr = []
            for slot in (0, 64):
                rws = slice(slot, slot + ROT)
                P.op("vector", lambda e, rws=rws: e.tensor_tensor(
                    sb["rtmp"][rws, :], sb["kown"][rws, :], sb["cos_t"][rws, :], ALU.mult),
                    deps=kev + [c_cos, c_sin], inc=s_dve)
                P.op("vector", lambda e, rws=rws: e.tensor_tensor(
                    sb["kown"][rws, :], sb["kswp"][rws, :], sb["sin_t"][rws, :], ALU.mult),
                    inc=s_dve)
                t3 = P.op("vector", lambda e, rws=rws: e.tensor_tensor(
                    sb["kown"][rws, :], sb["kown"][rws, :], sb["rtmp"][rws, :], ALU.add),
                    inc=s_dve)
                kr.append(t3)
            ksq = P.op("vector", lambda e: e.tensor_tensor(
                sb["kswp"][:, :], sb["kown"][:, :], sb["kown"][:, :], ALU.mult),
                deps=kr, inc=s_dve)
            yield
            nk = []
            for hl in (0, 1):
                for half in (0, 1):
                    def mmn(e, hl=hl, half=half):
                        for n in range(2):
                            ins = e.matmul(ps[0:1, 3072 + n * 512: 3072 + (n + 1) * 512],
                                           sb["ones_c"][hl * 64:hl * 64 + 64, 0:1],
                                           sb["kswp"][hl * 64:hl * 64 + 64,
                                                      half * 1024 + n * 512: half * 1024 + (n + 1) * 512],
                                           start=True, stop=True)
                        return ins
                    mm = P.op("tensor", mmn, deps=[ksq, c_onesc] + nk, inc=s_pe)
                    rd = P.op("vector", lambda e, hl=hl, half=half: e.reduce_max(
                        sb["nrm"][0:1, 8 + hl * 2 + half: 9 + hl * 2 + half],
                        ps[0:1, 3072:4096], AX.X), deps=[mm], inc=s_dve)
                    nk.append(rd)
                    yield
                P.op("vector", lambda e, hl=hl: e.tensor_tensor(
                    sb["nrm"][0:1, 2 + hl:3 + hl], sb["nrm"][0:1, 8 + hl * 2:9 + hl * 2],
                    sb["nrm"][0:1, 9 + hl * 2:10 + hl * 2], ALU.max), inc=s_dve)
            nk = [(s_dve, P.sems["dve"].v)]
            kb = P.op("sync", lambda e: e.dma_start(
                out=AP(kv_in, 0, [[2048, 128], [1, 2048]]),
                in_=sb["kown"][:, :]),
                deps=[ksq, st["cc_free"]], inc=s_bnc, inck=16)
            wv_l = wload("wv_t", wv_d, VOFF + hd0, s_wt["v"], wguard["v"])
            vset = P.op("vector", lambda e: e.memset(
                sb["vown"].ap().rearrange("p (jt h c) -> p jt h c", h=2, c=65)[:, :, :, 64:65], 1.0),
                deps=[st["vb"]], inc=s_dve)
            vev = []
            for jt in range(NCH):
                reg = ps[:, 3072 + (jt % 2) * 512: 3072 + (jt % 2) * 512 + 128]
                def mmv(e, jt=jt, reg=reg, li=li, hd0=hd0):
                    for dc in range(NDC):
                        ins = e.matmul(reg, xnc(dc, jt * 128, (jt + 1) * 128),
                                       sb["wv_t"][:, dc * 128:(dc + 1) * 128],
                                       start=(dc == 0), stop=(dc == NDC - 1 and li == 0))
                    if li == 1:
                        ins = e.matmul(reg, ones_f[:, 0:128], sb["row0"][:, hd0:hd0 + 128],
                                       start=False, stop=True)
                    return ins
                dd = [wv_l] + xn_dep + ([bvr, c_ones] if li == 1 else [])
                if len(vev) >= 2:
                    dd.append(vev[-2])
                else:
                    dd += nk
                mm = P.op("tensor", mmv, deps=dd, inc=s_pe)
                wguard["v"] = mm
                ev = P.op("vector", lambda e, jt=jt, reg=reg: e.tensor_copy(
                    vown3[:, jt, :].rearrange("p (h c) -> p h c", h=2, c=65)[:, :, 0:64],
                    reg.rearrange("p (h c) -> p h c", h=2, c=64)),
                    deps=[mm, vset], inc=s_dve)
                vev.append(ev)
                yield
            vb = P.op("sync", lambda e: e.dma_start(
                out=AP(kv_in, KIN_E, [[130, 128], [130 * 128, 16], [1, 130]]),
                in_=vown3),
                deps=vev + [st["cc_free"]], inc=s_bnc, inck=16)
            bnc_all = (s_bnc, P.sems["bnc"].v)
            st["kb"] = bnc_all
            st["vb"] = bnc_all
            cc = P.op("gpsimd", lambda e: e.collective_compute(
                "AllGather", ALU.bypass, RG_PAIRS,
                ins=[kv_in.ap().opt()], outs=[kv_out.ap().opt()]),
                deps=[bnc_all] + st["rel_done"], inc=s_cc)
            st["cc_free"] = cc
            st["cc_tok"] = cc
            wq_l = wload("wq_t", wq_d, QOFF + hd0, s_wt["q"], wguard["q"])
            wqs_l = wload("wqs_t", wqs_d, hd0, s_wt["qs"], wguard["qs"])
            qev = []
            for bi, (buf, wt, wl, bcol) in enumerate(
                    ((qb, "wq_t", wq_l, 0), ("qswp", "wqs_t", wqs_l, 1))):
                for n in range(4):
                    reg = ps[:, 3072 + (n % 2) * 512: 3072 + (n % 2) * 512 + 512]
                    def mmq(e, wt=wt, n=n, reg=reg):
                        for dc in range(NDC):
                            ins = e.matmul(reg, sb[wt][:, dc * 128:(dc + 1) * 128],
                                           xnc(dc, n * 512, (n + 1) * 512),
                                           start=(dc == 0), stop=(dc == NDC - 1))
                        return ins
                    dd = [wl, st["kb"]] + xn_dep
                    if len(qev) >= 2:
                        dd.append(qev[-2])
                    mm = P.op("tensor", mmq, deps=dd, inc=s_pe)
                    wguard["qs" if buf == "qswp" else "q"] = mm
                    ev = P.op("vector", lambda e, buf=buf, n=n, reg=reg: e.tensor_copy(
                        sb[buf][:, n * 512:(n + 1) * 512], reg), deps=[mm], inc=s_dve)
                    if li == 1:
                        bias = sb["biasfm"][:, bcol * 8 + g:bcol * 8 + g + 1]
                        ev = P.op("vector", lambda e, buf=buf, n=n, bias=bias: e.tensor_scalar(
                            sb[buf][:, n * 512:(n + 1) * 512], sb[buf][:, n * 512:(n + 1) * 512],
                            bias, None, ALU.add, ALU.bypass), deps=[ev, c_bias], inc=s_dve)
                    qev.append(ev)
                    yield
            qr = []
            for slot in (0, 64):
                rws = slice(slot, slot + ROT)
                P.op("vector", lambda e, rws=rws, qb=qb: e.tensor_tensor(
                    sb["rtmp"][rws, :], sb[qb][rws, :], sb["cos_t"][rws, :], ALU.mult),
                    deps=qev + [c_cos, c_sin], inc=s_dve)
                P.op("vector", lambda e, rws=rws, qb=qb: e.tensor_tensor(
                    sb[qb][rws, :], sb["qswp"][rws, :], sb["sin_t"][rws, :], ALU.mult),
                    inc=s_dve)
                t3 = P.op("vector", lambda e, rws=rws, qb=qb: e.tensor_tensor(
                    sb[qb][rws, :], sb[qb][rws, :], sb["rtmp"][rws, :], ALU.add),
                    inc=s_dve)
                qr.append(t3)
            qsq = P.op("vector", lambda e, qb=qb: e.tensor_tensor(
                sb["rtmp"][:, :], sb[qb][:, :], sb[qb][:, :], ALU.mult),
                deps=qr, inc=s_dve)
            yield
            nq = []
            for hl in (0, 1):
                for half in (0, 1):
                    def mmn(e, hl=hl, half=half):
                        for n in range(2):
                            ins = e.matmul(ps[0:1, 3072 + n * 512: 3072 + (n + 1) * 512],
                                           sb["ones_c"][hl * 64:hl * 64 + 64, 0:1],
                                           sb["rtmp"][hl * 64:hl * 64 + 64,
                                                      half * 1024 + n * 512: half * 1024 + (n + 1) * 512],
                                           start=True, stop=True)
                        return ins
                    mm = P.op("tensor", mmn, deps=[qsq, c_onesc] + nq, inc=s_pe)
                    rd = P.op("vector", lambda e, hl=hl, half=half: e.reduce_max(
                        sb["nrm"][0:1, 12 + hl * 2 + half: 13 + hl * 2 + half],
                        ps[0:1, 3072:4096], AX.X), deps=[mm], inc=s_dve)
                    nq.append(rd)
                    yield
                P.op("vector", lambda e, hl=hl: e.tensor_tensor(
                    sb["nrm"][0:1, hl:hl + 1], sb["nrm"][0:1, 12 + hl * 2:13 + hl * 2],
                    sb["nrm"][0:1, 13 + hl * 2:14 + hl * 2], ALU.max), inc=s_dve)
            P.op("vector", lambda e: e.tensor_tensor(
                sb["nrm"][0:1, 4:6], sb["nrm"][0:1, 0:2], sb["nrm"][0:1, 2:4], ALU.add),
                inc=s_dve)
            nb2 = P.op("vector", lambda e: e.tensor_scalar_mul(
                sb["nrm"][0:1, 4:6], sb["nrm"][0:1, 4:6], -SCALE / 2.0), inc=s_dve)
            gp = g % 2
            nbm = P.op("tensor", lambda e: e.matmul(
                ps[:, 3072:3074], ones_f[:, :], sb["nrm"][0:1, 4:6], start=True, stop=True),
                deps=[nb2, c_ones], inc=s_pe)
            negBd = P.op("vector", lambda e, gp=gp: e.tensor_copy(
                sb["negB"][:, gp * 2:gp * 2 + 2], ps[:, 3072:3074]),
                deps=[nbm, st["exp_last2"][gp]], inc=s_dve)
            st["negBd"] = negBd
            st["qr"] = qr

        def finish_prep(g):
            cc = st["cc_tok"]
            P.op("sync", lambda e: e.dma_start(
                out=sb["khat"].ap().rearrange("p (r j) -> p r j", r=2),
                in_=AP(kv_out, 0, [[2048, 128], [KVE, 2], [1, 2048]])),
                deps=[cc, pe_now()], inc=s_rel, inck=16)
            P.op("sync", lambda e: e.dma_start(
                out=vhat3[:, 0:16, :],
                in_=AP(kv_out, KIN_E, [[130, 128], [130 * 128, 16], [1, 130]])),
                deps=[cc], inc=s_rel, inck=16)
            P.op("sync", lambda e: e.dma_start(
                out=vhat3[:, 16:32, :],
                in_=AP(kv_out, KVE + KIN_E, [[130, 128], [130 * 128, 16], [1, 130]])),
                inc=s_rel, inck=16)
            st["rel_done"] = [(s_rel, P.sems["rel"].v)]

        gen = prep_gen(0)
        for _ in gen:
            pass
        finish_prep(0)
        gen = prep_gen(1) if NGRP > 1 else iter(())

        def pump(n=1):
            nonlocal_done = False
            for _ in range(n):
                if next(gen, "end") == "end":
                    break

        for g in range(1 if upto == "g1" else NGRP):
            gp = g % 2
            qb = "qbufA" if g % 2 == 0 else "qbufB"
            qr = st["qr"]
            negBd = st["negBd"]
            rel_done = st["rel_done"]
            OUTREG = ps[:, 2048:3072]
            pt = [sb["pt0"], sb["pt1"]]
            out_free = st.get("out_free")
            exp_tok = {}
            pv_tok = {}
            last = None
            for hl in (0, 1):
                slot = hl * 64
                for ic in (0, 1):
                    for j in range(32):
                        sreg = ps[:, (j % 2) * 1024:(j % 2) * 1024 + 1024]
                        def mmqk(e, j=j, slot=slot, ic=ic, sreg=sreg, qb=qb):
                            for nh in (0, 1):
                                ins = e.matmul(
                                    sreg[:, nh * 512:(nh + 1) * 512],
                                    sb["khat"][slot:slot + 64, j * 128:(j + 1) * 128],
                                    sb[qb][slot:slot + 64,
                                           ic * 1024 + nh * 512: ic * 1024 + (nh + 1) * 512],
                                    start=True, stop=True)
                            return ins
                        dd = rel_done + [qr[hl], negBd, prev_flash_act,
                                         exp_tok.get((hl, ic, j - 2)),
                                         exp_tok.get((hl, ic - 1, 30 + j)),
                                         exp_tok.get((hl - 1, 1, 30 + j))]
                        qk = P.op("tensor", mmqk, deps=dd, inc=s_pe)
                        dd = [qk, negBd, pv_tok.get((hl, ic, j - 2)),
                              pv_tok.get((hl, ic - 1, 30 + j)),
                              pv_tok.get((hl - 1, 1, 30 + j))]
                        exp_tok[(hl, ic, j)] = P.op("scalar", lambda e, j=j, gp=gp, hl=hl, sreg=sreg:
                                                    e.activation(
                            pt[j % 2][:, :], sreg, AF.Exp, scale=SCALE,
                            bias=sb["negB"][:, gp * 2 + hl:gp * 2 + hl + 1]),
                            deps=dd, inc=s_act)
                        def mmpv(e, j=j, hl=hl):
                            for isub in range(8):
                                ins = e.matmul(
                                    OUTREG[:, isub * 128: isub * 128 + 65],
                                    pt[j % 2][:, isub * 128:(isub + 1) * 128],
                                    vhat3[:, j, hl * 65:(hl + 1) * 65],
                                    start=(j == 0 and isub % 4 == 0),
                                    stop=(j == 31 and isub % 4 == 3),
                                    skip_group_check=True)
                            return ins
                        pv_tok[(hl, ic, j)] = P.op("tensor", mmpv,
                                                   deps=[exp_tok[(hl, ic, j)], out_free],
                                                   inc=s_pe)
                        if j % 2 == 0:
                            pump(1)
                    for isub in range(8):
                        rc = P.op("vector", lambda e, isub=isub: e.reciprocal(
                            sb["rcol"][:, :], OUTREG[:, isub * 128 + 64: isub * 128 + 65]),
                            deps=[pv_tok[(hl, ic, 31)]], inc=s_dve)
                        wr = P.op("vector", lambda e, isub=isub, ic=ic, g=g, hl=hl:
                                  e.tensor_scalar(
                            atm[:, (ic * 8 + isub) * 1024 + (2 * g + hl) * 64:
                                (ic * 8 + isub) * 1024 + (2 * g + hl) * 64 + 64],
                            OUTREG[:, isub * 128: isub * 128 + 64],
                            sb["rcol"][:, :], None, ALU.mult, ALU.bypass),
                            deps=[rc], inc=s_dve)
                        last = wr
                    out_free = last
                    st["out_free"] = last
                st["exp_last2"][gp] = exp_tok[(hl, 1, 31)]
            a_done = last
            prev_flash_act = act_now()
            if g + 1 < NGRP and upto != "g1":
                for _ in gen:
                    pass
                finish_prep(g + 1)
                gen = prep_gen(g + 2) if g + 2 < NGRP else iter(())

        if upto == "g1":
            continue
        if li == 0:
            wo_src, bo_src = dins[p + "wout"], dins[p + "bout"]
        else:
            wo_src, bo_src = dins[p + "wo"], dins[p + "bo"]
        wo_l = P.op("gpsimd", lambda e, wo_src=wo_src: e.dma_start(
            out=sb["wo_t"].ap().rearrange("p (dc c) -> p dc c", c=1024),
            in_=wo_src.ap().rearrange("(dc p) c -> p dc c", p=128)),
            deps=[prev_flash_act, dve_now()], inc=s_wt["o"], inck=16)
        bor = P.op("sync", lambda e, bo_src=bo_src: e.dma_start(
            out=sb["row0"][:, :], in_=bo_src.ap()),
            deps=[pe_now()], inc=s_ldc, inck=16)
        if li == 1:
            prev_ev = [bor, prev_flash_act, dve_now()]
            for nm, dst in (("gamma", "gb_bc"), ("beta", "bb_bc")):
                r = P.op("sync", lambda e, nm=nm, p=p: e.dma_start(out=sb["row0"][:, :], in_=dins[p + nm].ap()),
                         deps=prev_ev, inc=s_ldc, inck=16)
                mm = P.op("tensor", lambda e: (
                    e.matmul(ps[:, 3072:3584], ones_f[:, :], sb["row0"][:, 0:512], start=True, stop=True),
                    e.matmul(ps[:, 3584:4096], ones_f[:, :], sb["row0"][:, 512:1024], start=True, stop=True))[-1],
                    deps=[r, c_ones], inc=s_pe)
                ev = P.op("scalar", lambda e, dst=dst: (
                    e.activation(sb[dst][:, 0:512], ps[:, 3072:3584], AF.Copy),
                    e.activation(sb[dst][:, 512:1024], ps[:, 3584:4096], AF.Copy))[-1],
                    deps=[mm], inc=s_act)
                prev_ev = [ev]
            c_gb = ev
            bor = P.op("sync", lambda e, bo_src=bo_src: e.dma_start(
                out=sb["row0"][:, :], in_=bo_src.ap()),
                deps=[ev], inc=s_ldc, inck=16)

        tp_prev = [None, None]
        proj_prev = [None, None]
        for c in range(NCH):
            fm_ev = []
            for dc in range(NDC):
                reg = psb[:, (dc % 2) * 1024:(dc % 2) * 1024 + 128]
                mm = P.op("tensor", lambda e, c=c, dc=dc, reg=reg: e.transpose(
                    reg, atm[:, c * 1024 + dc * 128: c * 1024 + (dc + 1) * 128],
                    sb["ident_b"][:, :]),
                    deps=[a_done, c_idb, tp_prev[dc % 2]], inc=s_pe)
                ev = P.op("scalar", lambda e, dc=dc, reg=reg: e.activation(
                    sb["fmscr"][:, dc * 128:(dc + 1) * 128], reg, AF.Copy),
                    deps=[mm], inc=s_act)
                tp_prev[dc % 2] = ev
                fm_ev.append(ev)
            preg = ps[:, 2048 + (c % 2) * 1024: 2048 + (c % 2) * 1024 + 1024]
            def mmproj(e, preg=preg):
                for nh in (0, 1):
                    for dc in range(NDC):
                        e.matmul(preg[:, nh * 512:(nh + 1) * 512],
                                 sb["fmscr"][:, dc * 128:(dc + 1) * 128],
                                 sb["wo_t"][:, dc * 1024 + nh * 512: dc * 1024 + (nh + 1) * 512],
                                 start=(dc == 0), stop=False)
                    ins = e.matmul(preg[:, nh * 512:(nh + 1) * 512], ones_f[:, 0:128],
                                   sb["row0"][:, nh * 512:(nh + 1) * 512],
                                   start=False, stop=True)
                return ins
            mm = P.op("tensor", mmproj,
                      deps=fm_ev + [wo_l, bor, c_ones, proj_prev[c % 2]], inc=s_pe)
            if li == 0:
                wr = P.op("vector", lambda e, c=c, preg=preg: (
                    e.tensor_tensor(xch(c)[:, 0:512], xch(c)[:, 0:512], preg[:, 0:512], ALU.add),
                    e.tensor_tensor(xch(c)[:, 512:1024], xch(c)[:, 512:1024], preg[:, 512:1024],
                                    ALU.add))[-1], deps=[mm] + x_dep, inc=s_dve)
                proj_prev[c % 2] = wr
                st["x_dep"] = st["x_dep"] + [wr]
            else:
                P.op("vector", lambda e, c=c: e.tensor_scalar(
                    sb["scr0"][:, :], xch(c), sb["mu"][:, c:c + 1], sb["rstd"][:, c:c + 1],
                    ALU.subtract, ALU.mult), deps=[mm] + x_dep, inc=s_dve)
                h2 = P.op("vector", lambda e, preg=preg: e.tensor_tensor(
                    sb["scr0"][:, :], sb["scr0"][:, :], preg, ALU.add), inc=s_dve)
                proj_prev[c % 2] = h2
                su = P.op("vector", lambda e: e.reduce_sum(sb["muh"][:, :], sb["scr0"][:, :], AX.X),
                          inc=s_dve)
                sq2 = P.op("vector", lambda e: e.scalar_tensor_tensor(
                    sb["scr1"][:, :], sb["scr0"][:, :], 1.0, sb["scr0"][:, :], ALU.mult, ALU.mult,
                    accum_out=sb["ssq2"][:, :]), inc=s_dve)
                P.op("vector", lambda e: e.tensor_scalar_mul(sb["muh"][:, :], sb["muh"][:, :], 1.0 / D),
                     deps=[su], inc=s_dve)
                P.op("vector", lambda e: e.tensor_mul(sb["rstdh"][:, :], sb["muh"][:, :], sb["muh"][:, :]),
                     inc=s_dve)
                P.op("vector", lambda e: e.scalar_tensor_tensor(
                    sb["rstdh"][:, :], sb["ssq2"][:, :], 1.0 / D, sb["rstdh"][:, :],
                    ALU.mult, ALU.subtract), deps=[sq2], inc=s_dve)
                vv = P.op("vector", lambda e: e.tensor_scalar_add(
                    sb["rstdh"][:, :], sb["rstdh"][:, :], EPS), inc=s_dve)
                sq3 = P.op("scalar", lambda e: e.activation(
                    sb["rstdh"][:, :], sb["rstdh"][:, :], AF.Sqrt), deps=[vv], inc=s_act)
                rr = P.op("vector", lambda e: e.reciprocal(sb["rstdh"][:, :], sb["rstdh"][:, :]),
                          deps=[sq3], inc=s_dve)
                P.op("vector", lambda e: e.tensor_scalar(
                    sb["scr1"][:, :], sb["scr0"][:, :], sb["muh"][:, :], sb["rstdh"][:, :],
                    ALU.subtract, ALU.mult), deps=[rr], inc=s_dve)
                P.op("vector", lambda e: e.tensor_tensor(
                    sb["scr1"][:, :], sb["scr1"][:, :], sb["gb_bc"][:, :], ALU.mult),
                    deps=[c_gb], inc=s_dve)
                P.op("vector", lambda e: e.tensor_tensor(
                    sb["scr1"][:, :], sb["scr1"][:, :], sb["bb_bc"][:, :], ALU.add), inc=s_dve)
                wr = P.op("vector", lambda e, c=c: e.tensor_tensor(
                    xch(c), xch(c), sb["scr1"][:, :], ALU.add), inc=s_dve)
                st["x_dep"] = st["x_dep"] + [wr]
        x_dep = st["x_dep"]
        tap(f"l{li}_xattn", [128, NCH * 1024], F32, lambda: xsb.ap(), [dve_now()])
        if upto == "attn":
            continue

        if li == 0:
            wg_d, wu_d, wd_d = dins[p + "wg"], dins[p + "wu"], dins[p + "wd"]
            eorder = MOE_EORDER if MOE_EORDER is not None else [7] + list(range(7))
            actf = {e_: AF.Silu for e_ in range(8)}
            b1l = None
        else:
            wg_d, wd_d = dins[p + "w1"], dins[p + "w2"]
            wu_d = None
            eorder = [0, 2, 4, 6, 1, 3, 5, 7]
            actf = {e_: (AF.Gelu if e_ % 2 == 0 else AF.Silu) for e_ in range(8)}
            b1l = P.op("sync", lambda e, p=p: e.dma_start(out=sb["b1fm"][:, :], in_=dins[p + "b1_fm"].ap()),
                       deps=[dve_now(), act_now(), pe_now()], inc=s_ldc, inck=16)

        moe_dep = [pe_now(), dve_now(), act_now()]

        gu_guard = [None, None]
        wd_guard = [None, None]
        evac_last = None
        prev_evs = None
        ld_tok = {}
        seq = [(ic_, e__) for ic_ in (0, 1) for e__ in eorder]

        def gu_load(ic_, e_, hp, extra=()):
            par = hp % 2
            t1 = P.op("gpsimd", lambda e, hp=hp, par=par, e_=e_, wg_d=wg_d: e.dma_start(
                out=sb["wg_t"].ap().rearrange("p (b dc c) -> p b dc c", b=2, c=256)[:, par],
                in_=wg_d.ap()[e_].rearrange("(dc p) hh -> p dc hh", p=128)
                [:, :, hp * 256:(hp + 1) * 256]),
                deps=list(extra) + [gu_guard[par]], inc=s_wg2[par], inck=16)
            toks = [t1]
            if wu_d is not None:
                t2 = P.op("gpsimd", lambda e, hp=hp, par=par, e_=e_, wu_d=wu_d: e.dma_start(
                    out=sb["wu_t"].ap().rearrange("p (b dc c) -> p b dc c", b=2, c=256)[:, par],
                    in_=wu_d.ap()[e_].rearrange("(dc p) hh -> p dc hh", p=128)
                    [:, :, hp * 256:(hp + 1) * 256]),
                    deps=list(extra) + [gu_guard[par]], inc=s_wu2[par], inck=16)
                toks.append(t2)
            ld_tok[(ic_, e_, hp)] = toks
            return toks

        for si, (ic, e_) in enumerate(seq):
            if True:
                if (ic, e_, 0) not in ld_tok:
                    gu_load(ic, e_, 0, extra=moe_dep)
                h_tok = {}
                sil_tok = {}
                for hp in range(16):
                    if hp + 1 < 16 and (ic, e_, hp + 1) not in ld_tok:
                        gu_load(ic, e_, hp + 1)
                    par = hp % 2
                    wt = ld_tok[(ic, e_, hp)]
                    for hcl in (0, 1):
                        hc = hp * 2 + hcl
                        pg = hc % 2
                        for nh in (0, 1):
                            greg = ps[:, (2 * pg + nh) * 512:(2 * pg + nh + 1) * 512]
                            ureg = ps[:, 2048 + (2 * pg + nh) * 512: 2048 + (2 * pg + nh + 1) * 512]
                            def mmg(e, par=par, hcl=hcl, nh=nh, greg=greg, ic=ic):
                                for dc in range(NDC):
                                    ins = e.matmul(
                                        greg,
                                        sb["wg_t"][:, par * 2048 + dc * 256 + hcl * 128:
                                                   par * 2048 + dc * 256 + hcl * 128 + 128],
                                        xnc(dc, ic * 1024 + nh * 512, ic * 1024 + (nh + 1) * 512),
                                        start=(dc == 0), stop=(dc == NDC - 1))
                                return ins
                            bnk = 2 * pg + nh
                            dd = [wt[0]] + xn_dep + [h_tok.get((hc - 2, nh)),
                                                     prev_evs[bnk] if prev_evs else None]
                            if si == 0:
                                dd += moe_dep
                            mm = P.op("tensor", mmg, deps=dd, inc=s_pe)
                            gu_guard[par] = mm
                            if li == 0:
                                sil = P.op("scalar", lambda e, pg=pg, nh=nh, greg=greg:
                                           e.activation(sb["gsil"][:, pg * 1024 + nh * 512:
                                                                   pg * 1024 + (nh + 1) * 512],
                                                        greg, AF.Silu),
                                           deps=[mm], inc=s_act)
                                sil_tok[(hc, nh)] = sil
                                def mmu(e, par=par, hcl=hcl, nh=nh, ureg=ureg, ic=ic):
                                    for dc in range(NDC):
                                        ins = e.matmul(
                                            ureg,
                                            sb["wu_t"][:, par * 2048 + dc * 256 + hcl * 128:
                                                       par * 2048 + dc * 256 + hcl * 128 + 128],
                                            xnc(dc, ic * 1024 + nh * 512, ic * 1024 + (nh + 1) * 512),
                                            start=(dc == 0), stop=(dc == NDC - 1))
                                    return ins
                                mmu_t = P.op("tensor", mmu,
                                             deps=[wt[1], h_tok.get((hc - 2, nh)),
                                                   prev_evs[4 + bnk] if prev_evs else None] +
                                                  (moe_dep if si == 0 else []), inc=s_pe)
                                gu_guard[par] = mmu_t
                                ht = P.op("vector", lambda e, hc=hc, nh=nh, pg=pg, ureg=ureg:
                                          e.tensor_tensor(
                                    sb["hbuf"][:, hc * 1024 + nh * 512: hc * 1024 + (nh + 1) * 512],
                                    sb["gsil"][:, pg * 1024 + nh * 512: pg * 1024 + (nh + 1) * 512],
                                    ureg, ALU.mult), deps=[sil, mmu_t], inc=s_dve)
                            else:
                                ht = P.op("scalar", lambda e, hc=hc, nh=nh, e_=e_, greg=greg,
                                          actf=actf: e.activation(
                                    sb["hbuf"][:, hc * 1024 + nh * 512: hc * 1024 + (nh + 1) * 512],
                                    greg, actf[e_],
                                    bias=sb["b1fm"][:, e_ * 32 + hc: e_ * 32 + hc + 1]),
                                    deps=[mm, b1l], inc=s_act)
                            h_tok[(hc, nh)] = ht
                h_last = dve_now() if li == 0 else act_now()

                for dh in (0, 1):
                    wdld = {}

                    def wd_load(hq, dh=dh, e_=e_, wd_d=wd_d):
                        par = hq % 2
                        t = P.op("gpsimd", lambda e, hq=hq, dh=dh, par=par, e_=e_, wd_d=wd_d:
                                 e.dma_start(
                            out=sb["wd_t"].ap().rearrange("p (b q d) -> p b q d", b=2, d=512)[:, par],
                            in_=wd_d.ap()[e_][hq * 512:(hq + 1) * 512, dh * 512:(dh + 1) * 512]
                            .rearrange("(q p) d -> p q d", p=128)),
                            deps=[wd_guard[par]], inc=s_wd2[par], inck=16)
                        wdld[hq] = t
                        return t

                    wd_load(0)
                    wd_load(1)
                    if dh == 0 and si + 1 < len(seq):
                        nic, ne = seq[si + 1]
                        gu_load(nic, ne, 0)
                        gu_load(nic, ne, 1)
                    mm_last = None
                    for hq in range(8):
                        if hq + 1 < 8 and (hq + 1) not in wdld:
                            wd_load(hq + 1)
                        par = hq % 2
                        for ql in range(4):
                            hc = hq * 4 + ql
                            def mmd(e, hc=hc, ql=ql, par=par):
                                for isub in range(8):
                                    ins = e.matmul(
                                        ps[:, isub * 512:(isub + 1) * 512],
                                        sb["hbuf"][:, hc * 1024 + isub * 128: hc * 1024 + (isub + 1) * 128],
                                        sb["wd_t"][:, par * 2048 + ql * 512:(par * 2048) + (ql + 1) * 512],
                                        start=(hc == 0), stop=(hc == 31))
                                return ins
                            dd = [h_last, wdld[hq]]
                            if hc == 0:
                                dd.append(evac_last)
                            mm_last = P.op("tensor", mmd, deps=dd, inc=s_pe)
                            wd_guard[par] = mm_last
                    evs = []
                    for isub in range(8):
                        c = ic * 8 + isub
                        c = c
                        if li == 0 and e_ == 7:
                            fn = lambda e, c=c, isub=isub, dh=dh: e.scalar_tensor_tensor(
                                xch(c)[:, dh * 512:(dh + 1) * 512],
                                ps[:, isub * 512:(isub + 1) * 512], 1.0,
                                xch(c)[:, dh * 512:(dh + 1) * 512], ALU.mult, ALU.add)
                        else:
                            fn = lambda e, c=c, isub=isub, dh=dh, e_=e_: e.scalar_tensor_tensor(
                                xch(c)[:, dh * 512:(dh + 1) * 512],
                                ps[:, isub * 512:(isub + 1) * 512],
                                sb["comb"][:, c * 8 + e_: c * 8 + e_ + 1],
                                xch(c)[:, dh * 512:(dh + 1) * 512], ALU.mult, ALU.add)
                        ev = P.op("vector", fn, deps=[mm_last, comb_dep] + x_dep, inc=s_dve)
                        evs.append(ev)
                    evac_last = evs[-1]
                    st["x_dep"] = st["x_dep"] + [evac_last]
                    prev_evs = evs
                moe_dep = [evac_last]
                x_dep = st["x_dep"]

        st["x_dep"] = [dve_now()]
        st["exp_last2"] = [None, None]

    P.op("sync", lambda e: e.dma_start(
        out=x_out.ap().rearrange("(c p) d -> p c d", p=128),
        in_=xsb.ap().rearrange("p (c d) -> p c d", d=1024)),
        deps=st["x_dep"] + [pe_now(), act_now()], inc=s_st, inck=16)
    P.op("sync", lambda e: e.nop(), deps=[(s_st, s_st.v)])

    P.emit()
    return nc



def _swap_weight(w, rot):
    out = np.zeros_like(w)
    for dd in range(rot):
        cols = np.arange(H) * DH + dd
        src = np.arange(H) * DH + (dd ^ 1)
        out[:, cols] = w[:, src]
    return out


def _swap_bias(b, rot):
    out = np.zeros_like(b)
    for dd in range(rot):
        idx = np.arange(H) * DH + dd
        src = np.arange(H) * DH + (dd ^ 1)
        out[idx] = b[src]
    return out


def _fm(b):
    return np.ascontiguousarray(b.reshape(8, 128).T.astype(np.float32))


def _rope_tables(rot, t0):
    m = np.arange(rot // 2)
    inv = 10000.0 ** (-(2.0 * m) / rot)
    t = np.arange(t0, t0 + T, dtype=np.float64)
    ang = t[:, None] * inv[None, :]
    cos = np.cos(ang)
    sin = np.sin(ang)
    cosA = np.zeros((128, T), np.float32)
    sinB = np.zeros((128, T), np.float32)
    for d in range(rot):
        cc = cos[:, d // 2]
        ss = sin[:, d // 2] * (-1.0 if d % 2 == 0 else 1.0)
        cosA[d] = cc
        cosA[64 + d] = cc
        sinB[d] = ss
        sinB[64 + d] = ss
    return (np.ascontiguousarray(cosA.astype(ml_dtypes.bfloat16)),
            np.ascontiguousarray(sinB.astype(ml_dtypes.bfloat16)))


def _layer_inputs(li, inputs, half):
    f32 = np.float32
    o = {}
    if li == 0:
        wq = np.asarray(inputs["L0_attn_Wq"], f32)
        wkv = np.asarray(inputs["L0_attn_Wkv"], f32)
        o["l0_wq"] = wq
        o["l0_wqs"] = _swap_weight(wq, 32)
        o["l0_wkv"] = wkv
        o["l0_wks"] = _swap_weight(np.ascontiguousarray(wkv[:, 0:D]), 32)
        o["l0_wout"] = np.asarray(inputs["L0_attn_Wout"], f32)
        o["l0_bout"] = np.asarray(inputs["L0_attn_bout"], f32).reshape(1, D)
        rw = np.asarray(inputs["L0_ffn_router_w"], f32)
        o["l0_rwT"] = np.ascontiguousarray(rw.T.reshape(8, 128, 7).transpose(1, 0, 2).reshape(128, 56))
        o["l0_wg"] = np.ascontiguousarray(np.concatenate(
            [np.asarray(inputs["L0_ffn_w_gate"], f32), np.asarray(inputs["L0_ffn_sw_gate"], f32)], 0))
        o["l0_wu"] = np.ascontiguousarray(np.concatenate(
            [np.asarray(inputs["L0_ffn_w_up"], f32), np.asarray(inputs["L0_ffn_sw_up"], f32)], 0))
        o["l0_wd"] = np.ascontiguousarray(np.concatenate(
            [np.asarray(inputs["L0_ffn_w_down"], f32), np.asarray(inputs["L0_ffn_sw_down"], f32)], 0))
        o["l0_cos"], o["l0_sin"] = _rope_tables(32, half * T)
    else:
        wqkv = np.asarray(inputs["L1_attn_Wqkv"], f32)
        bqkv = np.asarray(inputs["L1_attn_bqkv"], f32)
        o["l1_wqkv"] = wqkv
        o["l1_wqs"] = _swap_weight(np.ascontiguousarray(wqkv[:, 0:D]), 64)
        o["l1_wks"] = _swap_weight(np.ascontiguousarray(wqkv[:, D:2 * D]), 64)
        bq, bk, bv = bqkv[0:D], bqkv[D:2 * D], bqkv[2 * D:3 * D]
        o["l1_bias_fm"] = np.ascontiguousarray(np.concatenate(
            [_fm(bq), _fm(_swap_bias(bq, 64)), _fm(bk), _fm(_swap_bias(bk, 64))], 1))
        o["l1_bv"] = np.ascontiguousarray(bv.reshape(1, D).astype(f32))
        o["l1_wo"] = np.asarray(inputs["L1_attn_Wo"], f32)
        o["l1_bo"] = np.asarray(inputs["L1_attn_bo"], f32).reshape(1, D)
        o["l1_gamma"] = np.asarray(inputs["L1_attn_gamma"], f32).reshape(1, D)
        o["l1_beta"] = np.asarray(inputs["L1_attn_beta"], f32).reshape(1, D)
        o["l1_w1"] = np.asarray(inputs["L1_ffn_W1"], f32)
        b1 = np.asarray(inputs["L1_ffn_b1"], f32)
        o["l1_b1_fm"] = np.ascontiguousarray(b1.reshape(8, 32, 128).transpose(2, 0, 1).reshape(128, 256))
        o["l1_w2"] = np.asarray(inputs["L1_ffn_W2"], f32)
        o["l1_b2"] = np.asarray(inputs["L1_ffn_b2"], f32)
        wg = np.asarray(inputs["L1_ffn_Wg"], f32)
        o["l1_wgT"] = np.ascontiguousarray(wg.reshape(8, 128, 8).transpose(1, 0, 2).reshape(128, 64))
        o["l1_bg"] = np.asarray(inputs["L1_ffn_bg"], f32).reshape(1, 8)
        o["l1_cos"], o["l1_sin"] = _rope_tables(64, half * T)
    return o


def _const_inputs():
    return {
        "ones_f": np.ones((1, 128), np.float32),
        "ident_f": np.eye(128, dtype=np.float32),
        "ident_b": np.eye(128, dtype=np.float32).astype(ml_dtypes.bfloat16),
    }


def kernel(**inputs):
    xfull = np.asarray(inputs["L0_attn_queries"], np.float32)
    consts = _const_inputs()
    nc = build_program([0, 1])
    maps = []
    for c in range(NCORES):
        b, half = c // 2, c % 2
        m = {"x": np.ascontiguousarray(xfull[b, half * T:(half + 1) * T])}
        m.update(consts)
        m.update(_layer_inputs(0, inputs, half))
        m.update(_layer_inputs(1, inputs, half))
        maps.append(m)
    res = run_bass_kernel_spmd(nc, maps, core_ids=list(range(NCORES)))
    out = np.empty((BATCH, SEQ, D), np.float32)
    for c in range(NCORES):
        b, half = c // 2, c % 2
        out[b, half * T:(half + 1) * T] = res.results[c]["xout"]
    return out


# version 05 (final)
# speedup vs baseline: 1.0689x; 1.0434x over previous
import numpy as np
import ml_dtypes
import concourse.bass as bass
import concourse.mybir as mybir
from concourse.bass_utils import run_bass_kernel_spmd

F32 = mybir.dt.float32
BF16 = mybir.dt.bfloat16
AF = mybir.ActivationFunctionType
ALU = mybir.AluOpType
AX = mybir.AxisListType
AP = bass.AP

NCORES = 8
BATCH, SEQ, D = 4, 4096, 1024
T = 2048
H, DH = 16, 64
HID = 4096
SCALE = 0.125
EPS = 1e-5
NCH = T // 128
NDC = D // 128
NGRP = 8
NEG = -30000.0
MOE_EORDER = None
XN_EVAC_DVE = True
KIN_E = 128 * T
VIN_E = T * 130
KVE = KIN_E + VIN_E
ENGINES = ("tensor", "vector", "scalar", "gpsimd", "sync")
RG_PAIRS = [[0, 1], [2, 3], [4, 5], [6, 7]]


class Sem:
    def __init__(self, nc, name):
        self.h = nc.alloc_semaphore(name=name)
        self.name = name
        self.v = 0


class Plan:
    def __init__(self, nc):
        self.nc = nc
        self.ops = {e: [] for e in ENGINES}
        self.waits = {e: {} for e in ENGINES}
        self.sems = {}
        self.last = {}

    def sem(self, name):
        if name not in self.sems:
            self.sems[name] = Sem(self.nc, name)
        return self.sems[name]

    def op(self, eng, fn, deps=(), inc=None, inck=1):
        if eng in ("vector", "scalar") and eng in self.last:
            deps = list(deps) + [self.last[eng]]
        need = []
        for d in deps:
            if d is None:
                continue
            s, v = d
            if self.waits[eng].get(s.name, 0) < v:
                self.waits[eng][s.name] = v
                need.append((s.h, v))
        tok = None
        if inc is not None:
            inc.v += inck
            tok = (inc, inc.v)

        def run(e, fn=fn, need=tuple(need), inc=inc, inck=inck):
            for h, v in need:
                e.wait_ge(h, v)
            ins = fn(e)
            if inc is not None:
                ins.then_inc(inc.h, inck)

        self.ops[eng].append(run)
        if tok is not None and eng in ("vector", "scalar"):
            self.last[eng] = tok
        return tok

    def emit(self):
        with self.nc.Block() as block:
            for eng in ENGINES:
                fns = self.ops[eng]

                def body(e, fns=fns):
                    for f in fns:
                        f(e)

                getattr(block, eng)(body)



def build_program(layers, dbg=(), upto="full"):
    nc = bass.Bass("TRN2", target_bir_lowering=False)
    P = Plan(nc)
    dbg = set(dbg)

    x_in = nc.dram_tensor("x", [T, D], F32, kind="ExternalInput")
    x_out = nc.dram_tensor("xout", [T, D], F32, kind="ExternalOutput")

    dins = {}

    def din(name, shape, dtype=F32):
        dins[name] = nc.dram_tensor(name, shape, dtype, kind="ExternalInput")

    din("ones_f", [1, 128])
    din("ident_f", [128, 128])
    din("ident_b", [128, 128], BF16)
    for li in layers:
        p = f"l{li}_"
        if li == 0:
            din(p + "wq", [D, D]); din(p + "wqs", [D, D])
            din(p + "wkv", [D, 2 * D]); din(p + "wks", [D, D])
            din(p + "wout", [D, D]); din(p + "bout", [1, D])
            din(p + "rwT", [128, 56])
            din(p + "wg", [8, D, HID]); din(p + "wu", [8, D, HID])
            din(p + "wd", [8, HID, D])
        else:
            din(p + "wqkv", [D, 3 * D]); din(p + "wqs", [D, D]); din(p + "wks", [D, D])
            din(p + "bias_fm", [128, 32])
            din(p + "bv", [1, D])
            din(p + "wo", [D, D]); din(p + "bo", [1, D])
            din(p + "gamma", [1, D]); din(p + "beta", [1, D])
            din(p + "w1", [8, D, HID]); din(p + "b1_fm", [128, 256])
            din(p + "w2", [8, HID, D]); din(p + "b2", [8, D])
            din(p + "wgT", [128, 64]); din(p + "bg", [1, 8])
        din(p + "cos", [128, T], BF16)
        din(p + "sin", [128, T], BF16)

    kv_in = nc.dram_tensor("kv_in", [KVE, 1], BF16, kind="Internal")
    kv_out = nc.dram_tensor("kv_out", [2 * KVE, 1], BF16, kind="Internal")

    sb = {}

    ARENA0 = 16512

    def sba(name, shape, dtype, off):
        sb[name] = nc.alloc_sbuf_tensor_at(name, shape, dtype, offset=ARENA0 + off)

    B0 = 98304
    sba("x", [128, 16 * 1024], F32, 0)
    sba("xnfm", [128, 8 * 2048], BF16, 65536)
    sba("hbuf", [128, 32 * 1024], BF16, B0)
    sba("khat", [128, 4096], BF16, B0 + 0)
    sba("vhat", [128, 32 * 130], BF16, B0 + 8192)
    sba("wo_t", [128, 8 * 1024], BF16, B0 + 0)
    sba("vown", [128, 16 * 130], BF16, B0 + 16640)
    sba("pt0", [128, 1024], BF16, B0 + 20800)
    sba("pt1", [128, 1024], BF16, B0 + 22848)
    sba("cos_t", [128, T], BF16, B0 + 24896)
    sba("sin_t", [128, T], BF16, B0 + 28992)
    sba("row0", [1, 1024], F32, B0 + 33088)
    sba("gb_bc", [128, 1024], BF16, B0 + 37184)
    sba("bb_bc", [128, 1024], BF16, B0 + 39232)
    sba("wk_t", [128, 1024], BF16, B0 + 41280)
    sba("wks_t", [128, 1024], BF16, B0 + 43328)
    sba("wv_t", [128, 1024], BF16, B0 + 45376)
    sba("wq_t", [128, 1024], BF16, B0 + 47424)
    sba("wqs_t", [128, 1024], BF16, B0 + 49472)
    sba("fmscr", [128, 1024], BF16, B0 + 51520)
    sba("combfm", [8, 128], BF16, B0 + 53568)
    sba("rtmp", [128, 2048], BF16, B0 + 53824)
    sba("kown", [128, 2048], BF16, B0 + 57920)
    sba("kswp", [128, 2048], BF16, B0 + 37184)
    A0 = B0 + 65536
    sba("atm", [128, 16 * 1024], BF16, A0)
    sba("wg_t", [128, 4096], BF16, A0 + 0)
    sba("wu_t", [128, 4096], BF16, A0 + 8192)
    sba("wd_t", [128, 4096], BF16, A0 + 16384)
    sba("gsil", [128, 2048], BF16, A0 + 24576)
    Q0 = A0 + 32768
    sba("scr0", [128, 1024], F32, Q0)
    sba("scr1", [128, 1024], F32, Q0 + 4096)
    sba("qbufA", [128, 2048], BF16, Q0 + 0)
    sba("qbufB", [128, 2048], BF16, Q0 + 4096)
    sba("qswp", [128, 2048], BF16, B0 + 57920)
    P0 = Q0 + 8192
    sba("ones_f", [1, 128], F32, P0 + 0)
    sba("ident_f", [128, 128], F32, P0 + 512)
    sba("ident_b", [128, 128], BF16, P0 + 1024)
    sba("ones_c", [128, 1], BF16, P0 + 1280)
    sba("mu", [128, 16], F32, P0 + 1312)
    sba("rstd", [128, 16], F32, P0 + 1376)
    sba("sums", [128, 16], F32, P0 + 1440)
    sba("sumsq", [128, 16], F32, P0 + 1504)
    sba("muh", [128, 1], F32, P0 + 1568)
    sba("rstdh", [128, 1], F32, P0 + 1600)
    sba("ssq2", [128, 1], F32, P0 + 1632)
    sba("comb", [128, 128], F32, P0 + 1664)
    sba("negB", [128, 4], F32, P0 + 2176)
    sba("nrm", [1, 64], F32, P0 + 2208)
    sba("lg", [128, 8], F32, P0 + 2464)
    sba("m1", [128, 1], F32, P0 + 2496)
    sba("m2", [128, 1], F32, P0 + 2528)
    sba("eq", [128, 8], F32, P0 + 2560)
    sba("rden", [128, 1], F32, P0 + 2592)
    sba("rwT", [128, 64], BF16, P0 + 2624)
    sba("bgrow", [1, 8], F32, P0 + 2752)
    sba("biasfm", [128, 32], F32, P0 + 2784)
    sba("rcol", [128, 1], F32, P0 + 2912)
    sba("b1fm", [128, 256], F32, P0 + 2944)
    sba("b2bf", [8, 1024], BF16, P0 + 3968)
    assert ARENA0 + P0 + 6016 <= 229344, P0 + 6016
    nc.sbuf_base = ARENA0 + P0 + 6016

    ps = nc.alloc_psum_tensor("ps", [128, 4096], F32)
    psb = ps.bitcast(BF16)

    s_ldx = P.sem("ldx"); s_ldc = P.sem("ldc"); s_rw = P.sem("rw")
    s_bnc = P.sem("bnc"); s_cc = P.sem("cc"); s_rel = P.sem("rel")
    s_pe = P.sem("pe"); s_act = P.sem("act"); s_dve = P.sem("dve"); s_st = P.sem("st")
    s_wt = {n: P.sem("w" + n) for n in ("k", "ks", "v", "q", "qs", "o")}
    s_wg2 = [P.sem("wg0"), P.sem("wg1")]
    s_wu2 = [P.sem("wu0"), P.sem("wu1")]
    s_wd2 = [P.sem("wd0"), P.sem("wd1")]

    xsb, xn, ones_f = sb["x"], sb["xnfm"], sb["ones_f"]

    def xch(c):
        return xsb[:, c * 1024:(c + 1) * 1024]

    def xnc(dc, a, b_):
        return xn[:, dc * 2048 + a: dc * 2048 + b_]

    def pe_now():
        return (s_pe, s_pe.v)

    def dve_now():
        return (s_dve, s_dve.v)

    def act_now():
        return (s_act, s_act.v)

    dbg_t = {}

    def tap(name, shape, dtype, ap_fn, deps):
        if name not in dbg:
            return
        t = nc.dram_tensor("dbg_" + name, shape, dtype, kind="ExternalOutput")
        dbg_t[name] = t
        P.op("sync", lambda e: e.dma_start(out=t.ap(), in_=ap_fn()), deps=deps,
             inc=s_st, inck=16)

    x_ld = P.op("sync", lambda e: e.dma_start(
        out=xsb.ap().rearrange("p (c d) -> p c d", d=1024),
        in_=x_in.ap().rearrange("(c p) d -> p c d", p=128)),
        inc=s_ldx, inck=16)
    P.op("sync", lambda e: e.dma_start(out=ones_f[:, :], in_=dins["ones_f"].ap()),
         inc=s_ldc, inck=16)
    P.op("sync", lambda e: e.dma_start(out=sb["ident_f"][:, :], in_=dins["ident_f"].ap()),
         inc=s_ldc, inck=16)
    P.op("sync", lambda e: e.dma_start(out=sb["ident_b"][:, :], in_=dins["ident_b"].ap()),
         inc=s_ldc, inck=16)
    c0 = (s_ldc, s_ldc.v)
    c_ones = c_idf = c_idb = c0
    c_onesc = P.op("vector", lambda e: e.memset(sb["ones_c"][:, :], 1.0), inc=s_dve)
    P.op("vector", lambda e: e.memset(sb["comb"][:, :], 0.0), inc=s_dve)

    st = {"x_dep": [x_ld], "cc_free": None, "rel_done": [], "kb": None, "vb": None,
          "exp_last2": [None, None]}

    for li in layers:
        if upto == "x":
            break
        p = f"l{li}_"
        ROT = 32 if li == 0 else 64
        NE = 7 if li == 0 else 8
        layer_start = [pe_now(), dve_now(), act_now()]

        P.op("sync", lambda e, p=p: e.dma_start(out=sb["cos_t"][:, :], in_=dins[p + "cos"].ap()),
             deps=layer_start + [(s_ldc, s_ldc.v)], inc=s_ldc, inck=16)
        P.op("sync", lambda e, p=p: e.dma_start(out=sb["sin_t"][:, :], in_=dins[p + "sin"].ap()),
             inc=s_ldc, inck=16)
        if li == 0:
            c_rw = P.op("gpsimd", lambda e, p=p: e.dma_start(out=sb["rwT"][:, 0:56], in_=dins[p + "rwT"].ap()),
                        deps=layer_start, inc=s_rw, inck=16)
            cbatch = (s_ldc, s_ldc.v)
            c_cos = c_sin = cbatch
            c_bg = c_bias = c_gb = None
        else:
            c_rw = P.op("gpsimd", lambda e, p=p: e.dma_start(out=sb["rwT"][:, 0:64], in_=dins[p + "wgT"].ap()),
                        deps=layer_start, inc=s_rw, inck=16)
            P.op("sync", lambda e, p=p: e.dma_start(out=sb["bgrow"][:, :], in_=dins[p + "bg"].ap()),
                 inc=s_ldc, inck=16)
            P.op("sync", lambda e, p=p: e.dma_start(out=sb["biasfm"][:, :], in_=dins[p + "bias_fm"].ap()),
                 inc=s_ldc, inck=16)
            cbatch = (s_ldc, s_ldc.v)
            c_cos = c_sin = c_bg = c_bias = cbatch
            c_gb = None

        x_dep = st["x_dep"]

        for c in range(NCH):
            P.op("vector", lambda e, c=c: e.reduce_sum(sb["sums"][:, c:c + 1], xch(c), AX.X),
                 deps=x_dep + layer_start, inc=s_dve)
            P.op("vector", lambda e, c=c: e.scalar_tensor_tensor(
                sb["scr0"][:, :], xch(c), 1.0, xch(c), ALU.mult, ALU.mult,
                accum_out=sb["sumsq"][:, c:c + 1]), inc=s_dve)
        stat_in = dve_now()
        P.op("vector", lambda e: e.tensor_scalar_mul(sb["mu"][:, :], sb["sums"][:, :], 1.0 / D),
             deps=[stat_in], inc=s_dve)
        P.op("vector", lambda e: e.tensor_mul(sb["rstd"][:, :], sb["mu"][:, :], sb["mu"][:, :]),
             inc=s_dve)
        P.op("vector", lambda e: e.scalar_tensor_tensor(
            sb["rstd"][:, :], sb["sumsq"][:, :], 1.0 / D, sb["rstd"][:, :], ALU.mult, ALU.subtract),
            inc=s_dve)
        P.op("vector", lambda e: e.tensor_scalar_add(sb["rstd"][:, :], sb["rstd"][:, :], EPS),
             inc=s_dve)
        var_dep = dve_now()
        sq = P.op("scalar", lambda e: e.activation(sb["rstd"][:, :], sb["rstd"][:, :], AF.Sqrt),
                  deps=[var_dep], inc=s_act)
        stats_dep = P.op("vector", lambda e: e.reciprocal(sb["rstd"][:, :], sb["rstd"][:, :]),
                         deps=[sq], inc=s_dve)

        if upto == "stats":
            break
        ev_prev = [None, None]
        tp_last = {}
        _nch = 2 if upto in ("lnt1", "xntm1", "tponly") else NCH
        for c in range(_nch):
            scr = sb["scr0"] if c % 2 == 0 else sb["scr1"]
            deps = [stats_dep] + x_dep
            if c >= 2:
                deps.append(tp_last[c - 2])
            xn_tm = P.op("vector", lambda e, c=c, scr=scr: e.tensor_scalar(
                scr[:, :], xch(c), sb["mu"][:, c:c + 1], sb["rstd"][:, c:c + 1],
                ALU.subtract, ALU.mult), deps=deps, inc=s_dve)
            if upto in ("xntm", "xntm1"):
                tp_last[c] = xn_tm
                if upto == "xntm1" or c >= _nch - 1:
                    pass
                continue
            last = None
            for dc in range(NDC):
                reg = ps[:, 1024 + (dc % 2) * 512: 1024 + (dc % 2) * 512 + 128]
                mm = P.op("tensor", lambda e, scr=scr, dc=dc, reg=reg: e.transpose(
                    reg, scr[:, dc * 128:(dc + 1) * 128], sb["ident_f"][:, :]),
                    deps=[xn_tm, c_idf, ev_prev[dc % 2]], inc=s_pe)
                if upto == "tponly":
                    ev_prev[dc % 2] = mm
                    last = mm
                    continue
                if XN_EVAC_DVE:
                    ev = P.op("vector", lambda e, c=c, dc=dc, reg=reg: e.tensor_copy(
                        xnc(dc, c * 128, (c + 1) * 128), reg), deps=[mm], inc=s_dve)
                else:
                    ev = P.op("scalar", lambda e, c=c, dc=dc, reg=reg: e.activation(
                        xnc(dc, c * 128, (c + 1) * 128), reg, AF.Copy),
                        deps=[mm], inc=s_act)
                ev_prev[dc % 2] = ev
                last = ev
            tp_last[c] = last
        if upto in ("lnt1", "xntm", "xntm1", "tponly"):
            break
        xn_dep = [tp_last[c] for c in range(NCH)]
        tap(f"l{li}_xn", [128, NDC * 2048], BF16, lambda: xn.ap(), xn_dep)
        if upto == "lnt":
            break

        lg_cp = None
        for c in range(NCH):
            def mm_lg(e, c=c, NE=NE, li=li):
                for dc in range(NDC):
                    ins = e.matmul(ps[:, 2048:2048 + NE],
                                   xnc(dc, c * 128, (c + 1) * 128),
                                   sb["rwT"][:, dc * NE:(dc + 1) * NE],
                                   start=(dc == 0), stop=(dc == NDC - 1 and li == 0))
                if li == 1:
                    ins = e.matmul(ps[:, 2048:2048 + NE], ones_f[:, 0:128],
                                   sb["bgrow"][:, 0:NE], start=False, stop=True)
                return ins
            mm = P.op("tensor", mm_lg,
                      deps=[tp_last[c], c_rw, c_ones, lg_cp] + ([c_bg] if li == 1 else []),
                      inc=s_pe)
            lg_cp = P.op("vector", lambda e, NE=NE: e.tensor_copy(sb["lg"][:, 0:NE], ps[:, 2048:2048 + NE]),
                         deps=[mm], inc=s_dve)
            P.op("vector", lambda e, NE=NE: e.reduce_max(sb["m1"][:, :], sb["lg"][:, 0:NE],
                                                         AX.X, negate=True), inc=s_dve)
            m1d = dve_now()
            P.op("vector", lambda e, NE=NE: e.tensor_scalar(
                sb["eq"][:, 0:NE], sb["lg"][:, 0:NE], sb["m1"][:, :], None, ALU.add, ALU.bypass),
                deps=[m1d], inc=s_dve)
            P.op("vector", lambda e, NE=NE: e.tensor_scalar(
                sb["eq"][:, 0:NE], sb["eq"][:, 0:NE], 0.0, NEG, ALU.is_equal, ALU.mult),
                inc=s_dve)
            P.op("vector", lambda e, NE=NE: e.tensor_tensor(
                sb["eq"][:, 0:NE], sb["eq"][:, 0:NE], sb["lg"][:, 0:NE], ALU.add), inc=s_dve)
            P.op("vector", lambda e, NE=NE: e.reduce_max(sb["m2"][:, :], sb["eq"][:, 0:NE], AX.X),
                 inc=s_dve)
            m2d = dve_now()
            en = P.op("scalar", lambda e, NE=NE: e.activation(
                sb["eq"][:, 0:NE], sb["lg"][:, 0:NE], AF.Exp, bias=sb["m1"][:, :]),
                deps=[m2d], inc=s_act)
            e2 = P.op("scalar", lambda e: e.activation(
                sb["rden"][:, :], sb["m2"][:, :], AF.Exp, bias=sb["m1"][:, :]), inc=s_act)
            P.op("vector", lambda e: e.tensor_scalar_add(sb["rden"][:, :], sb["rden"][:, :], 1.0),
                 deps=[e2], inc=s_dve)
            P.op("vector", lambda e: e.reciprocal(sb["rden"][:, :], sb["rden"][:, :]), inc=s_dve)
            rdend = dve_now()
            P.op("vector", lambda e, NE=NE: e.tensor_scalar(
                sb["lg"][:, 0:NE], sb["lg"][:, 0:NE], sb["m2"][:, :], None, ALU.is_ge, ALU.bypass),
                deps=[rdend, en], inc=s_dve)
            P.op("vector", lambda e, NE=NE: e.tensor_tensor(
                sb["lg"][:, 0:NE], sb["lg"][:, 0:NE], sb["eq"][:, 0:NE], ALU.mult), inc=s_dve)
            P.op("vector", lambda e, c=c, NE=NE: e.tensor_scalar(
                sb["comb"][:, c * 8:c * 8 + NE], sb["lg"][:, 0:NE], sb["rden"][:, :], None,
                ALU.mult, ALU.bypass), inc=s_dve)
        comb_dep = dve_now()
        tap(f"l{li}_comb", [128, 128], F32, lambda: sb["comb"].ap(), [comb_dep])

        if upto in ("ln", "route"):
            continue

        if li == 1:
            b2d = P.op("gpsimd", lambda e, p=p: e.dma_start(out=sb["b2bf"][:, :], in_=dins[p + "b2"].ap()),
                       deps=layer_start, inc=s_rw, inck=16)
            ctp_prev = None
            for c in range(NCH):
                ctp = P.op("tensor", lambda e, c=c: e.transpose(
                    ps[0:8, 1024:1152], sb["comb"][:, c * 8:(c + 1) * 8], sb["ident_f"][:, :]),
                    deps=[comb_dep, c_idf, ctp_prev], inc=s_pe)
                cev = P.op("vector", lambda e: e.tensor_copy(sb["combfm"][:, :], ps[0:8, 1024:1152]),
                           deps=[ctp], inc=s_dve)
                mmb = P.op("tensor", lambda e: (
                    e.matmul(ps[:, 0:512], sb["combfm"][:, :], sb["b2bf"][:, 0:512],
                             start=True, stop=True),
                    e.matmul(ps[:, 512:1024], sb["combfm"][:, :], sb["b2bf"][:, 512:1024],
                             start=True, stop=True))[-1], deps=[cev, b2d], inc=s_pe)
                wr = P.op("vector", lambda e, c=c: (
                    e.tensor_tensor(xch(c)[:, 0:512], xch(c)[:, 0:512], ps[:, 0:512], ALU.add),
                    e.tensor_tensor(xch(c)[:, 512:1024], xch(c)[:, 512:1024], ps[:, 512:1024],
                                    ALU.add))[-1], deps=[mmb] + x_dep, inc=s_dve)
                ctp_prev = wr
                st["x_dep"] = st["x_dep"] + [wr]
            x_dep = st["x_dep"]

        if li == 0:
            wq_d, wqs_d = dins[p + "wq"], dins[p + "wqs"]
            wk_d, wks_d, wv_d = dins[p + "wkv"], dins[p + "wks"], dins[p + "wkv"]
            KOFF, VOFF, QOFF = 0, D, 0
        else:
            wq_d, wqs_d = dins[p + "wqkv"], dins[p + "wqs"]
            wk_d, wks_d, wv_d = dins[p + "wqkv"], dins[p + "wks"], dins[p + "wqkv"]
            KOFF, VOFF, QOFF = D, 2 * D, 0
            bvr = P.op("sync", lambda e, p=p: e.dma_start(out=sb["row0"][:, :], in_=dins[p + "bv"].ap()),
                       deps=[cbatch], inc=s_ldc, inck=16)

        vown3 = sb["vown"].ap().rearrange("p (jt c) -> p jt c", c=130)
        vhat4 = sb["vhat"].ap().rearrange("p (r jt c) -> p r jt c", r=2, c=130)
        vhat3 = sb["vhat"].ap().rearrange("p (jt c) -> p jt c", c=130)
        atm = sb["atm"]

        def wload(dst, src_d, col0, sem, guard):
            return P.op("gpsimd", lambda e: e.dma_start(
                out=sb[dst].ap().rearrange("p (dc c) -> p dc c", c=128),
                in_=src_d.ap().rearrange("(dc p) c -> p dc c", p=128)[:, :, col0:col0 + 128]),
                deps=[guard], inc=sem, inck=16)

        a_done = None
        prev_flash_act = None
        wguard = {n: layer_start[0] for n in ("k", "ks", "v", "q", "qs")}
        atm = atm

        def prep_gen(g):
            hd0 = g * 128
            qb = "qbufA" if g % 2 == 0 else "qbufB"
            wk_l = wload("wk_t", wk_d, KOFF + hd0, s_wt["k"], wguard["k"])
            wks_l = wload("wks_t", wks_d, hd0, s_wt["ks"], wguard["ks"])
            kev = []
            for bi, (buf, wt, wl, bcol) in enumerate(
                    (("kown", "wk_t", wk_l, 2), ("kswp", "wks_t", wks_l, 3))):
                for n in range(4):
                    reg = ps[:, 3072 + (n % 2) * 512: 3072 + (n % 2) * 512 + 512]
                    def mmk(e, wt=wt, n=n, reg=reg):
                        for dc in range(NDC):
                            ins = e.matmul(reg, sb[wt][:, dc * 128:(dc + 1) * 128],
                                           xnc(dc, n * 512, (n + 1) * 512),
                                           start=(dc == 0), stop=(dc == NDC - 1))
                        return ins
                    dd = [wl, st["kb"], st.get("negBd")] + xn_dep
                    if len(kev) >= 2:
                        dd.append(kev[-2])
                    mm = P.op("tensor", mmk, deps=dd, inc=s_pe)
                    wguard["ks" if buf == "kswp" else "k"] = mm
                    ev = P.op("vector", lambda e, buf=buf, n=n, reg=reg: e.tensor_copy(
                        sb[buf][:, n * 512:(n + 1) * 512], reg), deps=[mm], inc=s_dve)
                    if li == 1:
                        bias = sb["biasfm"][:, bcol * 8 + g:bcol * 8 + g + 1]
                        ev = P.op("vector", lambda e, buf=buf, n=n, bias=bias: e.tensor_scalar(
                            sb[buf][:, n * 512:(n + 1) * 512], sb[buf][:, n * 512:(n + 1) * 512],
                            bias, None, ALU.add, ALU.bypass), deps=[ev, c_bias], inc=s_dve)
                    kev.append(ev)
                    yield
            kr = []
            for slot in (0, 64):
                rws = slice(slot, slot + ROT)
                P.op("vector", lambda e, rws=rws: e.tensor_tensor(
                    sb["rtmp"][rws, :], sb["kown"][rws, :], sb["cos_t"][rws, :], ALU.mult),
                    deps=kev + [c_cos, c_sin], inc=s_dve)
                P.op("vector", lambda e, rws=rws: e.tensor_tensor(
                    sb["kown"][rws, :], sb["kswp"][rws, :], sb["sin_t"][rws, :], ALU.mult),
                    inc=s_dve)
                t3 = P.op("vector", lambda e, rws=rws: e.tensor_tensor(
                    sb["kown"][rws, :], sb["kown"][rws, :], sb["rtmp"][rws, :], ALU.add),
                    inc=s_dve)
                kr.append(t3)
            ksq = P.op("vector", lambda e: e.tensor_tensor(
                sb["kswp"][:, :], sb["kown"][:, :], sb["kown"][:, :], ALU.mult),
                deps=kr, inc=s_dve)
            yield
            nk = []
            for hl in (0, 1):
                for half in (0, 1):
                    def mmn(e, hl=hl, half=half):
                        for n in range(2):
                            ins = e.matmul(ps[0:1, 3072 + n * 512: 3072 + (n + 1) * 512],
                                           sb["ones_c"][hl * 64:hl * 64 + 64, 0:1],
                                           sb["kswp"][hl * 64:hl * 64 + 64,
                                                      half * 1024 + n * 512: half * 1024 + (n + 1) * 512],
                                           start=True, stop=True)
                        return ins
                    mm = P.op("tensor", mmn, deps=[ksq, c_onesc] + nk, inc=s_pe)
                    rd = P.op("vector", lambda e, hl=hl, half=half: e.reduce_max(
                        sb["nrm"][0:1, 8 + hl * 2 + half: 9 + hl * 2 + half],
                        ps[0:1, 3072:4096], AX.X), deps=[mm], inc=s_dve)
                    nk.append(rd)
                    yield
                P.op("vector", lambda e, hl=hl: e.tensor_tensor(
                    sb["nrm"][0:1, 2 + hl:3 + hl], sb["nrm"][0:1, 8 + hl * 2:9 + hl * 2],
                    sb["nrm"][0:1, 9 + hl * 2:10 + hl * 2], ALU.max), inc=s_dve)
            nk = [(s_dve, P.sems["dve"].v)]
            kb = P.op("sync", lambda e: e.dma_start(
                out=AP(kv_in, 0, [[2048, 128], [1, 2048]]),
                in_=sb["kown"][:, :]),
                deps=[ksq, st["cc_free"]], inc=s_bnc, inck=16)
            wv_l = wload("wv_t", wv_d, VOFF + hd0, s_wt["v"], wguard["v"])
            vset = P.op("vector", lambda e: e.memset(
                sb["vown"].ap().rearrange("p (jt h c) -> p jt h c", h=2, c=65)[:, :, :, 64:65], 1.0),
                deps=[st["vb"]], inc=s_dve)
            vev = []
            for jt in range(NCH):
                reg = ps[:, 3072 + (jt % 2) * 512: 3072 + (jt % 2) * 512 + 128]
                def mmv(e, jt=jt, reg=reg, li=li, hd0=hd0):
                    for dc in range(NDC):
                        ins = e.matmul(reg, xnc(dc, jt * 128, (jt + 1) * 128),
                                       sb["wv_t"][:, dc * 128:(dc + 1) * 128],
                                       start=(dc == 0), stop=(dc == NDC - 1 and li == 0))
                    if li == 1:
                        ins = e.matmul(reg, ones_f[:, 0:128], sb["row0"][:, hd0:hd0 + 128],
                                       start=False, stop=True)
                    return ins
                dd = [wv_l] + xn_dep + ([bvr, c_ones] if li == 1 else [])
                if len(vev) >= 2:
                    dd.append(vev[-2])
                else:
                    dd += nk
                mm = P.op("tensor", mmv, deps=dd, inc=s_pe)
                wguard["v"] = mm
                ev = P.op("vector", lambda e, jt=jt, reg=reg: e.tensor_copy(
                    vown3[:, jt, :].rearrange("p (h c) -> p h c", h=2, c=65)[:, :, 0:64],
                    reg.rearrange("p (h c) -> p h c", h=2, c=64)),
                    deps=[mm, vset], inc=s_dve)
                vev.append(ev)
                yield
            vb = P.op("sync", lambda e: e.dma_start(
                out=AP(kv_in, KIN_E, [[130, 128], [130 * 128, 16], [1, 130]]),
                in_=vown3),
                deps=vev + [st["cc_free"]], inc=s_bnc, inck=16)
            bnc_all = (s_bnc, P.sems["bnc"].v)
            st["kb"] = bnc_all
            st["vb"] = bnc_all
            cc = P.op("gpsimd", lambda e: e.collective_compute(
                "AllGather", ALU.bypass, RG_PAIRS,
                ins=[kv_in.ap().opt()], outs=[kv_out.ap().opt()]),
                deps=[bnc_all] + st["rel_done"], inc=s_cc)
            st["cc_free"] = cc
            st["cc_tok"] = cc
            wq_l = wload("wq_t", wq_d, QOFF + hd0, s_wt["q"], wguard["q"])
            wqs_l = wload("wqs_t", wqs_d, hd0, s_wt["qs"], wguard["qs"])
            qev = []
            for bi, (buf, wt, wl, bcol) in enumerate(
                    ((qb, "wq_t", wq_l, 0), ("qswp", "wqs_t", wqs_l, 1))):
                for n in range(4):
                    reg = ps[:, 3072 + (n % 2) * 512: 3072 + (n % 2) * 512 + 512]
                    def mmq(e, wt=wt, n=n, reg=reg):
                        for dc in range(NDC):
                            ins = e.matmul(reg, sb[wt][:, dc * 128:(dc + 1) * 128],
                                           xnc(dc, n * 512, (n + 1) * 512),
                                           start=(dc == 0), stop=(dc == NDC - 1))
                        return ins
                    dd = [wl, st["kb"]] + xn_dep
                    if len(qev) >= 2:
                        dd.append(qev[-2])
                    mm = P.op("tensor", mmq, deps=dd, inc=s_pe)
                    wguard["qs" if buf == "qswp" else "q"] = mm
                    ev = P.op("vector", lambda e, buf=buf, n=n, reg=reg: e.tensor_copy(
                        sb[buf][:, n * 512:(n + 1) * 512], reg), deps=[mm], inc=s_dve)
                    if li == 1:
                        bias = sb["biasfm"][:, bcol * 8 + g:bcol * 8 + g + 1]
                        ev = P.op("vector", lambda e, buf=buf, n=n, bias=bias: e.tensor_scalar(
                            sb[buf][:, n * 512:(n + 1) * 512], sb[buf][:, n * 512:(n + 1) * 512],
                            bias, None, ALU.add, ALU.bypass), deps=[ev, c_bias], inc=s_dve)
                    qev.append(ev)
                    yield
            qr = []
            for slot in (0, 64):
                rws = slice(slot, slot + ROT)
                P.op("vector", lambda e, rws=rws, qb=qb: e.tensor_tensor(
                    sb["rtmp"][rws, :], sb[qb][rws, :], sb["cos_t"][rws, :], ALU.mult),
                    deps=qev + [c_cos, c_sin], inc=s_dve)
                P.op("vector", lambda e, rws=rws, qb=qb: e.tensor_tensor(
                    sb[qb][rws, :], sb["qswp"][rws, :], sb["sin_t"][rws, :], ALU.mult),
                    inc=s_dve)
                t3 = P.op("vector", lambda e, rws=rws, qb=qb: e.tensor_tensor(
                    sb[qb][rws, :], sb[qb][rws, :], sb["rtmp"][rws, :], ALU.add),
                    inc=s_dve)
                qr.append(t3)
            qsq = P.op("vector", lambda e, qb=qb: e.tensor_tensor(
                sb["rtmp"][:, :], sb[qb][:, :], sb[qb][:, :], ALU.mult),
                deps=qr, inc=s_dve)
            yield
            nq = []
            for hl in (0, 1):
                for half in (0, 1):
                    def mmn(e, hl=hl, half=half):
                        for n in range(2):
                            ins = e.matmul(ps[0:1, 3072 + n * 512: 3072 + (n + 1) * 512],
                                           sb["ones_c"][hl * 64:hl * 64 + 64, 0:1],
                                           sb["rtmp"][hl * 64:hl * 64 + 64,
                                                      half * 1024 + n * 512: half * 1024 + (n + 1) * 512],
                                           start=True, stop=True)
                        return ins
                    mm = P.op("tensor", mmn, deps=[qsq, c_onesc] + nq, inc=s_pe)
                    rd = P.op("vector", lambda e, hl=hl, half=half: e.reduce_max(
                        sb["nrm"][0:1, 12 + hl * 2 + half: 13 + hl * 2 + half],
                        ps[0:1, 3072:4096], AX.X), deps=[mm], inc=s_dve)
                    nq.append(rd)
                    yield
                P.op("vector", lambda e, hl=hl: e.tensor_tensor(
                    sb["nrm"][0:1, hl:hl + 1], sb["nrm"][0:1, 12 + hl * 2:13 + hl * 2],
                    sb["nrm"][0:1, 13 + hl * 2:14 + hl * 2], ALU.max), inc=s_dve)
            P.op("vector", lambda e: e.tensor_tensor(
                sb["nrm"][0:1, 4:6], sb["nrm"][0:1, 0:2], sb["nrm"][0:1, 2:4], ALU.add),
                inc=s_dve)
            nb2 = P.op("vector", lambda e: e.tensor_scalar_mul(
                sb["nrm"][0:1, 4:6], sb["nrm"][0:1, 4:6], -SCALE / 2.0), inc=s_dve)
            gp = g % 2
            nbm = P.op("tensor", lambda e: e.matmul(
                ps[:, 3072:3074], ones_f[:, :], sb["nrm"][0:1, 4:6], start=True, stop=True),
                deps=[nb2, c_ones], inc=s_pe)
            negBd = P.op("vector", lambda e, gp=gp: e.tensor_copy(
                sb["negB"][:, gp * 2:gp * 2 + 2], ps[:, 3072:3074]),
                deps=[nbm, st["exp_last2"][gp]], inc=s_dve)
            st["negBd"] = negBd
            st["qr"] = qr

        def finish_prep(g):
            cc = st["cc_tok"]
            P.op("sync", lambda e: e.dma_start(
                out=sb["khat"].ap().rearrange("p (r j) -> p r j", r=2),
                in_=AP(kv_out, 0, [[2048, 128], [KVE, 2], [1, 2048]])),
                deps=[cc, pe_now()], inc=s_rel, inck=16)
            P.op("sync", lambda e: e.dma_start(
                out=vhat3[:, 0:16, :],
                in_=AP(kv_out, KIN_E, [[130, 128], [130 * 128, 16], [1, 130]])),
                deps=[cc], inc=s_rel, inck=16)
            P.op("sync", lambda e: e.dma_start(
                out=vhat3[:, 16:32, :],
                in_=AP(kv_out, KVE + KIN_E, [[130, 128], [130 * 128, 16], [1, 130]])),
                inc=s_rel, inck=16)
            st["rel_done"] = [(s_rel, P.sems["rel"].v)]

        gen = prep_gen(0)
        for _ in gen:
            pass
        finish_prep(0)
        gen = prep_gen(1) if NGRP > 1 else iter(())

        def pump(n=1):
            nonlocal_done = False
            for _ in range(n):
                if next(gen, "end") == "end":
                    break

        for g in range(1 if upto == "g1" else NGRP):
            gp = g % 2
            qb = "qbufA" if g % 2 == 0 else "qbufB"
            qr = st["qr"]
            negBd = st["negBd"]
            rel_done = st["rel_done"]
            OUTREG = ps[:, 2048:3072]
            pt = [sb["pt0"], sb["pt1"]]
            out_free = st.get("out_free")
            exp_tok = {}
            pv_tok = {}
            last = None
            for hl in (0, 1):
                slot = hl * 64
                for ic in (0, 1):
                    def emit_pv(j, hl=hl, ic=ic):
                        def mmpv(e, j=j, hl=hl):
                            for isub in range(8):
                                ins = e.matmul(
                                    OUTREG[:, isub * 128: isub * 128 + 65],
                                    pt[j % 2][:, isub * 128:(isub + 1) * 128],
                                    vhat3[:, j, hl * 65:(hl + 1) * 65],
                                    start=(j == 0 and isub % 4 == 0),
                                    stop=(j == 31 and isub % 4 == 3),
                                    skip_group_check=True)
                            return ins
                        pv_tok[(hl, ic, j)] = P.op("tensor", mmpv,
                                                   deps=[exp_tok[(hl, ic, j)], out_free],
                                                   inc=s_pe)
                    for j in range(32):
                        sreg = ps[:, (j % 2) * 1024:(j % 2) * 1024 + 1024]
                        def mmqk(e, j=j, slot=slot, ic=ic, sreg=sreg, qb=qb):
                            for nh in (0, 1):
                                ins = e.matmul(
                                    sreg[:, nh * 512:(nh + 1) * 512],
                                    sb["khat"][slot:slot + 64, j * 128:(j + 1) * 128],
                                    sb[qb][slot:slot + 64,
                                           ic * 1024 + nh * 512: ic * 1024 + (nh + 1) * 512],
                                    start=True, stop=True)
                            return ins
                        dd = rel_done + [qr[hl], negBd, prev_flash_act,
                                         exp_tok.get((hl, ic, j - 2)),
                                         exp_tok.get((hl, ic - 1, 30 + j)),
                                         exp_tok.get((hl - 1, 1, 30 + j))]
                        qk = P.op("tensor", mmqk, deps=dd, inc=s_pe)
                        dd = [qk, negBd, pv_tok.get((hl, ic, j - 2)),
                              pv_tok.get((hl, ic - 1, 30 + j)),
                              pv_tok.get((hl - 1, 1, 30 + j))]
                        exp_tok[(hl, ic, j)] = P.op("scalar", lambda e, j=j, gp=gp, hl=hl, sreg=sreg:
                                                    e.activation(
                            pt[j % 2][:, :], sreg, AF.Exp, scale=SCALE,
                            bias=sb["negB"][:, gp * 2 + hl:gp * 2 + hl + 1]),
                            deps=dd, inc=s_act)
                        if j >= 1:
                            emit_pv(j - 1)
                        if j % 2 == 0:
                            pump(1)
                    emit_pv(31)
                    for isub in range(8):
                        rc = P.op("vector", lambda e, isub=isub: e.reciprocal(
                            sb["rcol"][:, :], OUTREG[:, isub * 128 + 64: isub * 128 + 65]),
                            deps=[pv_tok[(hl, ic, 31)]], inc=s_dve)
                        wr = P.op("vector", lambda e, isub=isub, ic=ic, g=g, hl=hl:
                                  e.tensor_scalar(
                            atm[:, (ic * 8 + isub) * 1024 + (2 * g + hl) * 64:
                                (ic * 8 + isub) * 1024 + (2 * g + hl) * 64 + 64],
                            OUTREG[:, isub * 128: isub * 128 + 64],
                            sb["rcol"][:, :], None, ALU.mult, ALU.bypass),
                            deps=[rc], inc=s_dve)
                        last = wr
                    out_free = last
                    st["out_free"] = last
                st["exp_last2"][gp] = exp_tok[(hl, 1, 31)]
            a_done = last
            prev_flash_act = act_now()
            if g + 1 < NGRP and upto != "g1":
                for _ in gen:
                    pass
                finish_prep(g + 1)
                gen = prep_gen(g + 2) if g + 2 < NGRP else iter(())

        if upto == "g1":
            continue
        if li == 0:
            wo_src, bo_src = dins[p + "wout"], dins[p + "bout"]
        else:
            wo_src, bo_src = dins[p + "wo"], dins[p + "bo"]
        wo_l = P.op("gpsimd", lambda e, wo_src=wo_src: e.dma_start(
            out=sb["wo_t"].ap().rearrange("p (dc c) -> p dc c", c=1024),
            in_=wo_src.ap().rearrange("(dc p) c -> p dc c", p=128)),
            deps=[prev_flash_act, dve_now()], inc=s_wt["o"], inck=16)
        bor = P.op("sync", lambda e, bo_src=bo_src: e.dma_start(
            out=sb["row0"][:, :], in_=bo_src.ap()),
            deps=[pe_now()], inc=s_ldc, inck=16)
        if li == 1:
            prev_ev = [bor, prev_flash_act, dve_now()]
            for nm, dst in (("gamma", "gb_bc"), ("beta", "bb_bc")):
                r = P.op("sync", lambda e, nm=nm, p=p: e.dma_start(out=sb["row0"][:, :], in_=dins[p + nm].ap()),
                         deps=prev_ev, inc=s_ldc, inck=16)
                mm = P.op("tensor", lambda e: (
                    e.matmul(ps[:, 3072:3584], ones_f[:, :], sb["row0"][:, 0:512], start=True, stop=True),
                    e.matmul(ps[:, 3584:4096], ones_f[:, :], sb["row0"][:, 512:1024], start=True, stop=True))[-1],
                    deps=[r, c_ones], inc=s_pe)
                ev = P.op("scalar", lambda e, dst=dst: (
                    e.activation(sb[dst][:, 0:512], ps[:, 3072:3584], AF.Copy),
                    e.activation(sb[dst][:, 512:1024], ps[:, 3584:4096], AF.Copy))[-1],
                    deps=[mm], inc=s_act)
                prev_ev = [ev]
            c_gb = ev
            bor = P.op("sync", lambda e, bo_src=bo_src: e.dma_start(
                out=sb["row0"][:, :], in_=bo_src.ap()),
                deps=[ev], inc=s_ldc, inck=16)

        tp_prev = [None, None]
        proj_prev = [None, None]
        for c in range(NCH):
            fm_ev = []
            for dc in range(NDC):
                reg = psb[:, (dc % 2) * 1024:(dc % 2) * 1024 + 128]
                mm = P.op("tensor", lambda e, c=c, dc=dc, reg=reg: e.transpose(
                    reg, atm[:, c * 1024 + dc * 128: c * 1024 + (dc + 1) * 128],
                    sb["ident_b"][:, :]),
                    deps=[a_done, c_idb, tp_prev[dc % 2]], inc=s_pe)
                ev = P.op("scalar", lambda e, dc=dc, reg=reg: e.activation(
                    sb["fmscr"][:, dc * 128:(dc + 1) * 128], reg, AF.Copy),
                    deps=[mm], inc=s_act)
                tp_prev[dc % 2] = ev
                fm_ev.append(ev)
            preg = ps[:, 2048 + (c % 2) * 1024: 2048 + (c % 2) * 1024 + 1024]
            def mmproj(e, preg=preg):
                for nh in (0, 1):
                    for dc in range(NDC):
                        e.matmul(preg[:, nh * 512:(nh + 1) * 512],
                                 sb["fmscr"][:, dc * 128:(dc + 1) * 128],
                                 sb["wo_t"][:, dc * 1024 + nh * 512: dc * 1024 + (nh + 1) * 512],
                                 start=(dc == 0), stop=False)
                    ins = e.matmul(preg[:, nh * 512:(nh + 1) * 512], ones_f[:, 0:128],
                                   sb["row0"][:, nh * 512:(nh + 1) * 512],
                                   start=False, stop=True)
                return ins
            mm = P.op("tensor", mmproj,
                      deps=fm_ev + [wo_l, bor, c_ones, proj_prev[c % 2]], inc=s_pe)
            if li == 0:
                wr = P.op("vector", lambda e, c=c, preg=preg: (
                    e.tensor_tensor(xch(c)[:, 0:512], xch(c)[:, 0:512], preg[:, 0:512], ALU.add),
                    e.tensor_tensor(xch(c)[:, 512:1024], xch(c)[:, 512:1024], preg[:, 512:1024],
                                    ALU.add))[-1], deps=[mm] + x_dep, inc=s_dve)
                proj_prev[c % 2] = wr
                st["x_dep"] = st["x_dep"] + [wr]
            else:
                P.op("vector", lambda e, c=c: e.tensor_scalar(
                    sb["scr0"][:, :], xch(c), sb["mu"][:, c:c + 1], sb["rstd"][:, c:c + 1],
                    ALU.subtract, ALU.mult), deps=[mm] + x_dep, inc=s_dve)
                h2 = P.op("vector", lambda e, preg=preg: e.tensor_tensor(
                    sb["scr0"][:, :], sb["scr0"][:, :], preg, ALU.add), inc=s_dve)
                proj_prev[c % 2] = h2
                su = P.op("vector", lambda e: e.reduce_sum(sb["muh"][:, :], sb["scr0"][:, :], AX.X),
                          inc=s_dve)
                sq2 = P.op("vector", lambda e: e.scalar_tensor_tensor(
                    sb["scr1"][:, :], sb["scr0"][:, :], 1.0, sb["scr0"][:, :], ALU.mult, ALU.mult,
                    accum_out=sb["ssq2"][:, :]), inc=s_dve)
                P.op("vector", lambda e: e.tensor_scalar_mul(sb["muh"][:, :], sb["muh"][:, :], 1.0 / D),
                     deps=[su], inc=s_dve)
                P.op("vector", lambda e: e.tensor_mul(sb["rstdh"][:, :], sb["muh"][:, :], sb["muh"][:, :]),
                     inc=s_dve)
                P.op("vector", lambda e: e.scalar_tensor_tensor(
                    sb["rstdh"][:, :], sb["ssq2"][:, :], 1.0 / D, sb["rstdh"][:, :],
                    ALU.mult, ALU.subtract), deps=[sq2], inc=s_dve)
                vv = P.op("vector", lambda e: e.tensor_scalar_add(
                    sb["rstdh"][:, :], sb["rstdh"][:, :], EPS), inc=s_dve)
                sq3 = P.op("scalar", lambda e: e.activation(
                    sb["rstdh"][:, :], sb["rstdh"][:, :], AF.Sqrt), deps=[vv], inc=s_act)
                rr = P.op("vector", lambda e: e.reciprocal(sb["rstdh"][:, :], sb["rstdh"][:, :]),
                          deps=[sq3], inc=s_dve)
                P.op("vector", lambda e: e.tensor_scalar(
                    sb["scr1"][:, :], sb["scr0"][:, :], sb["muh"][:, :], sb["rstdh"][:, :],
                    ALU.subtract, ALU.mult), deps=[rr], inc=s_dve)
                P.op("vector", lambda e: e.tensor_tensor(
                    sb["scr1"][:, :], sb["scr1"][:, :], sb["gb_bc"][:, :], ALU.mult),
                    deps=[c_gb], inc=s_dve)
                P.op("vector", lambda e: e.tensor_tensor(
                    sb["scr1"][:, :], sb["scr1"][:, :], sb["bb_bc"][:, :], ALU.add), inc=s_dve)
                wr = P.op("vector", lambda e, c=c: e.tensor_tensor(
                    xch(c), xch(c), sb["scr1"][:, :], ALU.add), inc=s_dve)
                st["x_dep"] = st["x_dep"] + [wr]
        x_dep = st["x_dep"]
        tap(f"l{li}_xattn", [128, NCH * 1024], F32, lambda: xsb.ap(), [dve_now()])
        if upto == "attn":
            continue

        if li == 0:
            wg_d, wu_d, wd_d = dins[p + "wg"], dins[p + "wu"], dins[p + "wd"]
            eorder = MOE_EORDER if MOE_EORDER is not None else [7] + list(range(7))
            actf = {e_: AF.Silu for e_ in range(8)}
            b1l = None
        else:
            wg_d, wd_d = dins[p + "w1"], dins[p + "w2"]
            wu_d = None
            eorder = [0, 2, 4, 6, 1, 3, 5, 7]
            actf = {e_: (AF.Gelu if e_ % 2 == 0 else AF.Silu) for e_ in range(8)}
            b1l = P.op("sync", lambda e, p=p: e.dma_start(out=sb["b1fm"][:, :], in_=dins[p + "b1_fm"].ap()),
                       deps=[dve_now(), act_now(), pe_now()], inc=s_ldc, inck=16)

        moe_dep = [pe_now(), dve_now(), act_now()]

        gu_guard = [None, None]
        wd_guard = [None, None]
        evac_last = None
        prev_evs = None
        ld_tok = {}
        seq = [(ic_, e__) for ic_ in (0, 1) for e__ in eorder]

        def gu_load(ic_, e_, hp, extra=()):
            par = hp % 2
            t1 = P.op("gpsimd", lambda e, hp=hp, par=par, e_=e_, wg_d=wg_d: e.dma_start(
                out=sb["wg_t"].ap().rearrange("p (b dc c) -> p b dc c", b=2, c=256)[:, par],
                in_=wg_d.ap()[e_].rearrange("(dc p) hh -> p dc hh", p=128)
                [:, :, hp * 256:(hp + 1) * 256]),
                deps=list(extra) + [gu_guard[par]], inc=s_wg2[par], inck=16)
            toks = [t1]
            if wu_d is not None:
                t2 = P.op("gpsimd", lambda e, hp=hp, par=par, e_=e_, wu_d=wu_d: e.dma_start(
                    out=sb["wu_t"].ap().rearrange("p (b dc c) -> p b dc c", b=2, c=256)[:, par],
                    in_=wu_d.ap()[e_].rearrange("(dc p) hh -> p dc hh", p=128)
                    [:, :, hp * 256:(hp + 1) * 256]),
                    deps=list(extra) + [gu_guard[par]], inc=s_wu2[par], inck=16)
                toks.append(t2)
            ld_tok[(ic_, e_, hp)] = toks
            return toks

        for si, (ic, e_) in enumerate(seq):
            if True:
                if (ic, e_, 0) not in ld_tok:
                    gu_load(ic, e_, 0, extra=moe_dep)
                h_tok = {}
                sil_tok = {}
                for hp in range(16):
                    if hp + 1 < 16 and (ic, e_, hp + 1) not in ld_tok:
                        gu_load(ic, e_, hp + 1)
                    par = hp % 2
                    wt = ld_tok[(ic, e_, hp)]
                    for hcl in (0, 1):
                        hc = hp * 2 + hcl
                        pg = hc % 2
                        for nh in (0, 1):
                            greg = ps[:, (2 * pg + nh) * 512:(2 * pg + nh + 1) * 512]
                            ureg = ps[:, 2048 + (2 * pg + nh) * 512: 2048 + (2 * pg + nh + 1) * 512]
                            def mmg(e, par=par, hcl=hcl, nh=nh, greg=greg, ic=ic):
                                for dc in range(NDC):
                                    ins = e.matmul(
                                        greg,
                                        sb["wg_t"][:, par * 2048 + dc * 256 + hcl * 128:
                                                   par * 2048 + dc * 256 + hcl * 128 + 128],
                                        xnc(dc, ic * 1024 + nh * 512, ic * 1024 + (nh + 1) * 512),
                                        start=(dc == 0), stop=(dc == NDC - 1))
                                return ins
                            bnk = 2 * pg + nh
                            dd = [wt[0]] + xn_dep + [h_tok.get((hc - 2, nh)),
                                                     prev_evs[bnk] if prev_evs else None]
                            if si == 0:
                                dd += moe_dep
                            mm = P.op("tensor", mmg, deps=dd, inc=s_pe)
                            gu_guard[par] = mm
                            if li == 0:
                                sil = P.op("scalar", lambda e, pg=pg, nh=nh, greg=greg:
                                           e.activation(sb["gsil"][:, pg * 1024 + nh * 512:
                                                                   pg * 1024 + (nh + 1) * 512],
                                                        greg, AF.Silu),
                                           deps=[mm], inc=s_act)
                                sil_tok[(hc, nh)] = sil
                                def mmu(e, par=par, hcl=hcl, nh=nh, ureg=ureg, ic=ic):
                                    for dc in range(NDC):
                                        ins = e.matmul(
                                            ureg,
                                            sb["wu_t"][:, par * 2048 + dc * 256 + hcl * 128:
                                                       par * 2048 + dc * 256 + hcl * 128 + 128],
                                            xnc(dc, ic * 1024 + nh * 512, ic * 1024 + (nh + 1) * 512),
                                            start=(dc == 0), stop=(dc == NDC - 1))
                                    return ins
                                mmu_t = P.op("tensor", mmu,
                                             deps=[wt[1], h_tok.get((hc - 2, nh)),
                                                   prev_evs[4 + bnk] if prev_evs else None] +
                                                  (moe_dep if si == 0 else []), inc=s_pe)
                                gu_guard[par] = mmu_t
                                ht = P.op("vector", lambda e, hc=hc, nh=nh, pg=pg, ureg=ureg:
                                          e.tensor_tensor(
                                    sb["hbuf"][:, hc * 1024 + nh * 512: hc * 1024 + (nh + 1) * 512],
                                    sb["gsil"][:, pg * 1024 + nh * 512: pg * 1024 + (nh + 1) * 512],
                                    ureg, ALU.mult), deps=[sil, mmu_t], inc=s_dve)
                            else:
                                ht = P.op("scalar", lambda e, hc=hc, nh=nh, e_=e_, greg=greg,
                                          actf=actf: e.activation(
                                    sb["hbuf"][:, hc * 1024 + nh * 512: hc * 1024 + (nh + 1) * 512],
                                    greg, actf[e_],
                                    bias=sb["b1fm"][:, e_ * 32 + hc: e_ * 32 + hc + 1]),
                                    deps=[mm, b1l], inc=s_act)
                            h_tok[(hc, nh)] = ht
                h_last = dve_now() if li == 0 else act_now()

                for dh in (0, 1):
                    wdld = {}

                    def wd_load(hq, dh=dh, e_=e_, wd_d=wd_d):
                        par = hq % 2
                        t = P.op("gpsimd", lambda e, hq=hq, dh=dh, par=par, e_=e_, wd_d=wd_d:
                                 e.dma_start(
                            out=sb["wd_t"].ap().rearrange("p (b q d) -> p b q d", b=2, d=512)[:, par],
                            in_=wd_d.ap()[e_][hq * 512:(hq + 1) * 512, dh * 512:(dh + 1) * 512]
                            .rearrange("(q p) d -> p q d", p=128)),
                            deps=[wd_guard[par]], inc=s_wd2[par], inck=16)
                        wdld[hq] = t
                        return t

                    wd_load(0)
                    wd_load(1)
                    if dh == 0 and si + 1 < len(seq):
                        nic, ne = seq[si + 1]
                        gu_load(nic, ne, 0)
                        gu_load(nic, ne, 1)
                    mm_last = None
                    for hq in range(8):
                        if hq + 1 < 8 and (hq + 1) not in wdld:
                            wd_load(hq + 1)
                        par = hq % 2
                        for ql in range(4):
                            hc = hq * 4 + ql
                            def mmd(e, hc=hc, ql=ql, par=par):
                                for isub in range(8):
                                    ins = e.matmul(
                                        ps[:, isub * 512:(isub + 1) * 512],
                                        sb["hbuf"][:, hc * 1024 + isub * 128: hc * 1024 + (isub + 1) * 128],
                                        sb["wd_t"][:, par * 2048 + ql * 512:(par * 2048) + (ql + 1) * 512],
                                        start=(hc == 0), stop=(hc == 31))
                                return ins
                            dd = [h_last, wdld[hq]]
                            if hc == 0:
                                dd.append(evac_last)
                            mm_last = P.op("tensor", mmd, deps=dd, inc=s_pe)
                            wd_guard[par] = mm_last
                    evs = []
                    for isub in range(8):
                        c = ic * 8 + isub
                        c = c
                        if li == 0 and e_ == 7:
                            fn = lambda e, c=c, isub=isub, dh=dh: e.scalar_tensor_tensor(
                                xch(c)[:, dh * 512:(dh + 1) * 512],
                                ps[:, isub * 512:(isub + 1) * 512], 1.0,
                                xch(c)[:, dh * 512:(dh + 1) * 512], ALU.mult, ALU.add)
                        else:
                            fn = lambda e, c=c, isub=isub, dh=dh, e_=e_: e.scalar_tensor_tensor(
                                xch(c)[:, dh * 512:(dh + 1) * 512],
                                ps[:, isub * 512:(isub + 1) * 512],
                                sb["comb"][:, c * 8 + e_: c * 8 + e_ + 1],
                                xch(c)[:, dh * 512:(dh + 1) * 512], ALU.mult, ALU.add)
                        ev = P.op("vector", fn, deps=[mm_last, comb_dep] + x_dep, inc=s_dve)
                        evs.append(ev)
                    evac_last = evs[-1]
                    st["x_dep"] = st["x_dep"] + [evac_last]
                    prev_evs = evs
                moe_dep = [evac_last]
                x_dep = st["x_dep"]

        st["x_dep"] = [dve_now()]
        st["exp_last2"] = [None, None]

    P.op("sync", lambda e: e.dma_start(
        out=x_out.ap().rearrange("(c p) d -> p c d", p=128),
        in_=xsb.ap().rearrange("p (c d) -> p c d", d=1024)),
        deps=st["x_dep"] + [pe_now(), act_now()], inc=s_st, inck=16)
    P.op("sync", lambda e: e.nop(), deps=[(s_st, s_st.v)])

    P.emit()
    return nc



def _swap_weight(w, rot):
    out = np.zeros_like(w)
    for dd in range(rot):
        cols = np.arange(H) * DH + dd
        src = np.arange(H) * DH + (dd ^ 1)
        out[:, cols] = w[:, src]
    return out


def _swap_bias(b, rot):
    out = np.zeros_like(b)
    for dd in range(rot):
        idx = np.arange(H) * DH + dd
        src = np.arange(H) * DH + (dd ^ 1)
        out[idx] = b[src]
    return out


def _fm(b):
    return np.ascontiguousarray(b.reshape(8, 128).T.astype(np.float32))


def _rope_tables(rot, t0):
    m = np.arange(rot // 2)
    inv = 10000.0 ** (-(2.0 * m) / rot)
    t = np.arange(t0, t0 + T, dtype=np.float64)
    ang = t[:, None] * inv[None, :]
    cos = np.cos(ang)
    sin = np.sin(ang)
    cosA = np.zeros((128, T), np.float32)
    sinB = np.zeros((128, T), np.float32)
    for d in range(rot):
        cc = cos[:, d // 2]
        ss = sin[:, d // 2] * (-1.0 if d % 2 == 0 else 1.0)
        cosA[d] = cc
        cosA[64 + d] = cc
        sinB[d] = ss
        sinB[64 + d] = ss
    return (np.ascontiguousarray(cosA.astype(ml_dtypes.bfloat16)),
            np.ascontiguousarray(sinB.astype(ml_dtypes.bfloat16)))


def _layer_inputs(li, inputs, half):
    f32 = np.float32
    o = {}
    if li == 0:
        wq = np.asarray(inputs["L0_attn_Wq"], f32)
        wkv = np.asarray(inputs["L0_attn_Wkv"], f32)
        o["l0_wq"] = wq
        o["l0_wqs"] = _swap_weight(wq, 32)
        o["l0_wkv"] = wkv
        o["l0_wks"] = _swap_weight(np.ascontiguousarray(wkv[:, 0:D]), 32)
        o["l0_wout"] = np.asarray(inputs["L0_attn_Wout"], f32)
        o["l0_bout"] = np.asarray(inputs["L0_attn_bout"], f32).reshape(1, D)
        rw = np.asarray(inputs["L0_ffn_router_w"], f32)
        o["l0_rwT"] = np.ascontiguousarray(rw.T.reshape(8, 128, 7).transpose(1, 0, 2).reshape(128, 56))
        o["l0_wg"] = np.ascontiguousarray(np.concatenate(
            [np.asarray(inputs["L0_ffn_w_gate"], f32), np.asarray(inputs["L0_ffn_sw_gate"], f32)], 0))
        o["l0_wu"] = np.ascontiguousarray(np.concatenate(
            [np.asarray(inputs["L0_ffn_w_up"], f32), np.asarray(inputs["L0_ffn_sw_up"], f32)], 0))
        o["l0_wd"] = np.ascontiguousarray(np.concatenate(
            [np.asarray(inputs["L0_ffn_w_down"], f32), np.asarray(inputs["L0_ffn_sw_down"], f32)], 0))
        o["l0_cos"], o["l0_sin"] = _rope_tables(32, half * T)
    else:
        wqkv = np.asarray(inputs["L1_attn_Wqkv"], f32)
        bqkv = np.asarray(inputs["L1_attn_bqkv"], f32)
        o["l1_wqkv"] = wqkv
        o["l1_wqs"] = _swap_weight(np.ascontiguousarray(wqkv[:, 0:D]), 64)
        o["l1_wks"] = _swap_weight(np.ascontiguousarray(wqkv[:, D:2 * D]), 64)
        bq, bk, bv = bqkv[0:D], bqkv[D:2 * D], bqkv[2 * D:3 * D]
        o["l1_bias_fm"] = np.ascontiguousarray(np.concatenate(
            [_fm(bq), _fm(_swap_bias(bq, 64)), _fm(bk), _fm(_swap_bias(bk, 64))], 1))
        o["l1_bv"] = np.ascontiguousarray(bv.reshape(1, D).astype(f32))
        o["l1_wo"] = np.asarray(inputs["L1_attn_Wo"], f32)
        o["l1_bo"] = np.asarray(inputs["L1_attn_bo"], f32).reshape(1, D)
        o["l1_gamma"] = np.asarray(inputs["L1_attn_gamma"], f32).reshape(1, D)
        o["l1_beta"] = np.asarray(inputs["L1_attn_beta"], f32).reshape(1, D)
        o["l1_w1"] = np.asarray(inputs["L1_ffn_W1"], f32)
        b1 = np.asarray(inputs["L1_ffn_b1"], f32)
        o["l1_b1_fm"] = np.ascontiguousarray(b1.reshape(8, 32, 128).transpose(2, 0, 1).reshape(128, 256))
        o["l1_w2"] = np.asarray(inputs["L1_ffn_W2"], f32)
        o["l1_b2"] = np.asarray(inputs["L1_ffn_b2"], f32)
        wg = np.asarray(inputs["L1_ffn_Wg"], f32)
        o["l1_wgT"] = np.ascontiguousarray(wg.reshape(8, 128, 8).transpose(1, 0, 2).reshape(128, 64))
        o["l1_bg"] = np.asarray(inputs["L1_ffn_bg"], f32).reshape(1, 8)
        o["l1_cos"], o["l1_sin"] = _rope_tables(64, half * T)
    return o


def _const_inputs():
    return {
        "ones_f": np.ones((1, 128), np.float32),
        "ident_f": np.eye(128, dtype=np.float32),
        "ident_b": np.eye(128, dtype=np.float32).astype(ml_dtypes.bfloat16),
    }


def kernel(**inputs):
    xfull = np.asarray(inputs["L0_attn_queries"], np.float32)
    consts = _const_inputs()
    nc = build_program([0, 1])
    maps = []
    for c in range(NCORES):
        b, half = c // 2, c % 2
        m = {"x": np.ascontiguousarray(xfull[b, half * T:(half + 1) * T])}
        m.update(consts)
        m.update(_layer_inputs(0, inputs, half))
        m.update(_layer_inputs(1, inputs, half))
        maps.append(m)
    res = run_bass_kernel_spmd(nc, maps, core_ids=list(range(NCORES)))
    out = np.empty((BATCH, SEQ, D), np.float32)
    for c in range(NCORES):
        b, half = c // 2, c % 2
        out[b, half * T:(half + 1) * T] = res.results[c]["xout"]
    return out
